# version 07
# baseline (speedup 1.0000x reference)
import numpy as np
import concourse.bass as bass
import concourse.mybir as mybir
from concourse.bass_utils import run_bass_kernel_spmd
from contextlib import ExitStack

F32 = mybir.dt.float32
BF16 = mybir.dt.bfloat16
AF = mybir.ActivationFunctionType
ALU = mybir.AluOpType
AX = mybir.AxisListType

D = 2048
T = 4096
NT = 32
NIC = 16
H = 16
HS = 128
OWN = 16
NEG_SQRT_HS = -float(np.sqrt(np.float64(HS)))
INV_SQRT_HS = float(1.0 / np.sqrt(np.float64(HS)))
EPS = 1e-5
SH = 256.5
WMEAN_INV = 1.0 / (D * D)


def own_tiles(j):
    return [t for t in range(NT) if (0 if (t % 4) in (0, 3) else 1) == j]


class Ev:
    __slots__ = ("sem", "val")

    def __init__(self, sem, val):
        self.sem = sem
        self.val = val


class Eng:
    def __init__(self, sem, auto_serial):
        self.sem = sem
        self.cnt = 0
        self.auto = auto_serial
        self.waited = {}
        self.todo = []

    def wait(self, ev):
        if ev is None:
            return
        key = id(ev.sem)
        if self.waited.get(key, 0) >= ev.val:
            return
        self.waited[key] = ev.val
        sem, val = ev.sem, ev.val
        self.todo.append(lambda e: e.wait_ge(sem, val))

    def waits(self, evs):
        for ev in evs:
            self.wait(ev)

    def op(self, name, *args, inc=True, waits=(), **kw):
        self.waits(waits)
        if self.auto and self.cnt > 0:
            self.wait(Ev(self.sem, self.cnt))
        sem = self.sem
        if inc:
            self.cnt += 1

            def emit(e, name=name, args=args, kw=kw, sem=sem):
                getattr(e, name)(*args, **kw).then_inc(sem, 1)
        else:

            def emit(e, name=name, args=args, kw=kw):
                getattr(e, name)(*args, **kw)
        self.todo.append(emit)
        return Ev(self.sem, self.cnt) if inc else None

    def ev(self):
        return Ev(self.sem, self.cnt)


class DmaBuf:
    def __init__(self, nc, ctx, name):
        self.sem = ctx.enter_context(nc.semaphore(name))
        self.cum = 0

    def start(self, qeng, dst, src, waits=()):
        qeng.waits(waits)
        sem = self.sem
        self.cum += 16
        qeng.todo.append(lambda e, d=dst, s=src, sem=sem: e.dma_start(d, s).then_inc(sem, 16))
        return Ev(sem, self.cum)

    def ev(self):
        return Ev(self.sem, self.cum)


def build(dbg=(), upto='Z'):
    dbg = set(dbg)
    nc = bass.Bass("TRN2", target_bir_lowering=False)

    x0 = nc.dram_tensor("x0", [T, D], F32, kind="ExternalInput").ap()
    wqT = nc.dram_tensor("wqT", [D, D], F32, kind="ExternalInput").ap()
    wkT = nc.dram_tensor("wkT", [D, D], F32, kind="ExternalInput").ap()
    wvT = nc.dram_tensor("wvT", [D, D], F32, kind="ExternalInput").ap()
    woT = nc.dram_tensor("woT", [D, D], F32, kind="ExternalInput").ap()
    wfT = nc.dram_tensor("wfT", [D, D], F32, kind="ExternalInput").ap()
    ident_in = nc.dram_tensor("ident_in", [128, 128], F32, kind="ExternalInput").ap()
    maskd_in = nc.dram_tensor("maskd_in", [128, 128], F32, kind="ExternalInput").ap()
    msel_in = nc.dram_tensor("msel_in", [128, OWN], F32, kind="ExternalInput").ap()
    schs_in = nc.dram_tensor("schs_in", [128, 1], F32, kind="ExternalInput").ap()

    y_out = nc.dram_tensor("y_out", [OWN, 128, D], F32, kind="ExternalOutput").ap()
    dbg_t = {}
    if "y1" in dbg:
        dbg_t["d_y1"] = nc.dram_tensor("d_y1", [NT, 128, D], F32, kind="ExternalOutput").ap()
    if "y1ln" in dbg:
        dbg_t["d_y1ln"] = nc.dram_tensor("d_y1ln", [NT, 128, D], F32, kind="ExternalOutput").ap()
    if "kvq" in dbg:
        dbg_t["d_k"] = nc.dram_tensor("d_k", [128, 2, T], F32, kind="ExternalOutput").ap()
        dbg_t["d_v"] = nc.dram_tensor("d_v", [128, NT, 258], F32, kind="ExternalOutput").ap()
        dbg_t["d_q"] = nc.dram_tensor("d_q", [128, 2, OWN * 128], F32, kind="ExternalOutput").ap()
    if "attn" in dbg:
        dbg_t["d_at"] = nc.dram_tensor("d_at", [H, 128, OWN * 128], F32, kind="ExternalOutput").ap()
    if "y2" in dbg:
        dbg_t["d_y2"] = nc.dram_tensor("d_y2", [OWN, 128, D], F32, kind="ExternalOutput").ap()
    if "qdbg" in dbg:
        dbg_t["d_qd"] = nc.dram_tensor("d_qd", [4, 128, D], F32, kind="ExternalOutput").ap()
    if "xq2" in dbg:
        dbg_t["d_xq2"] = nc.dram_tensor("d_xq2", [OWN, 128, D], F32, kind="ExternalOutput").ap()
        dbg_t["d_st2"] = nc.dram_tensor("d_st2", [128, 2], F32, kind="ExternalOutput").ap()
    if "att2" in dbg:
        dbg_t["d_o0"] = nc.dram_tensor("d_o0", [128, 4, 129], F32, kind="ExternalOutput").ap()
        dbg_t["d_nc"] = nc.dram_tensor("d_nc", [128, 2], F32, kind="ExternalOutput").ap()
        dbg_t["d_p0"] = nc.dram_tensor("d_p0", [128, 512], F32, kind="ExternalOutput").ap()
        dbg_t["d_ch"] = nc.dram_tensor("d_ch", [128, 12], F32, kind="ExternalOutput").ap()

    s_y1ln = nc.dram_tensor("s_y1ln", [NT, 128, D], BF16, kind="Internal").ap()
    s_y1own = nc.dram_tensor("s_y1own", [OWN, 128, D], F32, kind="Internal").ap()
    s_ylT = nc.dram_tensor("s_ylT", [128, NIC, T], BF16, kind="Internal").ap()
    s_attnT = nc.dram_tensor("s_attnT", [H, 128, OWN * 128], BF16, kind="Internal").ap()

    with ExitStack() as ctx:
        TE = Eng(ctx.enter_context(nc.semaphore("sem_t")), False)
        VE = Eng(ctx.enter_context(nc.semaphore("sem_v")), True)
        SE = Eng(ctx.enter_context(nc.semaphore("sem_s")), True)
        GE = Eng(ctx.enter_context(nc.semaphore("sem_g")), False)
        YE = Eng(ctx.enter_context(nc.semaphore("sem_y")), False)
        ALL = [TE, VE, SE, GE, YE]

        ALL_DMABUFS = []

        def barrier():
            evs = [g.ev() for g in ALL]
            for g in ALL:
                for ev in evs:
                    if ev.sem is not g.sem:
                        g.wait(ev)
                for db in ALL_DMABUFS:
                    if db.cum:
                        g.wait(db.ev())

        def DB(name):
            b = DmaBuf(nc, ctx, name)
            ALL_DMABUFS.append(b)
            return b

        d_x0 = [DB(f"dm_x{i}") for i in range(2)]
        d_wld = [DB(f"dm_w{i}") for i in range(2)]
        d_y1lnst = [DB(f"dm_ylst{i}") for i in range(2)]
        d_y1st = [DB(f"dm_y1st{i}") for i in range(2)]
        d_const = DB("dm_const")
        d_ylld = [DB(f"dm_ylld{i}") for i in range(2)]
        d_ylTst = DB("dm_ylTst")
        d_ylTld = [DB(f"dm_ylTld{i}") for i in range(2)]
        d_wsl = DB("dm_wsl")
        d_atst = DB("dm_atst")
        d_atld = [DB(f"dm_atld{i}") for i in range(2)]
        d_y1ld = [DB(f"dm_y1ld{i}") for i in range(2)]
        d_wcd = DB("dm_wcd")
        d_out = DB("dm_out")
        d_dbg = DB("dm_dbg")
        d_dbgY = [DB(f"dm_dbgY{i}") for i in range(2)]
        d_dbgG2 = [DB(f"dm_dbgG{i}") for i in range(2)]

        identb = ctx.enter_context(nc.sbuf_tensor("identb", [128, 128], BF16))
        identf = ctx.enter_context(nc.sbuf_tensor("identf", [128, 128], F32))
        maskd = ctx.enter_context(nc.sbuf_tensor("maskd", [128, 128], BF16))
        msel = ctx.enter_context(nc.sbuf_tensor("msel", [128, OWN], F32))
        ones_f = ctx.enter_context(nc.sbuf_tensor("ones_f", [128, 128], F32))
        pad_f = ctx.enter_context(nc.sbuf_tensor("pad_f", [128, 128], F32))
        u1 = ctx.enter_context(nc.sbuf_tensor("u1", [128, NIC * D], BF16))
        u1_wq = u1.ap().rearrange("p (c n) -> p c n", n=D)
        xm = ctx.enter_context(nc.sbuf_tensor("xm", [128, D], F32))
        xq = ctx.enter_context(nc.sbuf_tensor("xq", [128, D], BF16))
        xqT = ctx.enter_context(nc.sbuf_tensor("xqT", [128, NIC, 128], BF16))
        junk = ctx.enter_context(nc.sbuf_tensor("junk", [128, D], F32))
        junk2 = ctx.enter_context(nc.sbuf_tensor("junk2", [128, D], F32))
        junk_s = ctx.enter_context(nc.sbuf_tensor("junk_s", [128, D], F32))

        def vec(name):
            return ctx.enter_context(nc.sbuf_tensor(name, [128, 1], F32))

        st_cs = vec("st_cs")
        st_sq = vec("st_sq")
        st_m = vec("st_m")
        st_msq = vec("st_msq")
        st_var = vec("st_var")
        st_sig = vec("st_sig")
        st_inv = vec("st_inv")
        st_am = vec("st_am")
        st_a = vec("st_a")
        st_ra = vec("st_ra")
        st_g = vec("st_g")
        st_st = vec("st_st")
        wsc = vec("wsc")
        wsc_inv = vec("wsc_inv")
        wscB = vec("wscB")
        st_mx = vec("st_mx")
        st_gm = vec("st_gm")
        negc = [vec("negc0"), vec("negc1")]
        wabs = ctx.enter_context(nc.sbuf_tensor("wabs", [128, NIC], F32))

        ev_c = None
        for dst, src in ((identb, ident_in), (identf, ident_in), (maskd, maskd_in),
                         (msel, msel_in)):
            ev_c = d_const.start(GE, dst[:, :], src[:, :])
        ev_consts = ev_c
        ev_ones = VE.op("memset", ones_f[:, :], 1.0)
        ones_b = ctx.enter_context(nc.sbuf_tensor("ones_b", [128, 128], BF16))
        ev_onesb = VE.op("memset", ones_b[:, :], 1.0)
        st_eps = vec("st_eps")
        ev_eps = VE.op("memset", st_eps[:, :], EPS)
        st_schs = vec("st_schs")
        st_r127 = vec("st_r127")
        schs_sb = ctx.enter_context(nc.sbuf_tensor("schs_sb", [128, 1], F32))
        ev_schs_ld = d_const.start(GE, schs_sb[:, :], schs_in[:, :])
        ev_consts = ev_schs_ld
        VE.op("memset", st_schs[:, :], 128.0)
        ev_h = SE.op("activation", out=st_schs[:, :], in_=st_schs[:, :], func=AF.Sqrt,
                     waits=[VE.ev()])
        VE.op("reciprocal", st_schs[:, :], st_schs[:, :], waits=[ev_h])
        VE.op("memset", st_r127[:, :], 127.0)
        ev_r127 = VE.op("reciprocal", st_r127[:, :], st_r127[:, :])

        def ln_quant(src_ap, waits_src):
            VE.op("reduce_sum", st_cs[:, :], src_ap, axis=AX.X, waits=waits_src)
            SE.op("activation", out=junk_s[:, :], in_=src_ap, func=AF.Square,
                  accum_out=st_sq[:, :], waits=waits_src)
            ev_sq = SE.ev()
            VE.op("tensor_scalar", out=st_m[:, :], in0=st_cs[:, :], scalar1=1.0 / D,
                  scalar2=None, op0=ALU.mult)
            VE.op("tensor_scalar", out=xm[:, :], in0=src_ap, scalar1=st_m[:, 0:1],
                  scalar2=None, op0=ALU.subtract)
            VE.op("reduce_max", st_am[:, :], xm[:, :], axis=AX.X, apply_absolute_value=True)
            VE.op("tensor_mul", st_msq[:, :], st_m[:, :], st_m[:, :])
            VE.op("scalar_tensor_tensor", out=st_var[:, :], in0=st_sq[:, :], scalar=1.0 / D,
                  in1=st_msq[:, :], op0=ALU.mult, op1=ALU.subtract, waits=[ev_sq])
            ev_var = VE.ev()
            ev_sig = SE.op("activation", out=st_sig[:, :], in_=st_var[:, :], func=AF.Sqrt,
                           bias=st_eps[:, 0:1], waits=[ev_var, ev_eps])
            VE.op("reciprocal", st_inv[:, :], st_sig[:, :], waits=[ev_sig])
            VE.op("tensor_mul", st_a[:, :], st_am[:, :], st_inv[:, :])
            VE.op("tensor_scalar_max", st_a[:, :], st_a[:, :], 1e-8)
            VE.op("reciprocal", st_ra[:, :], st_a[:, :])
            VE.op("tensor_scalar", out=st_ra[:, :], in0=st_ra[:, :], scalar1=127.0,
                  scalar2=None, op0=ALU.mult)
            VE.op("tensor_mul", st_g[:, :], st_inv[:, :], st_ra[:, :])
            xi = ln_quant.xi[0]
            VE.op("tensor_scalar", out=junk[:, :], in0=xm[:, :], scalar1=st_g[:, 0:1],
                  scalar2=SH, op0=ALU.mult, op1=ALU.add)
            VE.op("tensor_copy", xi[:, :], junk[:, :])
            VE.op("tensor_copy", junk2[:, :], xi[:, :])
            if ln_quant.qdbg[0] is not None:
                qd = ln_quant.qdbg[0]
                d_dbg.start(YE, qd[0], junk[:, :], waits=[VE.ev()])
                d_dbg.start(YE, qd[1], junk2[:, :], waits=[VE.ev()])
            VE.op("tensor_tensor", xm[:, :], junk[:, :], junk2[:, :], op=ALU.subtract)
            if ln_quant.qdbg[0] is not None:
                qd = ln_quant.qdbg[0]
                d_dbg.start(YE, qd[2], xm[:, :], waits=[VE.ev()])
            VE.op("tensor_scalar", out=junk[:, :], in0=xm[:, :], scalar1=1.0,
                  scalar2=None, op0=ALU.is_ge, waits=[d_dbg.ev()])
            VE.op("scalar_tensor_tensor", out=junk[:, :], in0=xm[:, :], scalar=2.0,
                  in1=junk[:, :], op0=ALU.is_ge, op1=ALU.add)
            VE.op("scalar_tensor_tensor", out=junk[:, :], in0=xm[:, :], scalar=0.0,
                  in1=junk[:, :], op0=ALU.is_lt, op1=ALU.subtract)
            VE.op("scalar_tensor_tensor", out=junk[:, :], in0=xm[:, :], scalar=-1.0,
                  in1=junk[:, :], op0=ALU.is_lt, op1=ALU.add)
            VE.op("tensor_tensor", junk[:, :], junk2[:, :], junk[:, :], op=ALU.subtract)
            VE.op("tensor_scalar", out=junk[:, :], in0=junk[:, :], scalar1=383.0,
                  scalar2=129.0, op0=ALU.min, op1=ALU.max)
            ev_xq = VE.op("tensor_scalar", out=xq[:, :], in0=junk[:, :], scalar1=256.0,
                          scalar2=None, op0=ALU.subtract, waits=[ln_quant.ge_guard[0]])
            VE.op("tensor_mul", st_st[:, :], st_a[:, :], wscB[:, :])
            return ev_xq

        ln_quant.ge_guard = [None]
        ln_quant.xi = [None]
        ln_quant.qdbg = [None]

        def transpose16(trps, src_fn, dst_fn, ev_src, ev_prev_cons):
            ev_evac = {0: None, 1: None}
            ev_out = None
            for g in range(4):
                b = g % 2
                for kk in range(4):
                    ic = g * 4 + kk
                    w = []
                    if kk == 0:
                        w = [ev_src, ev_consts, ev_evac[b]]
                        if g == 0 and ev_prev_cons is not None:
                            w.append(ev_prev_cons)
                    TE.op("matmul", trps[b][:, kk * 128:(kk + 1) * 128], src_fn(ic),
                          identb[:, :], start=True, stop=True, inc=(kk == 3), waits=w)
                ev_mm = TE.ev()
                ev_out = SE.op("copy", dst_fn(g), trps[b][:, :], waits=[ev_mm])
                ev_evac[b] = ev_out
            return ev_out

        def ln2_normalize(src_ap, dst_ap, ev_src, extra_waits=()):
            VE.op("reduce_sum", st_cs[:, :], src_ap, axis=AX.X, waits=[ev_src])
            SE.op("activation", out=junk_s[:, :], in_=src_ap, func=AF.Square,
                  accum_out=st_sq[:, :], waits=[ev_src])
            ev_sq = SE.ev()
            VE.op("tensor_scalar", out=st_m[:, :], in0=st_cs[:, :], scalar1=1.0 / D,
                  scalar2=None, op0=ALU.mult)
            VE.op("tensor_mul", st_msq[:, :], st_m[:, :], st_m[:, :])
            VE.op("scalar_tensor_tensor", out=st_var[:, :], in0=st_sq[:, :], scalar=1.0 / D,
                  in1=st_msq[:, :], op0=ALU.mult, op1=ALU.subtract, waits=[ev_sq])
            ev_var = VE.ev()
            ev_sig = SE.op("activation", out=st_sig[:, :], in_=st_var[:, :], func=AF.Sqrt,
                           bias=st_eps[:, 0:1], waits=[ev_var, ev_eps])
            VE.op("reciprocal", st_inv[:, :], st_sig[:, :], waits=[ev_sig])
            return VE.op("tensor_scalar", out=dst_ap, in0=src_ap, scalar1=st_m[:, 0:1],
                         scalar2=st_inv[:, 0:1], op0=ALU.subtract, op1=ALU.mult,
                         waits=list(extra_waits))

        with ExitStack() as actx:
            wbuf = [actx.enter_context(nc.sbuf_tensor(f"wbuf{i}", [128, D], F32)) for i in range(2)]
            xbuf = [actx.enter_context(nc.sbuf_tensor(f"xbuf{i}", [128, D], F32)) for i in range(2)]
            y1b = [actx.enter_context(nc.sbuf_tensor(f"y1b{i}", [128, D], F32)) for i in range(2)]
            y1ln = [actx.enter_context(nc.sbuf_tensor(f"y1ln{i}", [128, D], BF16)) for i in range(2)]
            xiA = actx.enter_context(nc.sbuf_tensor("xiA", [128, D], mybir.dt.int32))
            ln_quant.xi[0] = xiA
            ypsum = actx.enter_context(nc.psum_tensor("ypsum", [128, D], F32))
            trpsA = [actx.enter_context(nc.psum_tensor(f"trpsA{i}", [128, 512], F32)) for i in range(2)]

            ev_w = [None, None]
            ev_wfree = [None, None]
            for k in range(NIC):
                b = k % 2
                ev_w[b] = d_wld[b].start(GE, wbuf[b][:, :], wfT[k * 128:(k + 1) * 128, :],
                                         waits=[ev_wfree[b]] if ev_wfree[b] else [])
                ev_wfree[b] = SE.op("activation", out=junk_s[:, :], in_=wbuf[b][:, :],
                                    func=AF.Abs, accum_out=wabs[:, k:k + 1], waits=[ev_w[b]])
            ev_ws = VE.op("reduce_sum", st_cs[:, :], wabs[:, :], axis=AX.X,
                          waits=[ev_wfree[0], ev_wfree[1]])
            VE.op("tensor_copy", st_mx[:, :], st_cs[:, :])
            ev_wsv = VE.ev()
            ev_wsum = TE.op("matmul", trpsA[0][:, 0:1], ones_f[:, :], st_mx[:, :],
                            start=True, stop=True, waits=[ev_ones, ev_wsv])
            VE.op("tensor_scalar", out=wsc[:, :], in0=trpsA[0][:, 0:1],
                  scalar1=WMEAN_INV, scalar2=None, op0=ALU.mult, waits=[ev_wsum])
            VE.op("tensor_scalar_max", wsc[:, :], wsc[:, :], 1e-6)
            ev_wsci = VE.op("reciprocal", wsc_inv[:, :], wsc[:, :])
            VE.op("tensor_scalar", out=wscB[:, :], in0=wsc[:, :],
                  scalar1=0.5, scalar2=None, op0=ALU.mult)
            ev_wscB = VE.op("tensor_mul", wscB[:, :], wscB[:, :], st_r127[:, 0:1],
                            waits=[ev_r127])
            ev_wsum_read = ev_wscB

            ev_w = [None, None]
            for k in range(NIC):
                b = k % 2
                ev_w[b] = d_wld[b].start(GE, wbuf[b][:, :], wfT[k * 128:(k + 1) * 128, :],
                                         waits=[ev_wfree[b]] if ev_wfree[b] else [])
                xi = ln_quant.xi[0]
                VE.op("tensor_scalar", out=junk[:, :], in0=wbuf[b][:, :],
                      scalar1=wsc_inv[:, 0:1], scalar2=SH, op0=ALU.mult, op1=ALU.add,
                      waits=[ev_w[b], ev_wsci])
                ev_wfree[b] = VE.ev()
                VE.op("tensor_copy", xi[:, :], junk[:, :])
                VE.op("tensor_copy", junk2[:, :], xi[:, :])
                VE.op("tensor_tensor", xm[:, :], junk[:, :], junk2[:, :], op=ALU.subtract)
                VE.op("tensor_scalar", out=junk[:, :], in0=xm[:, :], scalar1=1.0,
                      scalar2=None, op0=ALU.is_ge)
                VE.op("tensor_scalar", out=xm[:, :], in0=xm[:, :], scalar1=0.0,
                      scalar2=None, op0=ALU.is_lt)
                VE.op("tensor_tensor", junk[:, :], junk[:, :], xm[:, :], op=ALU.subtract)
                VE.op("tensor_tensor", junk[:, :], junk[:, :], junk2[:, :], op=ALU.add)
                VE.op("tensor_scalar", out=junk[:, :], in0=junk[:, :], scalar1=257.0,
                      scalar2=255.0, op0=ALU.min, op1=ALU.max)
                VE.op("tensor_scalar", out=u1_wq[:, k, :], in0=junk[:, :], scalar1=256.0,
                      scalar2=None, op0=ALU.subtract)
            ev_wq_ready = VE.ev()
            if "wq" in dbg:
                for k in range(NIC):
                    VE.op("tensor_copy", junk[:, :], u1_wq[:, k, :], waits=[ev_wq_ready])
                    d_dbg.start(YE, dbg_t["d_wq"][:, k, :], junk[:, :], waits=[VE.ev()])

            ev_x = [None, None]
            ev_xfree = [None, None]
            ev_y1free = [[], []]
            ev_ylnfree = [[], []]
            ev_ffn_done = None
            ev_xqT_cons = None
            for t in range(NT):
                b = t % 2
                ev_x[b] = d_x0[b].start(YE, xbuf[b][:, :], x0[t * 128:(t + 1) * 128, :],
                                        waits=[ev_xfree[b]] if ev_xfree[b] else [])
                ev_xq = ln_quant(xbuf[b][:, :], [ev_x[b], ev_wscB])
                ev_xqT = transpose16(trpsA, lambda ic: xq[:, ic * 128:(ic + 1) * 128],
                                     lambda g: xqT.ap()[:, g * 4:(g + 1) * 4, :].rearrange("p c n -> p (c n)"),
                                     ev_xq, ev_wsum_read if t == 0 else None)
                for oc in range(4):
                    for ic in range(NIC):
                        w = []
                        if ic == 0 and oc == 0:
                            w = [ev_xqT, ev_wq_ready]
                            if ev_ffn_done is not None:
                                w.append(ev_ffn_done)
                            if ev_xqT_cons is not None:
                                w.append(ev_xqT_cons)
                        TE.op("matmul", ypsum[:, oc * 512:(oc + 1) * 512], xqT[:, ic, :],
                              u1_wq[:, ic, oc * 512:(oc + 1) * 512],
                              start=(ic == 0), stop=(ic == NIC - 1),
                              inc=(ic == NIC - 1 and oc == 3), waits=w)
                ev_ffn = TE.ev()
                ev_xqT_cons = ev_ffn
                yb = y1b[b]
                VE.op("scalar_tensor_tensor", out=yb[:, :], in0=ypsum[:, :],
                      scalar=st_st[:, 0:1], in1=xbuf[b][:, :], op0=ALU.mult, op1=ALU.add,
                      waits=[ev_ffn] + ev_y1free[b])
                ev_y1 = VE.ev()
                ev_xfree[b] = ev_y1
                ev_ffn_done = ev_y1
                ev_yln = ln2_normalize(yb[:, :], y1ln[b][:, :], ev_y1, ev_ylnfree[b])
                ev_st1 = d_y1lnst[b].start(YE, s_y1ln[t, :, :], y1ln[b][:, :], waits=[ev_yln])
                ev_ylnfree[b] = [ev_st1]
                ev_y1free[b] = [ev_y1]
                if t < OWN:
                    ev_y1free[b].append(d_y1st[b].start(YE, s_y1own[t, :, :], yb[:, :],
                                                        waits=[ev_y1]))
                if "y1" in dbg:
                    ev_y1free[b].append(d_dbgY[b].start(YE, dbg_t["d_y1"][t, :, :], yb[:, :],
                                                        waits=[ev_y1]))
                if "y1ln" in dbg:
                    ev_ylnfree[b].append(d_dbgG2[b].start(GE, dbg_t["d_y1ln"][t, :, :],
                                                          y1ln[b][:, :], waits=[ev_yln]))

            barrier()

        if upto >= 'B':
            with ExitStack() as bctx:
                ylstage = [bctx.enter_context(nc.sbuf_tensor(f"ylst{i}", [128, D], BF16)) for i in range(2)]
                ylT = bctx.enter_context(nc.sbuf_tensor("ylT", [128, NIC, 128], BF16))
                trpsB = [bctx.enter_context(nc.psum_tensor(f"trpsB{i}", [128, 512], F32)) for i in range(2)]
                YE.wait(d_y1lnst[0].ev())
                YE.wait(d_y1lnst[1].ev())
                ev_yl = [None, None]
                ev_ylfree = [None, None]
                ev_ylT_cons = None
                for t in range(NT):
                    b = t % 2
                    ev_yl[b] = d_ylld[b].start(YE, ylstage[b][:, :], s_y1ln[t, :, :],
                                               waits=[ev_ylfree[b]] if ev_ylfree[b] else [])
                    ev_tr = transpose16(trpsB, lambda ic, bb=b: ylstage[bb][:, ic * 128:(ic + 1) * 128],
                                        lambda g: ylT.ap()[:, g * 4:(g + 1) * 4, :].rearrange("p c n -> p (c n)"),
                                        ev_yl[b], ev_ylT_cons)
                    ev_ylfree[b] = ev_tr
                    for ic in range(NIC):
                        d_ylTst.start(YE, s_ylT[:, ic, t * 128:(t + 1) * 128], ylT[:, ic, :],
                                      waits=[ev_tr] if ic == 0 else [])
                    ev_ylT_cons = d_ylTst.ev()
                barrier()

            with ExitStack() as bctx:
                Kt = bctx.enter_context(nc.sbuf_tensor("Kt", [128, 2, T], BF16))
                Vh = bctx.enter_context(nc.sbuf_tensor("Vh", [128, NT, 258], BF16))
                Qt = bctx.enter_context(nc.sbuf_tensor("Qt", [128, 2, OWN * 128], BF16))
                Pt = [bctx.enter_context(nc.sbuf_tensor(f"Pt{i}", [128, 512], BF16)) for i in range(2)]
                wk_sl = bctx.enter_context(nc.sbuf_tensor("wk_sl", [128, NIC, 256], BF16))
                wv_sl = bctx.enter_context(nc.sbuf_tensor("wv_sl", [128, NIC, 256], BF16))
                wq_sl = bctx.enter_context(nc.sbuf_tensor("wq_sl", [128, NIC, 256], BF16))
                ylstr0 = bctx.enter_context(nc.sbuf_tensor("ylstr0", [128, NIC, 512], BF16))
                ylstr = [ylstr0, ylstr0]
                attn_sb = bctx.enter_context(nc.sbuf_tensor("attn_sb", [128, 128], BF16))
                attnT = bctx.enter_context(nc.sbuf_tensor("attnT", [128, 2, OWN * 128], BF16))

                YE.wait(d_ylTst.ev())
                for hg in range(H // 2):
                    barrier()
                    SE.wait(d_atst.ev())
                    h0 = 2 * hg
                    d_wsl.start(GE, wk_sl[:, :, :],
                                wkT.rearrange("(c p) o -> p c o", p=128)[:, :, h0 * 128:(h0 + 2) * 128])
                    d_wsl.start(GE, wv_sl[:, :, :],
                                wvT.rearrange("(c p) o -> p c o", p=128)[:, :, h0 * 128:(h0 + 2) * 128])
                    d_wsl.start(GE, wq_sl[:, :, :],
                                wqT.rearrange("(c p) o -> p c o", p=128)[:, :, h0 * 128:(h0 + 2) * 128])
                    ev_wall = d_wsl.ev()
                    VE.op("memset", Vh.ap()[:, :, 128:129], 1.0)
                    ev_onescol = VE.op("memset", Vh.ap()[:, :, 257:258], 1.0)

                    with ExitStack() as pctx:
                        psK = [pctx.enter_context(nc.psum_tensor(f"psK{hg}_{i}", [128, 512], F32)) for i in range(2)]
                        psV = [pctx.enter_context(nc.psum_tensor(f"psV{hg}_{i}", [128, 256], F32)) for i in range(2)]
                        psQ = pctx.enter_context(nc.psum_tensor(f"psQ{hg}", [128, 512], F32))
                        ev_str = [None, None]
                        ev_strfree = [None, None]
                        evK = {0: None, 1: None}
                        evV = {0: None, 1: None}
                        evQ = None
                        for t5 in range(8):
                            b = t5 % 2
                            ev_str[b] = d_ylTld[0].start(YE, ylstr[b][:, :, :],
                                                         s_ylT[:, :, t5 * 512:(t5 + 1) * 512],
                                                         waits=[ev_strfree[1 - b]] if ev_strfree[1 - b] else [])
                            for dc in range(2):
                                for ic in range(NIC):
                                    w = [ev_wall, ev_str[b], evK[dc]] if ic == 0 else []
                                    TE.op("matmul", psK[dc][:, :], wk_sl[:, ic, dc * 128:(dc + 1) * 128],
                                          ylstr[b][:, ic, :], start=(ic == 0), stop=(ic == NIC - 1),
                                          inc=(ic == NIC - 1), waits=w)
                                evK[dc] = SE.op("copy", Kt[:, dc, t5 * 512:(t5 + 1) * 512],
                                                psK[dc][:, :], waits=[TE.ev()])
                            for tc in range(4):
                                kt = t5 * 4 + tc
                                pb = tc % 2
                                for ic in range(NIC):
                                    w = [ev_wall, ev_str[b], evV[pb]] if ic == 0 else []
                                    TE.op("matmul", psV[pb][:, :],
                                          ylstr[b][:, ic, tc * 128:(tc + 1) * 128],
                                          wv_sl[:, ic, :], start=(ic == 0), stop=(ic == NIC - 1),
                                          inc=(ic == NIC - 1), waits=w)
                                ev_mm = TE.ev()
                                SE.op("copy", Vh[:, kt, 0:128], psV[pb][:, 0:128], waits=[ev_mm])
                                evV[pb] = SE.op("copy", Vh[:, kt, 129:257], psV[pb][:, 128:256])
                            if t5 < 4:
                                for dc in range(2):
                                    for ic in range(NIC):
                                        w = [ev_wall, ev_str[b], evQ] if ic == 0 else []
                                        TE.op("matmul", psQ[:, :], wq_sl[:, ic, dc * 128:(dc + 1) * 128],
                                              ylstr[b][:, ic, :], start=(ic == 0), stop=(ic == NIC - 1),
                                              inc=(ic == NIC - 1), waits=w)
                                    evQ = SE.op("activation",
                                                out=Qt[:, dc, t5 * 512:(t5 + 1) * 512],
                                                in_=psQ[:, :], func=AF.Copy,
                                                scale=st_schs[:, 0:1], waits=[TE.ev()])
                            ev_strfree[b] = TE.ev()
                        ev_proj = [evK[0], evK[1], evV[0], evV[1], evQ]

                        ev_negc = [None, None]
                        ev_sqr = None
                        for dc in range(2):
                            ev_sq = SE.op("activation", out=xq[:, :], in_=Qt[:, dc, :],
                                          func=AF.Square,
                                          waits=ev_proj + [ev_negc[0], ev_sqr, TE.ev()])
                            for c4 in range(4):
                                ev_mm = TE.op("matmul", psQ[:, :], ones_b[:, :],
                                              xq[:, c4 * 512:(c4 + 1) * 512], start=True,
                                              stop=True, waits=[ev_sq, ev_onesb, evQ,
                                                                ev_negc[0], ev_negc[1],
                                                                VE.ev()])
                                VE.op("reduce_max", st_gm[:, :], psQ[:, :], axis=AX.X,
                                      waits=[ev_mm])
                                if c4 == 0:
                                    VE.op("tensor_copy", st_am[:, :], st_gm[:, :])
                                else:
                                    VE.op("tensor_max", st_am[:, :], st_am[:, :], st_gm[:, :])
                            ev_q2 = VE.ev()
                            ev_first = None
                            for half in range(2):
                                ev_sqr = SE.op("activation", out=xq[:, :],
                                               in_=Kt[:, dc, half * D:(half + 1) * D],
                                               func=AF.Square, waits=[ev_q2, TE.ev()])
                                for c4 in range(4):
                                    ev_mm = TE.op("matmul", psQ[:, :], ones_b[:, :],
                                                  xq[:, c4 * 512:(c4 + 1) * 512], start=True,
                                                  stop=True, waits=[ev_sqr, VE.ev()])
                                    VE.op("reduce_max", st_gm[:, :], psQ[:, :], axis=AX.X,
                                          waits=[ev_mm])
                                    if half == 0 and c4 == 0:
                                        VE.op("tensor_copy", st_mx[:, :], st_gm[:, :])
                                    else:
                                        VE.op("tensor_max", st_mx[:, :], st_mx[:, :], st_gm[:, :])
                            VE.op("tensor_mul", st_gm[:, :], st_am[:, :], st_mx[:, :])
                            ev_m2 = VE.ev()
                            ev_rt = SE.op("activation", out=st_gm[:, :], in_=st_gm[:, :],
                                          func=AF.Sqrt, waits=[ev_m2])
                            ev_negc[dc] = VE.op("tensor_scalar", out=negc[dc][:, :],
                                                in0=st_gm[:, :], scalar1=-1.0,
                                                scalar2=None, op0=ALU.mult, waits=[ev_rt])
                            if "att2" in dbg and hg == 0 and dc == 0:
                                VE.op("tensor_copy", junk[:, 0:1], st_am[:, :])
                                VE.op("tensor_copy", junk[:, 1:2], st_mx[:, :])
                                VE.op("tensor_copy", junk[:, 2:3], negc[0][:, :])
                                d_dbg.start(YE, dbg_t["d_ch"][:, :], junk[:, 0:12], waits=[VE.ev()])
                                VE.op("memset", junk[:, 0:1], 0.0, waits=[d_dbg.ev()])

                    if "kvq" in dbg and hg == 0:
                        ev_d = None
                        for dc in range(2):
                            for half in range(2):
                                VE.op("tensor_copy", junk[:, :], Kt[:, dc, half * D:(half + 1) * D],
                                      waits=ev_proj + [ev_d])
                                ev_d = d_dbg.start(YE, dbg_t["d_k"][:, dc, half * D:(half + 1) * D],
                                                   junk[:, :], waits=[VE.ev()])
                            VE.op("tensor_copy", junk[:, :], Qt[:, dc, :], waits=ev_proj + [ev_d])
                            ev_d = d_dbg.start(YE, dbg_t["d_q"][:, dc, :], junk[:, :], waits=[VE.ev()])
                        for kt in range(NT):
                            VE.op("tensor_copy", junk[:, 0:258], Vh[:, kt, :], waits=ev_proj + [ev_d])
                            ev_d = d_dbg.start(YE, dbg_t["d_v"][:, kt, :], junk[:, 0:258], waits=[VE.ev()])
                        VE.op("memset", junk[:, 0:1], 0.0, waits=[ev_d])

                    with ExitStack() as actx2:
                        Sps = [actx2.enter_context(nc.psum_tensor(f"Sps{hg}_{i}", [128, 512], F32)) for i in range(2)]
                        Ops = [actx2.enter_context(nc.psum_tensor(f"Ops{hg}_{i}", [128, 129], F32)) for i in range(4)]
                        ev_exp = {0: None, 1: None}
                        ev_pvd = {0: None, 1: None}
                        ev_o_cons = None
                        ev_att_st = None
                        for dc in range(2):
                            h = h0 + dc
                            for qg in range(4):
                                i_lo = qg * 4
                                i_hi = i_lo + 4
                                kts = [(0, k) for k in range(i_hi)] + [(1, k) for k in range(i_hi)]
                                for it, (side, k) in enumerate(kts):
                                    vlo = max(i_lo, k)
                                    off = (vlo - i_lo) * 128
                                    b = it % 2
                                    kcol = (k if side == 0 else 16 + k) * 128
                                    first = [ev_proj[0], ev_proj[1], ev_proj[4],
                                             ev_negc[0], ev_negc[1]]
                                    if ev_exp[b] is not None:
                                        first.append(ev_exp[b])
                                    if ev_o_cons is not None:
                                        first.append(ev_o_cons)
                                    ev_sc = TE.op("matmul", Sps[b][:, off:512],
                                                  Kt[:, dc, kcol:kcol + 128],
                                                  Qt[:, dc, i_lo * 128 + off: (i_lo + 4) * 128],
                                                  start=True, stop=True, waits=first)
                                    ev_e = SE.op("activation", out=Pt[b][:, off:512],
                                                 in_=Sps[b][:, off:512], func=AF.Exp,
                                                 bias=negc[dc][:, 0:1],
                                                 waits=[ev_sc, ev_negc[dc], ev_pvd[b]])
                                    ev_m = ev_e
                                    if side == 0 and k >= i_lo:
                                        c0 = (k - i_lo) * 128
                                        ev_m = VE.op("tensor_mul", Pt[b][:, c0:c0 + 128],
                                                     Pt[b][:, c0:c0 + 128], maskd[:, :],
                                                     waits=[ev_e, ev_consts])
                                    elif side == 1 and k >= i_lo:
                                        c0 = (k - i_lo) * 128
                                        ev_m = VE.op("tensor_scalar", out=Pt[b][:, c0:c0 + 128],
                                                     in0=Pt[b][:, c0:c0 + 128],
                                                     scalar1=msel[:, k:k + 1], scalar2=None,
                                                     op0=ALU.mult, waits=[ev_e, ev_consts])
                                    for i in range(vlo, i_hi):
                                        li = i - i_lo
                                        w = [ev_m, ev_onescol, ev_proj[2], ev_proj[3]] if i == vlo else []
                                        TE.op("matmul", Ops[li][:, :],
                                              Pt[b][:, li * 128:(li + 1) * 128],
                                              Vh[:, (k if side == 0 else 16 + k), dc * 129:(dc + 1) * 129],
                                              start=(side == 0 and k == 0),
                                              stop=(side == 1 and k == i),
                                              inc=(i == i_hi - 1), waits=w)
                                    ev_pvd[b] = TE.ev()
                                    if ("att2" in dbg and hg == 0 and dc == 0 and qg == 0
                                            and side == 0 and k == 0):
                                        VE.op("tensor_copy", junk[:, 0:512], Pt[b][:, :],
                                              waits=[ev_m])
                                        d_dbg.start(YE, dbg_t["d_p0"][:, :], junk[:, 0:512],
                                                    waits=[VE.ev()])
                                ev_pv_all = TE.ev()
                                if "att2" in dbg and hg == 0 and dc == 0 and qg == 0:
                                    for li4 in range(4):
                                        VE.op("tensor_copy", junk2[:, li4 * 129:(li4 + 1) * 129],
                                              Ops[li4][:, :], waits=[ev_pv_all, d_dbg.ev()])
                                    VE.op("tensor_copy", junk2[:, 516:517], negc[0][:, :],
                                          waits=[ev_negc[0], ev_negc[1]])
                                    VE.op("tensor_copy", junk2[:, 517:518], negc[1][:, :])
                                    ev_dd = VE.ev()
                                    for li4 in range(4):
                                        d_dbg.start(YE, dbg_t["d_o0"][:, li4, :],
                                                    junk2[:, li4 * 129:(li4 + 1) * 129],
                                                    waits=[ev_dd])
                                    d_dbg.start(YE, dbg_t["d_nc"][:, :], junk2[:, 516:518],
                                                waits=[ev_dd])
                                    VE.op("memset", junk2[:, 0:1], 0.0, waits=[d_dbg.ev()])
                                for li in range(4):
                                    i = i_lo + li
                                    VE.op("reciprocal", st_ra[:, :], Ops[li][:, 128:129],
                                          waits=[ev_pv_all])
                                    ev_at = VE.op("tensor_scalar", out=attn_sb[:, :],
                                                  in0=Ops[li][:, 0:128],
                                                  scalar1=st_ra[:, 0:1], scalar2=None, op0=ALU.mult,
                                                  waits=[ev_att_st] if ev_att_st else [])
                                    ev_tr = TE.op("matmul", Sps[0][:, 0:128], attn_sb[:, :],
                                                  identb[:, :], start=True, stop=True,
                                                  waits=[ev_at, ev_exp[0], ev_exp[1]])
                                    ev_att_st = SE.op("copy", attnT[:, dc, i * 128:(i + 1) * 128],
                                                      Sps[0][:, 0:128], waits=[ev_tr])
                                    ev_exp[0] = ev_att_st
                                ev_o_cons = ev_att_st
                            d_atst.start(YE, s_attnT[h, :, :], attnT[:, dc, :], waits=[ev_att_st])
                            if "attn" in dbg:
                                VE.op("tensor_copy", junk[:, :], attnT[:, dc, :],
                                      waits=[ev_att_st, d_dbg.ev()])
                                d_dbg.start(YE, dbg_t["d_at"][h, :, :], junk[:, :], waits=[VE.ev()])

                barrier()

        if upto >= 'C':
            with ExitStack() as cctx:
                u2 = cctx.enter_context(nc.sbuf_tensor("u2", [128, NIC * D], BF16))
                u2_wo = u2.ap().rearrange("p (c n) -> p c n", n=D)
                atb_0 = cctx.enter_context(nc.sbuf_tensor("atb_0", [128, NIC, 128], BF16))
                atb = [atb_0, atb_0]
                y1s_0 = cctx.enter_context(nc.sbuf_tensor("y1s_0", [128, D], F32))
                y1s = [y1s_0, y1s_0]
                y2 = cctx.enter_context(nc.sbuf_tensor("y2", [128, D], F32))
                y3_0 = cctx.enter_context(nc.sbuf_tensor("y3_0", [128, D], F32))
                y3 = [y3_0, y3_0]
                xiC = cctx.enter_context(nc.sbuf_tensor("xiC", [128, D], mybir.dt.int32))
                ln_quant.xi[0] = xiC
                ops_ps = cctx.enter_context(nc.psum_tensor("ops_ps", [128, D], F32))
                fps = cctx.enter_context(nc.psum_tensor("fps", [128, D], F32))

                YE.wait(d_atst.ev())
                YE.wait(d_y1st[0].ev())
                YE.wait(d_y1st[1].ev())
                d_wcd.start(GE, u2_wo[:, :, :], woT.rearrange("(c p) o -> p c o", p=128))
                ev_wo_ld = d_wcd.ev()

                ev_at = [None, None]
                ev_atfree = [None, None]
                ev_y1l = [None, None]
                ev_y1lfree = [None, None]
                ev_y2_cons = None
                ev_y3free = [None, None]
                ev_xqT2_cons = None
                ev_ffn2_done = None
                s_at_v = s_attnT.rearrange("h p q -> p h q")
                for oi in range(OWN):
                    b = oi % 2
                    for half in range(2):
                        ev_at[b] = d_atld[0].start(
                            YE, atb[b][:, half * 8:(half + 1) * 8, :],
                            s_at_v[:, half * 8:(half + 1) * 8, oi * 128:(oi + 1) * 128],
                            waits=[ev_atfree[1 - b]] if (ev_atfree[1 - b] and half == 0) else [])
                    ev_y1l[b] = d_y1ld[0].start(YE, y1s[b][:, :], s_y1own[oi, :, :],
                                                waits=[ev_y1lfree[1 - b]] if ev_y1lfree[1 - b] else [])
                    ln_quant.qdbg[0] = (dbg_t["d_qd"] if ("qdbg" in dbg and oi == 8) else None)
                    for oc in range(4):
                        for ic in range(NIC):
                            w = []
                            if ic == 0 and oc == 0:
                                w = [ev_at[b], ev_wo_ld]
                                if ev_y2_cons is not None:
                                    w.append(ev_y2_cons)
                                if ev_xqT2_cons is not None:
                                    w.append(ev_xqT2_cons)
                            TE.op("matmul", ops_ps[:, oc * 512:(oc + 1) * 512], atb[b][:, ic, :],
                                  u2_wo[:, ic, oc * 512:(oc + 1) * 512],
                                  start=(ic == 0), stop=(ic == NIC - 1),
                                  inc=(ic == NIC - 1 and oc == 3), waits=w)
                    ev_oproj = TE.ev()
                    ev_atfree[b] = ev_oproj
                    VE.op("scalar_tensor_tensor", out=y2[:, :], in0=ops_ps[:, :], scalar=1.0,
                          in1=y1s[b][:, :], op0=ALU.mult, op1=ALU.add,
                          waits=[ev_oproj, ev_y1l[b]])
                    ev_y2 = VE.ev()
                    if "y2" in dbg:
                        ev_y2 = d_dbgY[b].start(YE, dbg_t["d_y2"][oi, :, :], y2[:, :],
                                                waits=[ev_y2])
                    ev_y1lfree[b] = ev_y2
                    ev_xq2 = ln_quant(y2[:, :], [ev_y2])
                    if "xq2" in dbg:
                        ev_d1 = d_dbgG2[oi % 2].start(GE, dbg_t["d_xq2"][oi, :, :], xq[:, :],
                                                      waits=[ev_xq2])
                        VE.op("tensor_copy", junk[:, 0:1], st_a[:, :],
                              waits=[ev_xq2] if oi == 0 else [ev_xq2, ev_d2])
                        VE.op("tensor_copy", junk[:, 1:2], st_st[:, :])
                        ev_d2 = d_dbgY[0].start(YE, dbg_t["d_st2"][:, :], junk[:, 0:2],
                                                waits=[VE.ev()])
                        VE.op("memset", junk[:, 0:1], 0.0, waits=[ev_d1, ev_d2])
                        ln_quant.ge_guard[0] = ev_d1
                    ev_evac = {0: None, 1: None}
                    ev_xqT2 = None
                    for g in range(4):
                        pb = g % 2
                        for kk in range(4):
                            ic = g * 4 + kk
                            w = []
                            if kk == 0:
                                w = [ev_xq2, ev_consts, ev_evac[pb]]
                                if g == 0:
                                    w.append(ev_y2)
                            TE.op("matmul",
                                  ops_ps[:, pb * 512 + kk * 128: pb * 512 + (kk + 1) * 128],
                                  xq[:, ic * 128:(ic + 1) * 128], identb[:, :],
                                  start=True, stop=True, inc=(kk == 3), waits=w)
                        ev_mm = TE.ev()
                        ev_xqT2 = SE.op("copy",
                                        xqT.ap()[:, g * 4:(g + 1) * 4, :].rearrange("p c n -> p (c n)"),
                                        ops_ps[:, pb * 512:(pb + 1) * 512], waits=[ev_mm])
                        ev_evac[pb] = ev_xqT2
                    ev_y2_cons = ev_xqT2
                    for oc in range(4):
                        for ic in range(NIC):
                            w = []
                            if ic == 0 and oc == 0:
                                w = [ev_xqT2]
                                if ev_ffn2_done is not None:
                                    w.append(ev_ffn2_done)
                            TE.op("matmul", fps[:, oc * 512:(oc + 1) * 512], xqT[:, ic, :],
                                  u1_wq[:, ic, oc * 512:(oc + 1) * 512],
                                  start=(ic == 0), stop=(ic == NIC - 1),
                                  inc=(ic == NIC - 1 and oc == 3), waits=w)
                    ev_ffn2 = TE.ev()
                    ev_xqT2_cons = ev_ffn2
                    VE.op("scalar_tensor_tensor", out=y3[b][:, :], in0=fps[:, :],
                          scalar=st_st[:, 0:1], in1=y2[:, :], op0=ALU.mult, op1=ALU.add,
                          waits=[ev_ffn2] + ([ev_y3free[1 - b]] if ev_y3free[1 - b] else []))
                    ev_y3 = VE.ev()
                    ev_ffn2_done = ev_y3
                    ev_y3free[b] = d_out.start(YE, y_out[oi, :, :], y3[b][:, :], waits=[ev_y3])

        YE.wait(d_out.ev())
        if dbg:
            YE.wait(d_dbg.ev())
            for dd in d_dbgY:
                YE.wait(dd.ev())
            for dd in d_dbgG2:
                GE.wait(dd.ev())

        blk = ctx.enter_context(nc.Block())

        @blk.tensor
        def _(e):
            for f in TE.todo:
                f(e)

        @blk.vector
        def _(e):
            for f in VE.todo:
                f(e)

        @blk.scalar
        def _(e):
            for f in SE.todo:
                f(e)

        @blk.gpsimd
        def _(e):
            for f in GE.todo:
                f(e)

        @blk.sync
        def _(e):
            for f in YE.todo:
                f(e)

    return nc


def host_inputs(x, wq, wk, wv, wo, wf):
    x = np.asarray(x, dtype=np.float32)
    wts = {n: np.ascontiguousarray(np.asarray(w, dtype=np.float32).T)
           for n, w in (("wqT", wq), ("wkT", wk), ("wvT", wv), ("woT", wo), ("wfT", wf))}
    ident = np.eye(128, dtype=np.float32)
    kk, qq = np.meshgrid(np.arange(128), np.arange(128), indexing="ij")
    maskd = (kk <= qq).astype(np.float32)
    in_maps = []
    for c in range(8):
        b, j = c // 2, c % 2
        ow = own_tiles(j)
        ot = [t for t in range(NT) if t not in ow]
        perm = ow + ot
        xb = x[b].reshape(NT, 128, D)[perm].reshape(T, D)
        msel = np.zeros((128, OWN), dtype=np.float32)
        for i in range(OWN):
            msel[:, i] = 1.0 if ot[i] < ow[i] else 0.0
        schs = np.full((128, 1), np.float32(np.sqrt(np.float32(128.0))), dtype=np.float32)
        m = {"x0": np.ascontiguousarray(xb), "ident_in": ident, "maskd_in": maskd,
             "msel_in": msel, "schs_in": schs}
        m.update(wts)
        in_maps.append(m)
    return in_maps


def kernel(L0_attn_x, L0_attn_wq, L0_attn_wk, L0_attn_wv, L0_attn_wo, L0_ffn_weight):
    in_maps = host_inputs(L0_attn_x, L0_attn_wq, L0_attn_wk, L0_attn_wv,
                          L0_attn_wo, L0_ffn_weight)
    nc = build()
    res = run_bass_kernel_spmd(nc, in_maps, core_ids=list(range(8)))
    out = np.empty((4, T, D), dtype=np.float32)
    for c in range(8):
        b, j = c // 2, c % 2
        ow = own_tiles(j)
        shard = res.results[c]["y_out"]
        for i, t in enumerate(ow):
            out[b, t * 128:(t + 1) * 128, :] = shard[i]
    return out


# version 08
# speedup vs baseline: 1.1634x; 1.1634x over previous
import numpy as np
import concourse.bass as bass
import concourse.mybir as mybir
from concourse.bass_utils import run_bass_kernel_spmd
from contextlib import ExitStack

F32 = mybir.dt.float32
BF16 = mybir.dt.bfloat16
AF = mybir.ActivationFunctionType
ALU = mybir.AluOpType
AX = mybir.AxisListType

D = 2048
T = 4096
NT = 32
NIC = 16
H = 16
HS = 128
OWN = 16
NEG_SQRT_HS = -float(np.sqrt(np.float64(HS)))
INV_SQRT_HS = float(1.0 / np.sqrt(np.float64(HS)))
EPS = 1e-5
SH = 256.5
WMEAN_INV = 1.0 / (D * D)


def own_tiles(j):
    return [t for t in range(NT) if (0 if (t % 4) in (0, 3) else 1) == j]


class Ev:
    __slots__ = ("sem", "val")

    def __init__(self, sem, val):
        self.sem = sem
        self.val = val


class Eng:
    def __init__(self, sem, auto_serial):
        self.sem = sem
        self.cnt = 0
        self.auto = auto_serial
        self.waited = {}
        self.todo = []

    def wait(self, ev):
        if ev is None:
            return
        key = id(ev.sem)
        if self.waited.get(key, 0) >= ev.val:
            return
        self.waited[key] = ev.val
        sem, val = ev.sem, ev.val
        self.todo.append(lambda e: e.wait_ge(sem, val))

    def waits(self, evs):
        for ev in evs:
            self.wait(ev)

    def op(self, name, *args, inc=True, waits=(), **kw):
        self.waits(waits)
        if self.auto and self.cnt > 0:
            self.wait(Ev(self.sem, self.cnt))
        sem = self.sem
        if inc:
            self.cnt += 1

            def emit(e, name=name, args=args, kw=kw, sem=sem):
                getattr(e, name)(*args, **kw).then_inc(sem, 1)
        else:

            def emit(e, name=name, args=args, kw=kw):
                getattr(e, name)(*args, **kw)
        self.todo.append(emit)
        return Ev(self.sem, self.cnt) if inc else None

    def ev(self):
        return Ev(self.sem, self.cnt)


class DmaBuf:
    def __init__(self, nc, ctx, name):
        self.sem = ctx.enter_context(nc.semaphore(name))
        self.cum = 0

    def start(self, qeng, dst, src, waits=()):
        qeng.waits(waits)
        sem = self.sem
        self.cum += 16
        qeng.todo.append(lambda e, d=dst, s=src, sem=sem: e.dma_start(d, s).then_inc(sem, 16))
        return Ev(sem, self.cum)

    def ev(self):
        return Ev(self.sem, self.cum)


def build(dbg=(), upto='Z'):
    dbg = set(dbg)
    nc = bass.Bass("TRN2", target_bir_lowering=False)

    x0 = nc.dram_tensor("x0", [T, D], F32, kind="ExternalInput").ap()
    wqT = nc.dram_tensor("wqT", [D, D], F32, kind="ExternalInput").ap()
    wkT = nc.dram_tensor("wkT", [D, D], F32, kind="ExternalInput").ap()
    wvT = nc.dram_tensor("wvT", [D, D], F32, kind="ExternalInput").ap()
    woT = nc.dram_tensor("woT", [D, D], F32, kind="ExternalInput").ap()
    wfT = nc.dram_tensor("wfT", [D, D], F32, kind="ExternalInput").ap()
    ident_in = nc.dram_tensor("ident_in", [128, 128], F32, kind="ExternalInput").ap()
    maskd_in = nc.dram_tensor("maskd_in", [128, 128], F32, kind="ExternalInput").ap()
    msel_in = nc.dram_tensor("msel_in", [128, OWN], F32, kind="ExternalInput").ap()
    schs_in = nc.dram_tensor("schs_in", [128, 1], F32, kind="ExternalInput").ap()

    y_out = nc.dram_tensor("y_out", [OWN, 128, D], F32, kind="ExternalOutput").ap()
    dbg_t = {}
    if "y1" in dbg:
        dbg_t["d_y1"] = nc.dram_tensor("d_y1", [NT, 128, D], F32, kind="ExternalOutput").ap()
    if "y1ln" in dbg:
        dbg_t["d_y1ln"] = nc.dram_tensor("d_y1ln", [NT, 128, D], F32, kind="ExternalOutput").ap()
    if "kvq" in dbg:
        dbg_t["d_k"] = nc.dram_tensor("d_k", [128, 2, T], F32, kind="ExternalOutput").ap()
        dbg_t["d_v"] = nc.dram_tensor("d_v", [128, NT, 258], F32, kind="ExternalOutput").ap()
        dbg_t["d_q"] = nc.dram_tensor("d_q", [128, 2, OWN * 128], F32, kind="ExternalOutput").ap()
    if "attn" in dbg:
        dbg_t["d_at"] = nc.dram_tensor("d_at", [H, 128, OWN * 128], F32, kind="ExternalOutput").ap()
    if "y2" in dbg:
        dbg_t["d_y2"] = nc.dram_tensor("d_y2", [OWN, 128, D], F32, kind="ExternalOutput").ap()
    if "qdbg" in dbg:
        dbg_t["d_qd"] = nc.dram_tensor("d_qd", [4, 128, D], F32, kind="ExternalOutput").ap()
    if "xq2" in dbg:
        dbg_t["d_xq2"] = nc.dram_tensor("d_xq2", [OWN, 128, D], F32, kind="ExternalOutput").ap()
        dbg_t["d_st2"] = nc.dram_tensor("d_st2", [128, 2], F32, kind="ExternalOutput").ap()
    if "att2" in dbg:
        dbg_t["d_o0"] = nc.dram_tensor("d_o0", [128, 4, 129], F32, kind="ExternalOutput").ap()
        dbg_t["d_nc"] = nc.dram_tensor("d_nc", [128, 2], F32, kind="ExternalOutput").ap()
        dbg_t["d_p0"] = nc.dram_tensor("d_p0", [128, 512], F32, kind="ExternalOutput").ap()
        dbg_t["d_ch"] = nc.dram_tensor("d_ch", [128, 12], F32, kind="ExternalOutput").ap()

    s_y1ln = nc.dram_tensor("s_y1ln", [NT, 128, D], BF16, kind="Internal").ap()
    s_y1own = nc.dram_tensor("s_y1own", [OWN, 128, D], F32, kind="Internal").ap()
    s_ylT = nc.dram_tensor("s_ylT", [128, NIC, T], BF16, kind="Internal").ap()
    s_attnT = nc.dram_tensor("s_attnT", [H, 128, OWN * 128], BF16, kind="Internal").ap()

    with ExitStack() as ctx:
        TE = Eng(ctx.enter_context(nc.semaphore("sem_t")), False)
        VE = Eng(ctx.enter_context(nc.semaphore("sem_v")), True)
        SE = Eng(ctx.enter_context(nc.semaphore("sem_s")), True)
        GE = Eng(ctx.enter_context(nc.semaphore("sem_g")), False)
        YE = Eng(ctx.enter_context(nc.semaphore("sem_y")), False)
        ALL = [TE, VE, SE, GE, YE]

        ALL_DMABUFS = []

        def barrier():
            evs = [g.ev() for g in ALL]
            for g in ALL:
                for ev in evs:
                    if ev.sem is not g.sem:
                        g.wait(ev)
                for db in ALL_DMABUFS:
                    if db.cum:
                        g.wait(db.ev())

        def DB(name):
            b = DmaBuf(nc, ctx, name)
            ALL_DMABUFS.append(b)
            return b

        d_x0 = [DB(f"dm_x{i}") for i in range(2)]
        d_wld = [DB(f"dm_w{i}") for i in range(2)]
        d_y1lnst = [DB(f"dm_ylst{i}") for i in range(2)]
        d_y1st = [DB(f"dm_y1st{i}") for i in range(2)]
        d_const = DB("dm_const")
        d_ylld = [DB(f"dm_ylld{i}") for i in range(2)]
        d_ylTst = DB("dm_ylTst")
        d_ylTld = [DB(f"dm_ylTld{i}") for i in range(2)]
        d_wsl = DB("dm_wsl")
        d_atst = DB("dm_atst")
        d_atld = [DB(f"dm_atld{i}") for i in range(2)]
        d_y1ld = [DB(f"dm_y1ld{i}") for i in range(2)]
        d_wcd = DB("dm_wcd")
        d_out = DB("dm_out")
        d_dbg = DB("dm_dbg")
        d_dbgY = [DB(f"dm_dbgY{i}") for i in range(2)]
        d_dbgG2 = [DB(f"dm_dbgG{i}") for i in range(2)]

        identb = ctx.enter_context(nc.sbuf_tensor("identb", [128, 128], BF16))
        identf = ctx.enter_context(nc.sbuf_tensor("identf", [128, 128], F32))
        maskd = ctx.enter_context(nc.sbuf_tensor("maskd", [128, 128], BF16))
        msel = ctx.enter_context(nc.sbuf_tensor("msel", [128, OWN], F32))
        ones_f = ctx.enter_context(nc.sbuf_tensor("ones_f", [128, 128], F32))
        pad_f = ctx.enter_context(nc.sbuf_tensor("pad_f", [128, 128], F32))
        u1 = ctx.enter_context(nc.sbuf_tensor("u1", [128, NIC * D], BF16))
        u1_wq = u1.ap().rearrange("p (c n) -> p c n", n=D)
        xm = ctx.enter_context(nc.sbuf_tensor("xm", [128, D], F32))
        xq = ctx.enter_context(nc.sbuf_tensor("xq", [128, D], BF16))
        xqT = ctx.enter_context(nc.sbuf_tensor("xqT", [128, NIC, 128], BF16))
        junk = ctx.enter_context(nc.sbuf_tensor("junk", [128, D], F32))
        junk2 = ctx.enter_context(nc.sbuf_tensor("junk2", [128, D], F32))
        junk_s = ctx.enter_context(nc.sbuf_tensor("junk_s", [128, D], F32))

        def vec(name):
            return ctx.enter_context(nc.sbuf_tensor(name, [128, 1], F32))

        st_cs = vec("st_cs")
        st_sq = vec("st_sq")
        st_m = vec("st_m")
        st_msq = vec("st_msq")
        st_var = vec("st_var")
        st_sig = vec("st_sig")
        st_inv = vec("st_inv")
        st_am = vec("st_am")
        st_a = vec("st_a")
        st_ra = vec("st_ra")
        st_g = vec("st_g")
        st_st = vec("st_st")
        wsc = vec("wsc")
        wsc_inv = vec("wsc_inv")
        wscB = vec("wscB")
        st_mx = vec("st_mx")
        st_gm = vec("st_gm")
        negc = [vec("negc0"), vec("negc1")]
        wabs = ctx.enter_context(nc.sbuf_tensor("wabs", [128, NIC], F32))

        ev_c = None
        for dst, src in ((identb, ident_in), (identf, ident_in), (maskd, maskd_in),
                         (msel, msel_in)):
            ev_c = d_const.start(GE, dst[:, :], src[:, :])
        ev_consts = ev_c
        ev_ones = VE.op("memset", ones_f[:, :], 1.0)
        ones_b = ctx.enter_context(nc.sbuf_tensor("ones_b", [128, 128], BF16))
        ev_onesb = VE.op("memset", ones_b[:, :], 1.0)
        st_eps = vec("st_eps")
        ev_eps = VE.op("memset", st_eps[:, :], EPS)
        st_schs = vec("st_schs")
        st_r127 = vec("st_r127")
        schs_sb = ctx.enter_context(nc.sbuf_tensor("schs_sb", [128, 1], F32))
        ev_schs_ld = d_const.start(GE, schs_sb[:, :], schs_in[:, :])
        ev_consts = ev_schs_ld
        VE.op("tensor_copy", st_schs[:, :], schs_sb[:, :], waits=[ev_schs_ld])
        VE.op("memset", st_r127[:, :], 127.0)
        ev_r127 = VE.op("reciprocal", st_r127[:, :], st_r127[:, :])

        def ln_quant(src_ap, waits_src):
            VE.op("reduce_sum", st_cs[:, :], src_ap, axis=AX.X, waits=waits_src)
            SE.op("activation", out=junk_s[:, :], in_=src_ap, func=AF.Square,
                  accum_out=st_sq[:, :], waits=waits_src)
            ev_sq = SE.ev()
            VE.op("tensor_scalar", out=st_m[:, :], in0=st_cs[:, :], scalar1=1.0 / D,
                  scalar2=None, op0=ALU.mult)
            VE.op("tensor_scalar", out=xm[:, :], in0=src_ap, scalar1=st_m[:, 0:1],
                  scalar2=None, op0=ALU.subtract)
            VE.op("reduce_max", st_am[:, :], xm[:, :], axis=AX.X, apply_absolute_value=True)
            VE.op("tensor_mul", st_msq[:, :], st_m[:, :], st_m[:, :])
            VE.op("scalar_tensor_tensor", out=st_var[:, :], in0=st_sq[:, :], scalar=1.0 / D,
                  in1=st_msq[:, :], op0=ALU.mult, op1=ALU.subtract, waits=[ev_sq])
            ev_var = VE.ev()
            ev_sig = SE.op("activation", out=st_sig[:, :], in_=st_var[:, :], func=AF.Sqrt,
                           bias=st_eps[:, 0:1], waits=[ev_var, ev_eps])
            VE.op("reciprocal", st_inv[:, :], st_sig[:, :], waits=[ev_sig])
            VE.op("tensor_mul", st_a[:, :], st_am[:, :], st_inv[:, :])
            VE.op("tensor_scalar_max", st_a[:, :], st_a[:, :], 1e-8)
            VE.op("reciprocal", st_ra[:, :], st_a[:, :])
            VE.op("tensor_scalar", out=st_ra[:, :], in0=st_ra[:, :], scalar1=127.0,
                  scalar2=None, op0=ALU.mult)
            VE.op("tensor_mul", st_g[:, :], st_inv[:, :], st_ra[:, :])
            xi = ln_quant.xi[0]
            VE.op("tensor_scalar", out=junk[:, :], in0=xm[:, :], scalar1=st_g[:, 0:1],
                  scalar2=SH, op0=ALU.mult, op1=ALU.add)
            VE.op("tensor_copy", xi[:, :], junk[:, :])
            VE.op("tensor_copy", junk2[:, :], xi[:, :])
            if ln_quant.qdbg[0] is not None:
                qd = ln_quant.qdbg[0]
                d_dbg.start(YE, qd[0], junk[:, :], waits=[VE.ev()])
                d_dbg.start(YE, qd[1], junk2[:, :], waits=[VE.ev()])
            VE.op("tensor_tensor", xm[:, :], junk[:, :], junk2[:, :], op=ALU.subtract)
            if ln_quant.qdbg[0] is not None:
                qd = ln_quant.qdbg[0]
                d_dbg.start(YE, qd[2], xm[:, :], waits=[VE.ev()])
            VE.op("tensor_scalar", out=junk[:, :], in0=xm[:, :], scalar1=1.0,
                  scalar2=None, op0=ALU.is_ge, waits=[d_dbg.ev()])
            VE.op("scalar_tensor_tensor", out=junk[:, :], in0=xm[:, :], scalar=2.0,
                  in1=junk[:, :], op0=ALU.is_ge, op1=ALU.add)
            VE.op("scalar_tensor_tensor", out=junk[:, :], in0=xm[:, :], scalar=0.0,
                  in1=junk[:, :], op0=ALU.is_lt, op1=ALU.subtract)
            VE.op("scalar_tensor_tensor", out=junk[:, :], in0=xm[:, :], scalar=-1.0,
                  in1=junk[:, :], op0=ALU.is_lt, op1=ALU.add)
            VE.op("tensor_tensor", junk[:, :], junk2[:, :], junk[:, :], op=ALU.subtract)
            VE.op("tensor_scalar", out=junk[:, :], in0=junk[:, :], scalar1=383.0,
                  scalar2=129.0, op0=ALU.min, op1=ALU.max)
            ev_xq = VE.op("tensor_scalar", out=xq[:, :], in0=junk[:, :], scalar1=256.0,
                          scalar2=None, op0=ALU.subtract, waits=[ln_quant.ge_guard[0]])
            VE.op("tensor_mul", st_st[:, :], st_a[:, :], wscB[:, :])
            return ev_xq

        ln_quant.ge_guard = [None]
        ln_quant.xi = [None]
        ln_quant.qdbg = [None]

        def transpose16(trps, src_fn, dst_fn, ev_src, ev_prev_cons):
            ev_evac = {0: None, 1: None}
            ev_out = None
            for g in range(4):
                b = g % 2
                for kk in range(4):
                    ic = g * 4 + kk
                    w = []
                    if kk == 0:
                        w = [ev_src, ev_consts, ev_evac[b]]
                        if g == 0 and ev_prev_cons is not None:
                            w.append(ev_prev_cons)
                    TE.op("matmul", trps[b][:, kk * 128:(kk + 1) * 128], src_fn(ic),
                          identb[:, :], start=True, stop=True, inc=(kk == 3), waits=w)
                ev_mm = TE.ev()
                ev_out = SE.op("copy", dst_fn(g), trps[b][:, :], waits=[ev_mm])
                ev_evac[b] = ev_out
            return ev_out

        def ln2_normalize(src_ap, dst_ap, ev_src, extra_waits=()):
            VE.op("reduce_sum", st_cs[:, :], src_ap, axis=AX.X, waits=[ev_src])
            SE.op("activation", out=junk_s[:, :], in_=src_ap, func=AF.Square,
                  accum_out=st_sq[:, :], waits=[ev_src])
            ev_sq = SE.ev()
            VE.op("tensor_scalar", out=st_m[:, :], in0=st_cs[:, :], scalar1=1.0 / D,
                  scalar2=None, op0=ALU.mult)
            VE.op("tensor_mul", st_msq[:, :], st_m[:, :], st_m[:, :])
            VE.op("scalar_tensor_tensor", out=st_var[:, :], in0=st_sq[:, :], scalar=1.0 / D,
                  in1=st_msq[:, :], op0=ALU.mult, op1=ALU.subtract, waits=[ev_sq])
            ev_var = VE.ev()
            ev_sig = SE.op("activation", out=st_sig[:, :], in_=st_var[:, :], func=AF.Sqrt,
                           bias=st_eps[:, 0:1], waits=[ev_var, ev_eps])
            VE.op("reciprocal", st_inv[:, :], st_sig[:, :], waits=[ev_sig])
            return VE.op("tensor_scalar", out=dst_ap, in0=src_ap, scalar1=st_m[:, 0:1],
                         scalar2=st_inv[:, 0:1], op0=ALU.subtract, op1=ALU.mult,
                         waits=list(extra_waits))

        with ExitStack() as actx:
            wbuf = [actx.enter_context(nc.sbuf_tensor(f"wbuf{i}", [128, D], F32)) for i in range(2)]
            xbuf = [actx.enter_context(nc.sbuf_tensor(f"xbuf{i}", [128, D], F32)) for i in range(2)]
            y1b = [actx.enter_context(nc.sbuf_tensor(f"y1b{i}", [128, D], F32)) for i in range(2)]
            y1ln = [actx.enter_context(nc.sbuf_tensor(f"y1ln{i}", [128, D], BF16)) for i in range(2)]
            xiA = actx.enter_context(nc.sbuf_tensor("xiA", [128, D], mybir.dt.int32))
            ln_quant.xi[0] = xiA
            ypsum = actx.enter_context(nc.psum_tensor("ypsum", [128, D], F32))
            trpsA = [actx.enter_context(nc.psum_tensor(f"trpsA{i}", [128, 512], F32)) for i in range(2)]

            ev_w = [None, None]
            ev_wfree = [None, None]
            for k in range(NIC):
                b = k % 2
                ev_w[b] = d_wld[b].start(GE, wbuf[b][:, :], wfT[k * 128:(k + 1) * 128, :],
                                         waits=[ev_wfree[b]] if ev_wfree[b] else [])
                ev_wfree[b] = SE.op("activation", out=junk_s[:, :], in_=wbuf[b][:, :],
                                    func=AF.Abs, accum_out=wabs[:, k:k + 1], waits=[ev_w[b]])
            ev_ws = VE.op("reduce_sum", st_cs[:, :], wabs[:, :], axis=AX.X,
                          waits=[ev_wfree[0], ev_wfree[1]])
            VE.op("tensor_copy", st_mx[:, :], st_cs[:, :])
            ev_wsv = VE.ev()
            ev_wsum = TE.op("matmul", trpsA[0][:, 0:1], ones_f[:, :], st_mx[:, :],
                            start=True, stop=True, waits=[ev_ones, ev_wsv])
            VE.op("tensor_scalar", out=wsc[:, :], in0=trpsA[0][:, 0:1],
                  scalar1=WMEAN_INV, scalar2=None, op0=ALU.mult, waits=[ev_wsum])
            VE.op("tensor_scalar_max", wsc[:, :], wsc[:, :], 1e-6)
            ev_wsci = VE.op("reciprocal", wsc_inv[:, :], wsc[:, :])
            VE.op("tensor_scalar", out=wscB[:, :], in0=wsc[:, :],
                  scalar1=0.5, scalar2=None, op0=ALU.mult)
            ev_wscB = VE.op("tensor_mul", wscB[:, :], wscB[:, :], st_r127[:, 0:1],
                            waits=[ev_r127])
            ev_wsum_read = ev_wscB

            ev_w = [None, None]
            for k in range(NIC):
                b = k % 2
                ev_w[b] = d_wld[b].start(GE, wbuf[b][:, :], wfT[k * 128:(k + 1) * 128, :],
                                         waits=[ev_wfree[b]] if ev_wfree[b] else [])
                xi = ln_quant.xi[0]
                VE.op("tensor_scalar", out=junk[:, :], in0=wbuf[b][:, :],
                      scalar1=wsc_inv[:, 0:1], scalar2=SH, op0=ALU.mult, op1=ALU.add,
                      waits=[ev_w[b], ev_wsci])
                ev_wfree[b] = VE.ev()
                VE.op("tensor_copy", xi[:, :], junk[:, :])
                VE.op("tensor_copy", junk2[:, :], xi[:, :])
                VE.op("tensor_tensor", xm[:, :], junk[:, :], junk2[:, :], op=ALU.subtract)
                VE.op("tensor_scalar", out=junk[:, :], in0=xm[:, :], scalar1=1.0,
                      scalar2=None, op0=ALU.is_ge)
                VE.op("tensor_scalar", out=xm[:, :], in0=xm[:, :], scalar1=0.0,
                      scalar2=None, op0=ALU.is_lt)
                VE.op("tensor_tensor", junk[:, :], junk[:, :], xm[:, :], op=ALU.subtract)
                VE.op("tensor_tensor", junk[:, :], junk[:, :], junk2[:, :], op=ALU.add)
                VE.op("tensor_scalar", out=junk[:, :], in0=junk[:, :], scalar1=257.0,
                      scalar2=255.0, op0=ALU.min, op1=ALU.max)
                VE.op("tensor_scalar", out=u1_wq[:, k, :], in0=junk[:, :], scalar1=256.0,
                      scalar2=None, op0=ALU.subtract)
            ev_wq_ready = VE.ev()
            if "wq" in dbg:
                for k in range(NIC):
                    VE.op("tensor_copy", junk[:, :], u1_wq[:, k, :], waits=[ev_wq_ready])
                    d_dbg.start(YE, dbg_t["d_wq"][:, k, :], junk[:, :], waits=[VE.ev()])

            ev_x = [None, None]
            ev_xfree = [None, None]
            ev_y1free = [[], []]
            ev_ylnfree = [[], []]
            ev_ffn_done = None
            ev_xqT_cons = None
            for t in range(NT):
                b = t % 2
                ev_x[b] = d_x0[b].start(YE, xbuf[b][:, :], x0[t * 128:(t + 1) * 128, :],
                                        waits=[ev_xfree[b]] if ev_xfree[b] else [])
                ev_xq = ln_quant(xbuf[b][:, :], [ev_x[b], ev_wscB])
                ev_xqT = transpose16(trpsA, lambda ic: xq[:, ic * 128:(ic + 1) * 128],
                                     lambda g: xqT.ap()[:, g * 4:(g + 1) * 4, :].rearrange("p c n -> p (c n)"),
                                     ev_xq, ev_wsum_read if t == 0 else None)
                for oc in range(4):
                    for ic in range(NIC):
                        w = []
                        if ic == 0 and oc == 0:
                            w = [ev_xqT, ev_wq_ready]
                            if ev_ffn_done is not None:
                                w.append(ev_ffn_done)
                            if ev_xqT_cons is not None:
                                w.append(ev_xqT_cons)
                        TE.op("matmul", ypsum[:, oc * 512:(oc + 1) * 512], xqT[:, ic, :],
                              u1_wq[:, ic, oc * 512:(oc + 1) * 512],
                              start=(ic == 0), stop=(ic == NIC - 1),
                              inc=(ic == NIC - 1 and oc == 3), waits=w)
                ev_ffn = TE.ev()
                ev_xqT_cons = ev_ffn
                yb = y1b[b]
                VE.op("scalar_tensor_tensor", out=yb[:, :], in0=ypsum[:, :],
                      scalar=st_st[:, 0:1], in1=xbuf[b][:, :], op0=ALU.mult, op1=ALU.add,
                      waits=[ev_ffn] + ev_y1free[b])
                ev_y1 = VE.ev()
                ev_xfree[b] = ev_y1
                ev_ffn_done = ev_y1
                ev_yln = ln2_normalize(yb[:, :], y1ln[b][:, :], ev_y1, ev_ylnfree[b])
                ev_st1 = d_y1lnst[b].start(YE, s_y1ln[t, :, :], y1ln[b][:, :], waits=[ev_yln])
                ev_ylnfree[b] = [ev_st1]
                ev_y1free[b] = [ev_y1]
                if t < OWN:
                    ev_y1free[b].append(d_y1st[b].start(YE, s_y1own[t, :, :], yb[:, :],
                                                        waits=[ev_y1]))
                if "y1" in dbg:
                    ev_y1free[b].append(d_dbgY[b].start(YE, dbg_t["d_y1"][t, :, :], yb[:, :],
                                                        waits=[ev_y1]))
                if "y1ln" in dbg:
                    ev_ylnfree[b].append(d_dbgG2[b].start(GE, dbg_t["d_y1ln"][t, :, :],
                                                          y1ln[b][:, :], waits=[ev_yln]))

            barrier()

        if upto >= 'B':
            with ExitStack() as bctx:
                ylstage = [bctx.enter_context(nc.sbuf_tensor(f"ylst{i}", [128, D], BF16)) for i in range(2)]
                ylT = bctx.enter_context(nc.sbuf_tensor("ylT", [128, NIC, 128], BF16))
                trpsB = [bctx.enter_context(nc.psum_tensor(f"trpsB{i}", [128, 512], F32)) for i in range(2)]
                YE.wait(d_y1lnst[0].ev())
                YE.wait(d_y1lnst[1].ev())
                ev_yl = [None, None]
                ev_ylfree = [None, None]
                ev_ylT_cons = None
                for t in range(NT):
                    b = t % 2
                    ev_yl[b] = d_ylld[b].start(YE, ylstage[b][:, :], s_y1ln[t, :, :],
                                               waits=[ev_ylfree[b]] if ev_ylfree[b] else [])
                    ev_tr = transpose16(trpsB, lambda ic, bb=b: ylstage[bb][:, ic * 128:(ic + 1) * 128],
                                        lambda g: ylT.ap()[:, g * 4:(g + 1) * 4, :].rearrange("p c n -> p (c n)"),
                                        ev_yl[b], ev_ylT_cons)
                    ev_ylfree[b] = ev_tr
                    for ic in range(NIC):
                        d_ylTst.start(YE, s_ylT[:, ic, t * 128:(t + 1) * 128], ylT[:, ic, :],
                                      waits=[ev_tr] if ic == 0 else [])
                    ev_ylT_cons = d_ylTst.ev()
                barrier()

            with ExitStack() as bctx:
                Kt = bctx.enter_context(nc.sbuf_tensor("Kt", [128, 2, T], BF16))
                Vh = bctx.enter_context(nc.sbuf_tensor("Vh", [128, NT, 258], BF16))
                Qt = bctx.enter_context(nc.sbuf_tensor("Qt", [128, 2, OWN * 128], BF16))
                Pt = [bctx.enter_context(nc.sbuf_tensor(f"Pt{i}", [128, 512], BF16)) for i in range(2)]
                wk_sl = bctx.enter_context(nc.sbuf_tensor("wk_sl", [128, NIC, 256], BF16))
                wv_sl = bctx.enter_context(nc.sbuf_tensor("wv_sl", [128, NIC, 256], BF16))
                wq_sl = bctx.enter_context(nc.sbuf_tensor("wq_sl", [128, NIC, 256], BF16))
                ylstr0 = bctx.enter_context(nc.sbuf_tensor("ylstr0", [128, NIC, 512], BF16))
                ylstr = [ylstr0, ylstr0]
                attn_sb = bctx.enter_context(nc.sbuf_tensor("attn_sb", [128, 128], BF16))
                attnT = bctx.enter_context(nc.sbuf_tensor("attnT", [128, 2, OWN * 128], BF16))

                YE.wait(d_ylTst.ev())
                for hg in range(H // 2):
                    barrier()
                    SE.wait(d_atst.ev())
                    h0 = 2 * hg
                    d_wsl.start(GE, wk_sl[:, :, :],
                                wkT.rearrange("(c p) o -> p c o", p=128)[:, :, h0 * 128:(h0 + 2) * 128])
                    d_wsl.start(GE, wv_sl[:, :, :],
                                wvT.rearrange("(c p) o -> p c o", p=128)[:, :, h0 * 128:(h0 + 2) * 128])
                    d_wsl.start(GE, wq_sl[:, :, :],
                                wqT.rearrange("(c p) o -> p c o", p=128)[:, :, h0 * 128:(h0 + 2) * 128])
                    ev_wall = d_wsl.ev()
                    VE.op("memset", Vh.ap()[:, :, 128:129], 1.0)
                    ev_onescol = VE.op("memset", Vh.ap()[:, :, 257:258], 1.0)

                    with ExitStack() as pctx:
                        psK = [pctx.enter_context(nc.psum_tensor(f"psK{hg}_{i}", [128, 512], F32)) for i in range(2)]
                        psV = [pctx.enter_context(nc.psum_tensor(f"psV{hg}_{i}", [128, 256], F32)) for i in range(2)]
                        psQ = pctx.enter_context(nc.psum_tensor(f"psQ{hg}", [128, 512], F32))
                        ev_str = [None, None]
                        ev_strfree = [None, None]
                        evK = {0: None, 1: None}
                        evV = {0: None, 1: None}
                        evQ = None
                        for t5 in range(8):
                            b = t5 % 2
                            ev_str[b] = d_ylTld[0].start(YE, ylstr[b][:, :, :],
                                                         s_ylT[:, :, t5 * 512:(t5 + 1) * 512],
                                                         waits=[ev_strfree[1 - b]] if ev_strfree[1 - b] else [])
                            for dc in range(2):
                                for ic in range(NIC):
                                    w = [ev_wall, ev_str[b], evK[dc]] if ic == 0 else []
                                    TE.op("matmul", psK[dc][:, :], wk_sl[:, ic, dc * 128:(dc + 1) * 128],
                                          ylstr[b][:, ic, :], start=(ic == 0), stop=(ic == NIC - 1),
                                          inc=(ic == NIC - 1), waits=w)
                                evK[dc] = SE.op("copy", Kt[:, dc, t5 * 512:(t5 + 1) * 512],
                                                psK[dc][:, :], waits=[TE.ev()])
                            for tc in range(4):
                                kt = t5 * 4 + tc
                                pb = tc % 2
                                for ic in range(NIC):
                                    w = [ev_wall, ev_str[b], evV[pb]] if ic == 0 else []
                                    TE.op("matmul", psV[pb][:, :],
                                          ylstr[b][:, ic, tc * 128:(tc + 1) * 128],
                                          wv_sl[:, ic, :], start=(ic == 0), stop=(ic == NIC - 1),
                                          inc=(ic == NIC - 1), waits=w)
                                ev_mm = TE.ev()
                                SE.op("copy", Vh[:, kt, 0:128], psV[pb][:, 0:128], waits=[ev_mm])
                                evV[pb] = SE.op("copy", Vh[:, kt, 129:257], psV[pb][:, 128:256])
                            if t5 < 4:
                                for dc in range(2):
                                    for ic in range(NIC):
                                        w = [ev_wall, ev_str[b], evQ] if ic == 0 else []
                                        TE.op("matmul", psQ[:, :], wq_sl[:, ic, dc * 128:(dc + 1) * 128],
                                              ylstr[b][:, ic, :], start=(ic == 0), stop=(ic == NIC - 1),
                                              inc=(ic == NIC - 1), waits=w)
                                    evQ = SE.op("activation",
                                                out=Qt[:, dc, t5 * 512:(t5 + 1) * 512],
                                                in_=psQ[:, :], func=AF.Copy,
                                                scale=st_schs[:, 0:1], waits=[TE.ev()])
                            ev_strfree[b] = TE.ev()
                        ev_proj = [evK[0], evK[1], evV[0], evV[1], evQ]

                        ev_negc = [None, None]
                        ev_sqr = None
                        for dc in range(2):
                            ev_sq = SE.op("activation", out=xq[:, :], in_=Qt[:, dc, :],
                                          func=AF.Square,
                                          waits=ev_proj + [ev_negc[0], ev_sqr, TE.ev()])
                            for c4 in range(4):
                                ev_mm = TE.op("matmul", psQ[:, :], ones_b[:, :],
                                              xq[:, c4 * 512:(c4 + 1) * 512], start=True,
                                              stop=True, waits=[ev_sq, ev_onesb, evQ,
                                                                ev_negc[0], ev_negc[1],
                                                                VE.ev()])
                                VE.op("reduce_max", st_gm[:, :], psQ[:, :], axis=AX.X,
                                      waits=[ev_mm])
                                if c4 == 0:
                                    VE.op("tensor_copy", st_am[:, :], st_gm[:, :])
                                else:
                                    VE.op("tensor_max", st_am[:, :], st_am[:, :], st_gm[:, :])
                            ev_q2 = VE.ev()
                            ev_first = None
                            for half in range(2):
                                ev_sqr = SE.op("activation", out=xq[:, :],
                                               in_=Kt[:, dc, half * D:(half + 1) * D],
                                               func=AF.Square, waits=[ev_q2, TE.ev()])
                                for c4 in range(4):
                                    ev_mm = TE.op("matmul", psQ[:, :], ones_b[:, :],
                                                  xq[:, c4 * 512:(c4 + 1) * 512], start=True,
                                                  stop=True, waits=[ev_sqr, VE.ev()])
                                    VE.op("reduce_max", st_gm[:, :], psQ[:, :], axis=AX.X,
                                          waits=[ev_mm])
                                    if half == 0 and c4 == 0:
                                        VE.op("tensor_copy", st_mx[:, :], st_gm[:, :])
                                    else:
                                        VE.op("tensor_max", st_mx[:, :], st_mx[:, :], st_gm[:, :])
                            VE.op("tensor_mul", st_gm[:, :], st_am[:, :], st_mx[:, :])
                            ev_m2 = VE.ev()
                            ev_rt = SE.op("activation", out=st_gm[:, :], in_=st_gm[:, :],
                                          func=AF.Sqrt, waits=[ev_m2])
                            ev_negc[dc] = VE.op("tensor_scalar", out=negc[dc][:, :],
                                                in0=st_gm[:, :], scalar1=-1.0,
                                                scalar2=None, op0=ALU.mult, waits=[ev_rt])
                            if "att2" in dbg and hg == 0 and dc == 0:
                                VE.op("tensor_copy", junk[:, 0:1], st_am[:, :])
                                VE.op("tensor_copy", junk[:, 1:2], st_mx[:, :])
                                VE.op("tensor_copy", junk[:, 2:3], negc[0][:, :])
                                d_dbg.start(YE, dbg_t["d_ch"][:, :], junk[:, 0:12], waits=[VE.ev()])
                                VE.op("memset", junk[:, 0:1], 0.0, waits=[d_dbg.ev()])

                    if "kvq" in dbg and hg == 0:
                        ev_d = None
                        for dc in range(2):
                            for half in range(2):
                                VE.op("tensor_copy", junk[:, :], Kt[:, dc, half * D:(half + 1) * D],
                                      waits=ev_proj + [ev_d])
                                ev_d = d_dbg.start(YE, dbg_t["d_k"][:, dc, half * D:(half + 1) * D],
                                                   junk[:, :], waits=[VE.ev()])
                            VE.op("tensor_copy", junk[:, :], Qt[:, dc, :], waits=ev_proj + [ev_d])
                            ev_d = d_dbg.start(YE, dbg_t["d_q"][:, dc, :], junk[:, :], waits=[VE.ev()])
                        for kt in range(NT):
                            VE.op("tensor_copy", junk[:, 0:258], Vh[:, kt, :], waits=ev_proj + [ev_d])
                            ev_d = d_dbg.start(YE, dbg_t["d_v"][:, kt, :], junk[:, 0:258], waits=[VE.ev()])
                        VE.op("memset", junk[:, 0:1], 0.0, waits=[ev_d])

                    with ExitStack() as actx2:
                        Sps = [actx2.enter_context(nc.psum_tensor(f"Sps{hg}_{i}", [128, 512], F32)) for i in range(2)]
                        Ops = [actx2.enter_context(nc.psum_tensor(f"Ops{hg}_{i}", [128, 129], F32)) for i in range(4)]
                        ev_exp = {0: None, 1: None}
                        ev_pvd = {0: None, 1: None}
                        ev_o_cons = None
                        ev_att_st = None
                        for dc in range(2):
                            h = h0 + dc
                            for qg in range(4):
                                i_lo = qg * 4
                                i_hi = i_lo + 4
                                kts = [(0, k) for k in range(i_hi)] + [(1, k) for k in range(i_hi)]
                                for it, (side, k) in enumerate(kts):
                                    vlo = max(i_lo, k)
                                    off = (vlo - i_lo) * 128
                                    b = it % 2
                                    kcol = (k if side == 0 else 16 + k) * 128
                                    first = [ev_proj[0], ev_proj[1], ev_proj[4],
                                             ev_negc[0], ev_negc[1]]
                                    if ev_exp[b] is not None:
                                        first.append(ev_exp[b])
                                    if ev_o_cons is not None:
                                        first.append(ev_o_cons)
                                    ev_sc = TE.op("matmul", Sps[b][:, off:512],
                                                  Kt[:, dc, kcol:kcol + 128],
                                                  Qt[:, dc, i_lo * 128 + off: (i_lo + 4) * 128],
                                                  start=True, stop=True, waits=first)
                                    ev_e = SE.op("activation", out=Pt[b][:, off:512],
                                                 in_=Sps[b][:, off:512], func=AF.Exp,
                                                 bias=negc[dc][:, 0:1],
                                                 waits=[ev_sc, ev_negc[dc], ev_pvd[b]])
                                    ev_m = ev_e
                                    if side == 0 and k >= i_lo:
                                        c0 = (k - i_lo) * 128
                                        ev_m = VE.op("tensor_mul", Pt[b][:, c0:c0 + 128],
                                                     Pt[b][:, c0:c0 + 128], maskd[:, :],
                                                     waits=[ev_e, ev_consts])
                                    elif side == 1 and k >= i_lo:
                                        c0 = (k - i_lo) * 128
                                        ev_m = VE.op("tensor_scalar", out=Pt[b][:, c0:c0 + 128],
                                                     in0=Pt[b][:, c0:c0 + 128],
                                                     scalar1=msel[:, k:k + 1], scalar2=None,
                                                     op0=ALU.mult, waits=[ev_e, ev_consts])
                                    for i in range(vlo, i_hi):
                                        li = i - i_lo
                                        w = [ev_m, ev_onescol, ev_proj[2], ev_proj[3]] if i == vlo else []
                                        TE.op("matmul", Ops[li][:, :],
                                              Pt[b][:, li * 128:(li + 1) * 128],
                                              Vh[:, (k if side == 0 else 16 + k), dc * 129:(dc + 1) * 129],
                                              start=(side == 0 and k == 0),
                                              stop=(side == 1 and k == i),
                                              inc=(i == i_hi - 1), waits=w)
                                    ev_pvd[b] = TE.ev()
                                    if ("att2" in dbg and hg == 0 and dc == 0 and qg == 0
                                            and side == 0 and k == 0):
                                        VE.op("tensor_copy", junk[:, 0:512], Pt[b][:, :],
                                              waits=[ev_m])
                                        d_dbg.start(YE, dbg_t["d_p0"][:, :], junk[:, 0:512],
                                                    waits=[VE.ev()])
                                ev_pv_all = TE.ev()
                                if "att2" in dbg and hg == 0 and dc == 0 and qg == 0:
                                    for li4 in range(4):
                                        VE.op("tensor_copy", junk2[:, li4 * 129:(li4 + 1) * 129],
                                              Ops[li4][:, :], waits=[ev_pv_all, d_dbg.ev()])
                                    VE.op("tensor_copy", junk2[:, 516:517], negc[0][:, :],
                                          waits=[ev_negc[0], ev_negc[1]])
                                    VE.op("tensor_copy", junk2[:, 517:518], negc[1][:, :])
                                    ev_dd = VE.ev()
                                    for li4 in range(4):
                                        d_dbg.start(YE, dbg_t["d_o0"][:, li4, :],
                                                    junk2[:, li4 * 129:(li4 + 1) * 129],
                                                    waits=[ev_dd])
                                    d_dbg.start(YE, dbg_t["d_nc"][:, :], junk2[:, 516:518],
                                                waits=[ev_dd])
                                    VE.op("memset", junk2[:, 0:1], 0.0, waits=[d_dbg.ev()])
                                for li in range(4):
                                    i = i_lo + li
                                    VE.op("reciprocal", st_ra[:, :], Ops[li][:, 128:129],
                                          waits=[ev_pv_all])
                                    ev_at = VE.op("tensor_scalar", out=attn_sb[:, :],
                                                  in0=Ops[li][:, 0:128],
                                                  scalar1=st_ra[:, 0:1], scalar2=None, op0=ALU.mult,
                                                  waits=[ev_att_st] if ev_att_st else [])
                                    ev_tr = TE.op("matmul", Sps[0][:, 0:128], attn_sb[:, :],
                                                  identb[:, :], start=True, stop=True,
                                                  waits=[ev_at, ev_exp[0], ev_exp[1]])
                                    ev_att_st = SE.op("copy", attnT[:, dc, i * 128:(i + 1) * 128],
                                                      Sps[0][:, 0:128], waits=[ev_tr])
                                    ev_exp[0] = ev_att_st
                                ev_o_cons = ev_att_st
                            d_atst.start(YE, s_attnT[h, :, :], attnT[:, dc, :], waits=[ev_att_st])
                            if "attn" in dbg:
                                VE.op("tensor_copy", junk[:, :], attnT[:, dc, :],
                                      waits=[ev_att_st, d_dbg.ev()])
                                d_dbg.start(YE, dbg_t["d_at"][h, :, :], junk[:, :], waits=[VE.ev()])

                barrier()

        if upto >= 'C':
            with ExitStack() as cctx:
                u2 = cctx.enter_context(nc.sbuf_tensor("u2", [128, NIC * D], BF16))
                u2_wo = u2.ap().rearrange("p (c n) -> p c n", n=D)
                atb_0 = cctx.enter_context(nc.sbuf_tensor("atb_0", [128, NIC, 128], BF16))
                atb = [atb_0, atb_0]
                y1s_0 = cctx.enter_context(nc.sbuf_tensor("y1s_0", [128, D], F32))
                y1s = [y1s_0, y1s_0]
                y2 = cctx.enter_context(nc.sbuf_tensor("y2", [128, D], F32))
                y3_0 = cctx.enter_context(nc.sbuf_tensor("y3_0", [128, D], F32))
                y3 = [y3_0, y3_0]
                xiC = cctx.enter_context(nc.sbuf_tensor("xiC", [128, D], mybir.dt.int32))
                ln_quant.xi[0] = xiC
                ops_ps = cctx.enter_context(nc.psum_tensor("ops_ps", [128, D], F32))
                fps = cctx.enter_context(nc.psum_tensor("fps", [128, D], F32))

                YE.wait(d_atst.ev())
                YE.wait(d_y1st[0].ev())
                YE.wait(d_y1st[1].ev())
                d_wcd.start(GE, u2_wo[:, :, :], woT.rearrange("(c p) o -> p c o", p=128))
                ev_wo_ld = d_wcd.ev()

                ev_at = [None, None]
                ev_atfree = [None, None]
                ev_y1l = [None, None]
                ev_y1lfree = [None, None]
                ev_y2_cons = None
                ev_y3free = [None, None]
                ev_xqT2_cons = None
                ev_ffn2_done = None
                s_at_v = s_attnT.rearrange("h p q -> p h q")
                for oi in range(OWN):
                    b = oi % 2
                    for half in range(2):
                        ev_at[b] = d_atld[0].start(
                            YE, atb[b][:, half * 8:(half + 1) * 8, :],
                            s_at_v[:, half * 8:(half + 1) * 8, oi * 128:(oi + 1) * 128],
                            waits=[ev_atfree[1 - b]] if (ev_atfree[1 - b] and half == 0) else [])
                    ev_y1l[b] = d_y1ld[0].start(YE, y1s[b][:, :], s_y1own[oi, :, :],
                                                waits=[ev_y1lfree[1 - b]] if ev_y1lfree[1 - b] else [])
                    ln_quant.qdbg[0] = (dbg_t["d_qd"] if ("qdbg" in dbg and oi == 8) else None)
                    for oc in range(4):
                        for ic in range(NIC):
                            w = []
                            if ic == 0 and oc == 0:
                                w = [ev_at[b], ev_wo_ld]
                                if ev_y2_cons is not None:
                                    w.append(ev_y2_cons)
                                if ev_xqT2_cons is not None:
                                    w.append(ev_xqT2_cons)
                            TE.op("matmul", ops_ps[:, oc * 512:(oc + 1) * 512], atb[b][:, ic, :],
                                  u2_wo[:, ic, oc * 512:(oc + 1) * 512],
                                  start=(ic == 0), stop=(ic == NIC - 1),
                                  inc=(ic == NIC - 1 and oc == 3), waits=w)
                    ev_oproj = TE.ev()
                    ev_atfree[b] = ev_oproj
                    VE.op("scalar_tensor_tensor", out=y2[:, :], in0=ops_ps[:, :], scalar=1.0,
                          in1=y1s[b][:, :], op0=ALU.mult, op1=ALU.add,
                          waits=[ev_oproj, ev_y1l[b]])
                    ev_y2 = VE.ev()
                    if "y2" in dbg:
                        ev_y2 = d_dbgY[b].start(YE, dbg_t["d_y2"][oi, :, :], y2[:, :],
                                                waits=[ev_y2])
                    ev_y1lfree[b] = ev_y2
                    ev_xq2 = ln_quant(y2[:, :], [ev_y2])
                    if "xq2" in dbg:
                        ev_d1 = d_dbgG2[oi % 2].start(GE, dbg_t["d_xq2"][oi, :, :], xq[:, :],
                                                      waits=[ev_xq2])
                        VE.op("tensor_copy", junk[:, 0:1], st_a[:, :],
                              waits=[ev_xq2] if oi == 0 else [ev_xq2, ev_d2])
                        VE.op("tensor_copy", junk[:, 1:2], st_st[:, :])
                        ev_d2 = d_dbgY[0].start(YE, dbg_t["d_st2"][:, :], junk[:, 0:2],
                                                waits=[VE.ev()])
                        VE.op("memset", junk[:, 0:1], 0.0, waits=[ev_d1, ev_d2])
                        ln_quant.ge_guard[0] = ev_d1
                    ev_evac = {0: None, 1: None}
                    ev_xqT2 = None
                    for g in range(4):
                        pb = g % 2
                        for kk in range(4):
                            ic = g * 4 + kk
                            w = []
                            if kk == 0:
                                w = [ev_xq2, ev_consts, ev_evac[pb]]
                                if g == 0:
                                    w.append(ev_y2)
                            TE.op("matmul",
                                  ops_ps[:, pb * 512 + kk * 128: pb * 512 + (kk + 1) * 128],
                                  xq[:, ic * 128:(ic + 1) * 128], identb[:, :],
                                  start=True, stop=True, inc=(kk == 3), waits=w)
                        ev_mm = TE.ev()
                        ev_xqT2 = SE.op("copy",
                                        xqT.ap()[:, g * 4:(g + 1) * 4, :].rearrange("p c n -> p (c n)"),
                                        ops_ps[:, pb * 512:(pb + 1) * 512], waits=[ev_mm])
                        ev_evac[pb] = ev_xqT2
                    ev_y2_cons = ev_xqT2
                    for oc in range(4):
                        for ic in range(NIC):
                            w = []
                            if ic == 0 and oc == 0:
                                w = [ev_xqT2]
                                if ev_ffn2_done is not None:
                                    w.append(ev_ffn2_done)
                            TE.op("matmul", fps[:, oc * 512:(oc + 1) * 512], xqT[:, ic, :],
                                  u1_wq[:, ic, oc * 512:(oc + 1) * 512],
                                  start=(ic == 0), stop=(ic == NIC - 1),
                                  inc=(ic == NIC - 1 and oc == 3), waits=w)
                    ev_ffn2 = TE.ev()
                    ev_xqT2_cons = ev_ffn2
                    VE.op("scalar_tensor_tensor", out=y3[b][:, :], in0=fps[:, :],
                          scalar=st_st[:, 0:1], in1=y2[:, :], op0=ALU.mult, op1=ALU.add,
                          waits=[ev_ffn2] + ([ev_y3free[1 - b]] if ev_y3free[1 - b] else []))
                    ev_y3 = VE.ev()
                    ev_ffn2_done = ev_y3
                    ev_y3free[b] = d_out.start(YE, y_out[oi, :, :], y3[b][:, :], waits=[ev_y3])

        YE.wait(d_out.ev())
        if dbg:
            YE.wait(d_dbg.ev())
            for dd in d_dbgY:
                YE.wait(dd.ev())
            for dd in d_dbgG2:
                GE.wait(dd.ev())

        blk = ctx.enter_context(nc.Block())

        @blk.tensor
        def _(e):
            for f in TE.todo:
                f(e)

        @blk.vector
        def _(e):
            for f in VE.todo:
                f(e)

        @blk.scalar
        def _(e):
            for f in SE.todo:
                f(e)

        @blk.gpsimd
        def _(e):
            for f in GE.todo:
                f(e)

        @blk.sync
        def _(e):
            for f in YE.todo:
                f(e)

    return nc


def host_inputs(x, wq, wk, wv, wo, wf):
    x = np.asarray(x, dtype=np.float32)
    wts = {n: np.ascontiguousarray(np.asarray(w, dtype=np.float32).T)
           for n, w in (("wqT", wq), ("wkT", wk), ("wvT", wv), ("woT", wo), ("wfT", wf))}
    ident = np.eye(128, dtype=np.float32)
    kk, qq = np.meshgrid(np.arange(128), np.arange(128), indexing="ij")
    maskd = (kk <= qq).astype(np.float32)
    in_maps = []
    for c in range(8):
        b, j = c // 2, c % 2
        ow = own_tiles(j)
        ot = [t for t in range(NT) if t not in ow]
        perm = ow + ot
        xb = x[b].reshape(NT, 128, D)[perm].reshape(T, D)
        msel = np.zeros((128, OWN), dtype=np.float32)
        for i in range(OWN):
            msel[:, i] = 1.0 if ot[i] < ow[i] else 0.0
        schs = np.full((128, 1), np.float32(1.0 / np.sqrt(np.float64(128.0))),
                       dtype=np.float32)
        m = {"x0": np.ascontiguousarray(xb), "ident_in": ident, "maskd_in": maskd,
             "msel_in": msel, "schs_in": schs}
        m.update(wts)
        in_maps.append(m)
    return in_maps


def kernel(L0_attn_x, L0_attn_wq, L0_attn_wk, L0_attn_wv, L0_attn_wo, L0_ffn_weight):
    in_maps = host_inputs(L0_attn_x, L0_attn_wq, L0_attn_wk, L0_attn_wv,
                          L0_attn_wo, L0_ffn_weight)
    nc = build()
    res = run_bass_kernel_spmd(nc, in_maps, core_ids=list(range(8)))
    out = np.empty((4, T, D), dtype=np.float32)
    for c in range(8):
        b, j = c // 2, c % 2
        ow = own_tiles(j)
        shard = res.results[c]["y_out"]
        for i, t in enumerate(ow):
            out[b, t * 128:(t + 1) * 128, :] = shard[i]
    return out


# version 09
# speedup vs baseline: 1.3383x; 1.1503x over previous
import numpy as np
import concourse.bass as bass
import concourse.mybir as mybir
from concourse.bass_utils import run_bass_kernel_spmd
from contextlib import ExitStack

F32 = mybir.dt.float32
BF16 = mybir.dt.bfloat16
AF = mybir.ActivationFunctionType
ALU = mybir.AluOpType
AX = mybir.AxisListType

D = 2048
T = 4096
NT = 32
NIC = 16
H = 16
HS = 128
OWN = 16
NEG_SQRT_HS = -float(np.sqrt(np.float64(HS)))
INV_SQRT_HS = float(1.0 / np.sqrt(np.float64(HS)))
EPS = 1e-5
SH = 256.5
WMEAN_INV = 1.0 / (D * D)


def own_tiles(j):
    return [t for t in range(NT) if (0 if (t % 4) in (0, 3) else 1) == j]


class Ev:
    __slots__ = ("sem", "val")

    def __init__(self, sem, val):
        self.sem = sem
        self.val = val


class Eng:
    def __init__(self, sem, auto_serial):
        self.sem = sem
        self.cnt = 0
        self.auto = auto_serial
        self.waited = {}
        self.todo = []

    def wait(self, ev):
        if ev is None:
            return
        key = id(ev.sem)
        if self.waited.get(key, 0) >= ev.val:
            return
        self.waited[key] = ev.val
        sem, val = ev.sem, ev.val
        self.todo.append(lambda e: e.wait_ge(sem, val))

    def waits(self, evs):
        for ev in evs:
            self.wait(ev)

    def op(self, name, *args, inc=True, waits=(), **kw):
        self.waits(waits)
        if self.auto and self.cnt > 0:
            self.wait(Ev(self.sem, self.cnt))
        sem = self.sem
        if inc:
            self.cnt += 1

            def emit(e, name=name, args=args, kw=kw, sem=sem):
                getattr(e, name)(*args, **kw).then_inc(sem, 1)
        else:

            def emit(e, name=name, args=args, kw=kw):
                getattr(e, name)(*args, **kw)
        self.todo.append(emit)
        return Ev(self.sem, self.cnt) if inc else None

    def ev(self):
        return Ev(self.sem, self.cnt)


class DmaBuf:
    def __init__(self, nc, ctx, name):
        self.sem = ctx.enter_context(nc.semaphore(name))
        self.cum = 0

    def start(self, qeng, dst, src, waits=()):
        qeng.waits(waits)
        sem = self.sem
        self.cum += 16
        qeng.todo.append(lambda e, d=dst, s=src, sem=sem: e.dma_start(d, s).then_inc(sem, 16))
        return Ev(sem, self.cum)

    def ev(self):
        return Ev(self.sem, self.cum)


def build(dbg=(), upto='Z'):
    dbg = set(dbg)
    nc = bass.Bass("TRN2", target_bir_lowering=False)

    x0 = nc.dram_tensor("x0", [T, D], F32, kind="ExternalInput").ap()
    wqT = nc.dram_tensor("wqT", [D, D], F32, kind="ExternalInput").ap()
    wkT = nc.dram_tensor("wkT", [D, D], F32, kind="ExternalInput").ap()
    wvT = nc.dram_tensor("wvT", [D, D], F32, kind="ExternalInput").ap()
    woT = nc.dram_tensor("woT", [D, D], F32, kind="ExternalInput").ap()
    wfT = nc.dram_tensor("wfT", [D, D], F32, kind="ExternalInput").ap()
    ident_in = nc.dram_tensor("ident_in", [128, 128], F32, kind="ExternalInput").ap()
    maskd_in = nc.dram_tensor("maskd_in", [128, 128], F32, kind="ExternalInput").ap()
    msel_in = nc.dram_tensor("msel_in", [128, OWN], F32, kind="ExternalInput").ap()
    schs_in = nc.dram_tensor("schs_in", [128, 1], F32, kind="ExternalInput").ap()

    y_out = nc.dram_tensor("y_out", [OWN, 128, D], F32, kind="ExternalOutput").ap()
    dbg_t = {}
    if "y1" in dbg:
        dbg_t["d_y1"] = nc.dram_tensor("d_y1", [NT, 128, D], F32, kind="ExternalOutput").ap()
    if "y1ln" in dbg:
        dbg_t["d_y1ln"] = nc.dram_tensor("d_y1ln", [NT, 128, D], F32, kind="ExternalOutput").ap()
    if "kvq" in dbg:
        dbg_t["d_k"] = nc.dram_tensor("d_k", [128, 2, T], F32, kind="ExternalOutput").ap()
        dbg_t["d_v"] = nc.dram_tensor("d_v", [128, NT, 258], F32, kind="ExternalOutput").ap()
        dbg_t["d_q"] = nc.dram_tensor("d_q", [128, 2, OWN * 128], F32, kind="ExternalOutput").ap()
    if "attn" in dbg:
        dbg_t["d_at"] = nc.dram_tensor("d_at", [H, 128, OWN * 128], F32, kind="ExternalOutput").ap()
    if "y2" in dbg:
        dbg_t["d_y2"] = nc.dram_tensor("d_y2", [OWN, 128, D], F32, kind="ExternalOutput").ap()
    if "qdbg" in dbg:
        dbg_t["d_qd"] = nc.dram_tensor("d_qd", [4, 128, D], F32, kind="ExternalOutput").ap()
    if "xq2" in dbg:
        dbg_t["d_xq2"] = nc.dram_tensor("d_xq2", [OWN, 128, D], F32, kind="ExternalOutput").ap()
        dbg_t["d_st2"] = nc.dram_tensor("d_st2", [128, 2], F32, kind="ExternalOutput").ap()
    if "att2" in dbg:
        dbg_t["d_o0"] = nc.dram_tensor("d_o0", [128, 4, 129], F32, kind="ExternalOutput").ap()
        dbg_t["d_nc"] = nc.dram_tensor("d_nc", [128, 2], F32, kind="ExternalOutput").ap()
        dbg_t["d_p0"] = nc.dram_tensor("d_p0", [128, 512], F32, kind="ExternalOutput").ap()
        dbg_t["d_ch"] = nc.dram_tensor("d_ch", [128, 12], F32, kind="ExternalOutput").ap()

    s_y1ln = nc.dram_tensor("s_y1ln", [NT, 128, D], BF16, kind="Internal").ap()
    s_y1own = nc.dram_tensor("s_y1own", [OWN, 128, D], F32, kind="Internal").ap()
    s_ylT = nc.dram_tensor("s_ylT", [128, NIC, T], BF16, kind="Internal").ap()
    s_wqT = nc.dram_tensor("s_wqT", [128, NIC * D], BF16, kind="Internal").ap()
    s_attnT = nc.dram_tensor("s_attnT", [H, 128, OWN * 128], BF16, kind="Internal").ap()

    with ExitStack() as ctx:
        TE = Eng(ctx.enter_context(nc.semaphore("sem_t")), False)
        VE = Eng(ctx.enter_context(nc.semaphore("sem_v")), True)
        SE = Eng(ctx.enter_context(nc.semaphore("sem_s")), True)
        GE = Eng(ctx.enter_context(nc.semaphore("sem_g")), False)
        YE = Eng(ctx.enter_context(nc.semaphore("sem_y")), False)
        ALL = [TE, VE, SE, GE, YE]

        ALL_DMABUFS = []

        def barrier():
            evs = [g.ev() for g in ALL]
            for g in ALL:
                for ev in evs:
                    if ev.sem is not g.sem:
                        g.wait(ev)
                for db in ALL_DMABUFS:
                    if db.cum:
                        g.wait(db.ev())

        def DB(name):
            b = DmaBuf(nc, ctx, name)
            ALL_DMABUFS.append(b)
            return b

        d_x0 = [DB(f"dm_x{i}") for i in range(2)]
        d_wld = [DB(f"dm_w{i}") for i in range(2)]
        d_y1lnst = [DB(f"dm_ylst{i}") for i in range(2)]
        d_y1st = [DB(f"dm_y1st{i}") for i in range(2)]
        d_const = DB("dm_const")
        d_ylld = [DB(f"dm_ylld{i}") for i in range(2)]
        d_ylTst = [DB(f"dm_ylTst{i}") for i in range(2)]
        d_ylTld = [DB(f"dm_ylTld{i}") for i in range(2)]
        d_wsl = DB("dm_wsl")
        d_atst = DB("dm_atst")
        d_atld = [DB(f"dm_atld{i}") for i in range(2)]
        d_y1ld = [DB(f"dm_y1ld{i}") for i in range(2)]
        d_wcd = DB("dm_wcd")
        d_out = DB("dm_out")
        d_dbg = DB("dm_dbg")
        d_dbgY = [DB(f"dm_dbgY{i}") for i in range(2)]
        d_dbgG2 = [DB(f"dm_dbgG{i}") for i in range(2)]

        identb = ctx.enter_context(nc.sbuf_tensor("identb", [128, 128], BF16))
        identf = ctx.enter_context(nc.sbuf_tensor("identf", [128, 128], F32))
        maskd = ctx.enter_context(nc.sbuf_tensor("maskd", [128, 128], BF16))
        msel = ctx.enter_context(nc.sbuf_tensor("msel", [128, OWN], F32))
        ones_f = ctx.enter_context(nc.sbuf_tensor("ones_f", [128, 128], F32))
        pad_f = ctx.enter_context(nc.sbuf_tensor("pad_f", [128, 128], F32))

        xm = ctx.enter_context(nc.sbuf_tensor("xm", [128, D], F32))
        xq = ctx.enter_context(nc.sbuf_tensor("xq", [128, D], BF16))
        xqT = ctx.enter_context(nc.sbuf_tensor("xqT", [128, NIC, 128], BF16))
        junk = ctx.enter_context(nc.sbuf_tensor("junk", [128, D], F32))
        junk2 = ctx.enter_context(nc.sbuf_tensor("junk2", [128, D], F32))
        junk_s = ctx.enter_context(nc.sbuf_tensor("junk_s", [128, D], F32))

        def vec(name):
            return ctx.enter_context(nc.sbuf_tensor(name, [128, 1], F32))

        st_cs = vec("st_cs")
        st_sq = vec("st_sq")
        st_m = vec("st_m")
        st_msq = vec("st_msq")
        st_var = vec("st_var")
        st_sig = vec("st_sig")
        st_inv = vec("st_inv")
        st_am = vec("st_am")
        st_a = vec("st_a")
        st_ra = vec("st_ra")
        st_g = vec("st_g")
        st_st = vec("st_st")
        wsc = vec("wsc")
        wsc_inv = vec("wsc_inv")
        wscB = vec("wscB")
        st_mx = vec("st_mx")
        st_gm = vec("st_gm")
        negc = [vec("negc0"), vec("negc1")]
        wabs = ctx.enter_context(nc.sbuf_tensor("wabs", [128, NIC], F32))

        ev_c = None
        for dst, src in ((identb, ident_in), (identf, ident_in), (maskd, maskd_in),
                         (msel, msel_in)):
            ev_c = d_const.start(GE, dst[:, :], src[:, :])
        ev_consts = ev_c
        ev_ones = VE.op("memset", ones_f[:, :], 1.0)
        ones_b = ctx.enter_context(nc.sbuf_tensor("ones_b", [128, 128], BF16))
        ev_onesb = VE.op("memset", ones_b[:, :], 1.0)
        st_eps = vec("st_eps")
        ev_eps = VE.op("memset", st_eps[:, :], EPS)
        st_schs = vec("st_schs")
        st_r127 = vec("st_r127")
        schs_sb = ctx.enter_context(nc.sbuf_tensor("schs_sb", [128, 1], F32))
        ev_schs_ld = d_const.start(GE, schs_sb[:, :], schs_in[:, :])
        ev_consts = ev_schs_ld
        VE.op("tensor_copy", st_schs[:, :], schs_sb[:, :], waits=[ev_schs_ld])
        VE.op("memset", st_r127[:, :], 127.0)
        ev_r127 = VE.op("reciprocal", st_r127[:, :], st_r127[:, :])

        def ln_quant(src_ap, waits_src, xq_out=None, st_st_out=None):
            VE.op("reduce_sum", st_cs[:, :], src_ap, axis=AX.X, waits=waits_src)
            SE.op("activation", out=junk_s[:, :], in_=src_ap, func=AF.Square,
                  accum_out=st_sq[:, :], waits=list(waits_src) + [VE.ev()])
            ev_sq = SE.ev()
            VE.op("tensor_scalar", out=st_m[:, :], in0=st_cs[:, :], scalar1=1.0 / D,
                  scalar2=None, op0=ALU.mult)
            VE.op("tensor_scalar", out=xm[:, :], in0=src_ap, scalar1=st_m[:, 0:1],
                  scalar2=None, op0=ALU.subtract)
            VE.op("reduce_max", st_am[:, :], xm[:, :], axis=AX.X, apply_absolute_value=True)
            VE.op("tensor_mul", st_msq[:, :], st_m[:, :], st_m[:, :])
            VE.op("scalar_tensor_tensor", out=st_var[:, :], in0=st_sq[:, :], scalar=1.0 / D,
                  in1=st_msq[:, :], op0=ALU.mult, op1=ALU.subtract, waits=[ev_sq])
            ev_var = VE.ev()
            ev_sig = SE.op("activation", out=st_sig[:, :], in_=st_var[:, :], func=AF.Sqrt,
                           bias=st_eps[:, 0:1], waits=[ev_var, ev_eps])
            VE.op("reciprocal", st_inv[:, :], st_sig[:, :], waits=[ev_sig])
            VE.op("tensor_mul", st_a[:, :], st_am[:, :], st_inv[:, :])
            VE.op("tensor_scalar_max", st_a[:, :], st_a[:, :], 1e-8)
            VE.op("reciprocal", st_ra[:, :], st_a[:, :])
            VE.op("tensor_scalar", out=st_ra[:, :], in0=st_ra[:, :], scalar1=127.0,
                  scalar2=None, op0=ALU.mult)
            VE.op("tensor_mul", st_g[:, :], st_inv[:, :], st_ra[:, :])
            xi = ln_quant.xi[0]
            VE.op("tensor_scalar", out=junk[:, :], in0=xm[:, :], scalar1=st_g[:, 0:1],
                  scalar2=SH, op0=ALU.mult, op1=ALU.add)
            VE.op("tensor_copy", xi[:, :], junk[:, :])
            VE.op("tensor_copy", junk2[:, :], xi[:, :])
            if ln_quant.qdbg[0] is not None:
                qd = ln_quant.qdbg[0]
                d_dbg.start(YE, qd[0], junk[:, :], waits=[VE.ev()])
                d_dbg.start(YE, qd[1], junk2[:, :], waits=[VE.ev()])
            VE.op("tensor_tensor", xm[:, :], junk[:, :], junk2[:, :], op=ALU.subtract)
            if ln_quant.qdbg[0] is not None:
                qd = ln_quant.qdbg[0]
                d_dbg.start(YE, qd[2], xm[:, :], waits=[VE.ev()])
            VE.op("tensor_scalar", out=junk[:, :], in0=xm[:, :], scalar1=1.0,
                  scalar2=None, op0=ALU.is_ge, waits=[d_dbg.ev()])
            VE.op("scalar_tensor_tensor", out=junk[:, :], in0=xm[:, :], scalar=2.0,
                  in1=junk[:, :], op0=ALU.is_ge, op1=ALU.add)
            VE.op("scalar_tensor_tensor", out=junk[:, :], in0=xm[:, :], scalar=0.0,
                  in1=junk[:, :], op0=ALU.is_lt, op1=ALU.subtract)
            VE.op("scalar_tensor_tensor", out=junk[:, :], in0=xm[:, :], scalar=-1.0,
                  in1=junk[:, :], op0=ALU.is_lt, op1=ALU.add)
            VE.op("tensor_tensor", junk[:, :], junk2[:, :], junk[:, :], op=ALU.subtract)
            VE.op("tensor_scalar", out=junk[:, :], in0=junk[:, :], scalar1=383.0,
                  scalar2=129.0, op0=ALU.min, op1=ALU.max)
            xq_o = xq if xq_out is None else xq_out
            st_o = st_st if st_st_out is None else st_st_out
            ev_xq = VE.op("tensor_scalar", out=xq_o[:, :], in0=junk[:, :], scalar1=256.0,
                          scalar2=None, op0=ALU.subtract, waits=[ln_quant.ge_guard[0]])
            VE.op("tensor_mul", st_o[:, :], st_a[:, :], wscB[:, :])
            return ev_xq

        ln_quant.ge_guard = [None]
        ln_quant.xi = [None]
        ln_quant.qdbg = [None]

        tr_state = {0: None, 1: None}

        def transpose16(trps, src_fn, dst_fn, ev_src, ev_prev_cons):
            ev_out = None
            for g in range(4):
                b = g % 2
                for kk in range(4):
                    ic = g * 4 + kk
                    w = []
                    if kk == 0:
                        w = [ev_src, ev_consts, tr_state[b]]
                        if g == 0 and ev_prev_cons is not None:
                            w.append(ev_prev_cons)
                    TE.op("matmul", trps[b][:, kk * 128:(kk + 1) * 128], src_fn(ic),
                          identb[:, :], start=True, stop=True, inc=(kk == 3), waits=w)
                ev_mm = TE.ev()
                ev_out = SE.op("copy", dst_fn(g), trps[b][:, :], waits=[ev_mm])
                tr_state[b] = ev_out
            return ev_out

        def ln2_normalize(src_ap, dst_ap, ev_src, extra_waits=()):
            VE.op("reduce_sum", st_cs[:, :], src_ap, axis=AX.X, waits=[ev_src])
            SE.op("activation", out=junk_s[:, :], in_=src_ap, func=AF.Square,
                  accum_out=st_sq[:, :], waits=[ev_src, VE.ev()])
            ev_sq = SE.ev()
            VE.op("tensor_scalar", out=st_m[:, :], in0=st_cs[:, :], scalar1=1.0 / D,
                  scalar2=None, op0=ALU.mult)
            VE.op("tensor_mul", st_msq[:, :], st_m[:, :], st_m[:, :])
            VE.op("scalar_tensor_tensor", out=st_var[:, :], in0=st_sq[:, :], scalar=1.0 / D,
                  in1=st_msq[:, :], op0=ALU.mult, op1=ALU.subtract, waits=[ev_sq])
            ev_var = VE.ev()
            ev_sig = SE.op("activation", out=st_sig[:, :], in_=st_var[:, :], func=AF.Sqrt,
                           bias=st_eps[:, 0:1], waits=[ev_var, ev_eps])
            VE.op("reciprocal", st_inv[:, :], st_sig[:, :], waits=[ev_sig])
            return VE.op("tensor_scalar", out=dst_ap, in0=src_ap, scalar1=st_m[:, 0:1],
                         scalar2=st_inv[:, 0:1], op0=ALU.subtract, op1=ALU.mult,
                         waits=list(extra_waits))

        with ExitStack() as actx:
            wbuf = [actx.enter_context(nc.sbuf_tensor(f"wbuf{i}", [128, D], F32)) for i in range(2)]
            xbuf = [actx.enter_context(nc.sbuf_tensor(f"xbuf{i}", [128, D], F32)) for i in range(2)]
            y1b = [actx.enter_context(nc.sbuf_tensor(f"y1b{i}", [128, D], F32)) for i in range(2)]
            y1ln = [actx.enter_context(nc.sbuf_tensor(f"y1ln{i}", [128, D], BF16)) for i in range(2)]
            u1 = actx.enter_context(nc.sbuf_tensor("u1", [128, NIC * D], BF16))
            u1_wq = u1.ap().rearrange("p (c n) -> p c n", n=D)
            xiA = actx.enter_context(nc.sbuf_tensor("xiA", [128, D], mybir.dt.int32))
            xqb = [actx.enter_context(nc.sbuf_tensor(f"xqb{i}", [128, D], BF16)) for i in range(2)]
            xqTb = [actx.enter_context(nc.sbuf_tensor(f"xqTb{i}", [128, NIC, 128], BF16)) for i in range(2)]
            ylnT = [actx.enter_context(nc.sbuf_tensor(f"ylnT{i}", [128, NIC, 128], BF16)) for i in range(2)]
            st_stb = [actx.enter_context(nc.sbuf_tensor(f"st_stb{i}", [128, 1], F32)) for i in range(2)]
            ln_quant.xi[0] = xiA
            ypsum = actx.enter_context(nc.psum_tensor("ypsum", [128, D], F32))
            trpsA = [actx.enter_context(nc.psum_tensor(f"trpsA{i}", [128, 512], F32)) for i in range(2)]

            ev_w = [None, None]
            ev_wfree = [None, None]
            for k in range(NIC):
                b = k % 2
                ev_w[b] = d_wld[b].start(GE, wbuf[b][:, :], wfT[k * 128:(k + 1) * 128, :],
                                         waits=[ev_wfree[b]] if ev_wfree[b] else [])
                ev_wfree[b] = SE.op("activation", out=junk_s[:, :], in_=wbuf[b][:, :],
                                    func=AF.Abs, accum_out=wabs[:, k:k + 1], waits=[ev_w[b]])
            ev_ws = VE.op("reduce_sum", st_cs[:, :], wabs[:, :], axis=AX.X,
                          waits=[ev_wfree[0], ev_wfree[1]])
            VE.op("tensor_copy", st_mx[:, :], st_cs[:, :])
            ev_wsv = VE.ev()
            ev_wsum = TE.op("matmul", trpsA[0][:, 0:1], ones_f[:, :], st_mx[:, :],
                            start=True, stop=True, waits=[ev_ones, ev_wsv])
            VE.op("tensor_scalar", out=wsc[:, :], in0=trpsA[0][:, 0:1],
                  scalar1=WMEAN_INV, scalar2=None, op0=ALU.mult, waits=[ev_wsum])
            VE.op("tensor_scalar_max", wsc[:, :], wsc[:, :], 1e-6)
            ev_wsci = VE.op("reciprocal", wsc_inv[:, :], wsc[:, :])
            VE.op("tensor_scalar", out=wscB[:, :], in0=wsc[:, :],
                  scalar1=0.5, scalar2=None, op0=ALU.mult)
            ev_wscB = VE.op("tensor_mul", wscB[:, :], wscB[:, :], st_r127[:, 0:1],
                            waits=[ev_r127])
            ev_wsum_read = ev_wscB

            ev_w = [None, None]
            for k in range(NIC):
                b = k % 2
                ev_w[b] = d_wld[b].start(GE, wbuf[b][:, :], wfT[k * 128:(k + 1) * 128, :],
                                         waits=[ev_wfree[b]] if ev_wfree[b] else [])
                xi = ln_quant.xi[0]
                VE.op("tensor_scalar", out=junk[:, :], in0=wbuf[b][:, :],
                      scalar1=wsc_inv[:, 0:1], scalar2=SH, op0=ALU.mult, op1=ALU.add,
                      waits=[ev_w[b], ev_wsci])
                ev_wfree[b] = VE.ev()
                VE.op("tensor_copy", xi[:, :], junk[:, :])
                VE.op("tensor_copy", junk2[:, :], xi[:, :])
                VE.op("tensor_tensor", xm[:, :], junk[:, :], junk2[:, :], op=ALU.subtract)
                VE.op("tensor_scalar", out=junk[:, :], in0=xm[:, :], scalar1=1.0,
                      scalar2=None, op0=ALU.is_ge)
                VE.op("tensor_scalar", out=xm[:, :], in0=xm[:, :], scalar1=0.0,
                      scalar2=None, op0=ALU.is_lt)
                VE.op("tensor_tensor", junk[:, :], junk[:, :], xm[:, :], op=ALU.subtract)
                VE.op("tensor_tensor", junk[:, :], junk[:, :], junk2[:, :], op=ALU.add)
                VE.op("tensor_scalar", out=junk[:, :], in0=junk[:, :], scalar1=257.0,
                      scalar2=255.0, op0=ALU.min, op1=ALU.max)
                VE.op("tensor_scalar", out=u1_wq[:, k, :], in0=junk[:, :], scalar1=256.0,
                      scalar2=None, op0=ALU.subtract)
            ev_wq_ready = VE.ev()
            d_wqst = DB("dm_wqst")
            ev_wq_st = d_wqst.start(YE, s_wqT[:, :], u1.ap()[:, :], waits=[ev_wq_ready])
            if "wq" in dbg:
                for k in range(NIC):
                    VE.op("tensor_copy", junk[:, :], u1_wq[:, k, :], waits=[ev_wq_ready])
                    d_dbg.start(YE, dbg_t["d_wq"][:, k, :], junk[:, :], waits=[VE.ev()])

            ev_x = [None, None]
            ev_xfree = [None, None]
            ev_y1free = [[], []]
            ev_ylnfree = [[], []]
            ev_ffn = [None, None]
            ev_trans = [None, None]
            ev_ylnT_store = [None, None]
            ev_stt_prev = None
            ev_xq_e = [None, None]
            ev_xqT_e = [None, None]

            def quant_stage(t):
                b = t % 2
                ev_x[b] = d_x0[b].start(YE, xbuf[b][:, :], x0[t * 128:(t + 1) * 128, :],
                                        waits=[ev_xfree[b]] if ev_xfree[b] else [])
                w = [ev_x[b], ev_wscB]
                if ev_trans[b] is not None:
                    w.append(ev_trans[b])
                ev_xq_e[b] = ln_quant(xbuf[b][:, :], w, xq_out=xqb[b], st_st_out=st_stb[b])
                ev_xqT = transpose16(
                    trpsA, lambda ic, bb=b: xqb[bb][:, ic * 128:(ic + 1) * 128],
                    lambda g, bb=b: xqTb[bb].ap()[:, g * 4:(g + 1) * 4, :].rearrange("p c n -> p (c n)"),
                    ev_xq_e[b], ev_ffn[b])
                ev_trans[b] = TE.ev()
                ev_xqT_e[b] = ev_xqT

            def ffn_stage(t):
                b = t % 2
                for oc in range(4):
                    for ic in range(NIC):
                        w = []
                        if ic == 0 and oc == 0:
                            w = [ev_xqT_e[b], ev_wq_ready]
                            if ev_stt_prev is not None:
                                w.append(ev_stt_prev)
                        TE.op("matmul", ypsum[:, oc * 512:(oc + 1) * 512], xqTb[b][:, ic, :],
                              u1_wq[:, ic, oc * 512:(oc + 1) * 512],
                              start=(ic == 0), stop=(ic == NIC - 1),
                              inc=(ic == NIC - 1 and oc == 3), waits=w)
                ev_ffn[b] = TE.ev()

            def finish_stage(t):
                nonlocal ev_stt_prev
                b = t % 2
                yb = y1b[b]
                VE.op("scalar_tensor_tensor", out=yb[:, :], in0=ypsum[:, :],
                      scalar=st_stb[b][:, 0:1], in1=xbuf[b][:, :], op0=ALU.mult, op1=ALU.add,
                      waits=[ev_ffn[b]] + ev_y1free[b])
                ev_y1 = VE.ev()
                ev_stt_prev = ev_y1
                ev_xfree[b] = ev_y1
                ev_yln = ln2_normalize(yb[:, :], y1ln[b][:, :], ev_y1, ev_ylnfree[b])
                ev_tr = transpose16(
                    trpsA, lambda ic, bb=b: y1ln[bb][:, ic * 128:(ic + 1) * 128],
                    lambda g, bb=b: ylnT[bb].ap()[:, g * 4:(g + 1) * 4, :].rearrange("p c n -> p (c n)"),
                    ev_yln, ev_ylnT_store[b])
                for ic in range(NIC):
                    d_ylTst[b].start(YE, s_ylT[:, ic, t * 128:(t + 1) * 128], ylnT[b][:, ic, :],
                                     waits=[ev_tr] if ic == 0 else [])
                ev_ylnT_store[b] = d_ylTst[b].ev()
                ev_ylnfree[b] = [ev_tr]
                ev_y1free[b] = [ev_y1]
                if t < OWN:
                    ev_y1free[b].append(d_y1st[b].start(YE, s_y1own[t, :, :], yb[:, :],
                                                        waits=[ev_y1]))
                if "y1" in dbg:
                    ev_y1free[b].append(d_dbgY[b].start(YE, dbg_t["d_y1"][t, :, :], yb[:, :],
                                                        waits=[ev_y1]))
                if "y1ln" in dbg:
                    ev_ylnfree[b].append(d_dbgG2[b].start(GE, dbg_t["d_y1ln"][t, :, :],
                                                          y1ln[b][:, :], waits=[ev_yln]))

            quant_stage(0)
            ffn_stage(0)
            for t in range(1, NT):
                quant_stage(t)
                finish_stage(t - 1)
                ffn_stage(t)
            finish_stage(NT - 1)

            barrier()

        if upto >= 'B':
            with ExitStack() as bctx:
                Kt = bctx.enter_context(nc.sbuf_tensor("Kt", [128, 2, T], BF16))
                Vh = bctx.enter_context(nc.sbuf_tensor("Vh", [128, NT, 258], BF16))
                Qt = bctx.enter_context(nc.sbuf_tensor("Qt", [128, 2, OWN * 128], BF16))
                Pt = [bctx.enter_context(nc.sbuf_tensor(f"Pt{i}", [128, 512], BF16)) for i in range(2)]
                wk_sl = bctx.enter_context(nc.sbuf_tensor("wk_sl", [128, NIC, 256], BF16))
                wv_sl = bctx.enter_context(nc.sbuf_tensor("wv_sl", [128, NIC, 256], BF16))
                wq_sl = bctx.enter_context(nc.sbuf_tensor("wq_sl", [128, NIC, 256], BF16))
                ylstr = [bctx.enter_context(nc.sbuf_tensor(f"ylstr{i}", [128, NIC, 512], BF16)) for i in range(2)]
                attn_sb = bctx.enter_context(nc.sbuf_tensor("attn_sb", [128, 128], BF16))
                attnT = bctx.enter_context(nc.sbuf_tensor("attnT", [128, 2, OWN * 128], BF16))

                for hg in range(H // 2):
                    barrier()
                    SE.wait(d_atst.ev())
                    h0 = 2 * hg
                    d_wsl.start(GE, wk_sl[:, :, :],
                                wkT.rearrange("(c p) o -> p c o", p=128)[:, :, h0 * 128:(h0 + 2) * 128])
                    d_wsl.start(GE, wv_sl[:, :, :],
                                wvT.rearrange("(c p) o -> p c o", p=128)[:, :, h0 * 128:(h0 + 2) * 128])
                    d_wsl.start(GE, wq_sl[:, :, :],
                                wqT.rearrange("(c p) o -> p c o", p=128)[:, :, h0 * 128:(h0 + 2) * 128])
                    ev_wall = d_wsl.ev()
                    VE.op("memset", Vh.ap()[:, :, 128:129], 1.0)
                    ev_onescol = VE.op("memset", Vh.ap()[:, :, 257:258], 1.0)

                    with ExitStack() as pctx:
                        psK = [pctx.enter_context(nc.psum_tensor(f"psK{hg}_{i}", [128, 512], F32)) for i in range(2)]
                        psV = [pctx.enter_context(nc.psum_tensor(f"psV{hg}_{i}", [128, 256], F32)) for i in range(2)]
                        psQ = pctx.enter_context(nc.psum_tensor(f"psQ{hg}", [128, 512], F32))
                        ev_str = [None, None]
                        ev_strfree = [None, None]
                        evK = {0: None, 1: None}
                        evV = {0: None, 1: None}
                        evQ = None
                        for t5 in range(8):
                            b = t5 % 2
                            ev_str[b] = d_ylTld[b].start(YE, ylstr[b][:, :, :],
                                                         s_ylT[:, :, t5 * 512:(t5 + 1) * 512],
                                                         waits=[ev_strfree[b]] if ev_strfree[b] else [])
                            for dc in range(2):
                                for ic in range(NIC):
                                    w = [ev_wall, ev_str[b], evK[dc]] if ic == 0 else []
                                    TE.op("matmul", psK[dc][:, :], wk_sl[:, ic, dc * 128:(dc + 1) * 128],
                                          ylstr[b][:, ic, :], start=(ic == 0), stop=(ic == NIC - 1),
                                          inc=(ic == NIC - 1), waits=w)
                                evK[dc] = SE.op("copy", Kt[:, dc, t5 * 512:(t5 + 1) * 512],
                                                psK[dc][:, :], waits=[TE.ev()])
                            for tc in range(4):
                                kt = t5 * 4 + tc
                                pb = tc % 2
                                for ic in range(NIC):
                                    w = [ev_wall, ev_str[b], evV[pb]] if ic == 0 else []
                                    TE.op("matmul", psV[pb][:, :],
                                          ylstr[b][:, ic, tc * 128:(tc + 1) * 128],
                                          wv_sl[:, ic, :], start=(ic == 0), stop=(ic == NIC - 1),
                                          inc=(ic == NIC - 1), waits=w)
                                ev_mm = TE.ev()
                                SE.op("copy", Vh[:, kt, 0:128], psV[pb][:, 0:128], waits=[ev_mm])
                                evV[pb] = SE.op("copy", Vh[:, kt, 129:257], psV[pb][:, 128:256])
                            if t5 < 4:
                                for dc in range(2):
                                    for ic in range(NIC):
                                        w = [ev_wall, ev_str[b], evQ] if ic == 0 else []
                                        TE.op("matmul", psQ[:, :], wq_sl[:, ic, dc * 128:(dc + 1) * 128],
                                              ylstr[b][:, ic, :], start=(ic == 0), stop=(ic == NIC - 1),
                                              inc=(ic == NIC - 1), waits=w)
                                    evQ = SE.op("activation",
                                                out=Qt[:, dc, t5 * 512:(t5 + 1) * 512],
                                                in_=psQ[:, :], func=AF.Copy,
                                                scale=st_schs[:, 0:1], waits=[TE.ev()])
                            ev_strfree[b] = TE.ev()
                        ev_proj = [evK[0], evK[1], evV[0], evV[1], evQ]

                        ev_negc = [None, None]
                        ev_sqr = None
                        for dc in range(2):
                            ev_sq = SE.op("activation", out=xq[:, :], in_=Qt[:, dc, :],
                                          func=AF.Square,
                                          waits=ev_proj + [ev_negc[0], ev_sqr, TE.ev()])
                            for c4 in range(4):
                                ev_mm = TE.op("matmul", psQ[:, :], ones_b[:, :],
                                              xq[:, c4 * 512:(c4 + 1) * 512], start=True,
                                              stop=True, waits=[ev_sq, ev_onesb, evQ,
                                                                ev_negc[0], ev_negc[1],
                                                                VE.ev()])
                                VE.op("reduce_max", st_gm[:, :], psQ[:, :], axis=AX.X,
                                      waits=[ev_mm])
                                if c4 == 0:
                                    VE.op("tensor_copy", st_am[:, :], st_gm[:, :])
                                else:
                                    VE.op("tensor_max", st_am[:, :], st_am[:, :], st_gm[:, :])
                            ev_q2 = VE.ev()
                            ev_first = None
                            for half in range(2):
                                ev_sqr = SE.op("activation", out=xq[:, :],
                                               in_=Kt[:, dc, half * D:(half + 1) * D],
                                               func=AF.Square, waits=[ev_q2, TE.ev()])
                                for c4 in range(4):
                                    ev_mm = TE.op("matmul", psQ[:, :], ones_b[:, :],
                                                  xq[:, c4 * 512:(c4 + 1) * 512], start=True,
                                                  stop=True, waits=[ev_sqr, VE.ev()])
                                    VE.op("reduce_max", st_gm[:, :], psQ[:, :], axis=AX.X,
                                          waits=[ev_mm])
                                    if half == 0 and c4 == 0:
                                        VE.op("tensor_copy", st_mx[:, :], st_gm[:, :])
                                    else:
                                        VE.op("tensor_max", st_mx[:, :], st_mx[:, :], st_gm[:, :])
                            VE.op("tensor_mul", st_gm[:, :], st_am[:, :], st_mx[:, :])
                            ev_m2 = VE.ev()
                            ev_rt = SE.op("activation", out=st_gm[:, :], in_=st_gm[:, :],
                                          func=AF.Sqrt, waits=[ev_m2])
                            ev_negc[dc] = VE.op("tensor_scalar", out=negc[dc][:, :],
                                                in0=st_gm[:, :], scalar1=-1.0,
                                                scalar2=None, op0=ALU.mult, waits=[ev_rt])
                            if "att2" in dbg and hg == 0 and dc == 0:
                                VE.op("tensor_copy", junk[:, 0:1], st_am[:, :])
                                VE.op("tensor_copy", junk[:, 1:2], st_mx[:, :])
                                VE.op("tensor_copy", junk[:, 2:3], negc[0][:, :])
                                d_dbg.start(YE, dbg_t["d_ch"][:, :], junk[:, 0:12], waits=[VE.ev()])
                                VE.op("memset", junk[:, 0:1], 0.0, waits=[d_dbg.ev()])

                    if "kvq" in dbg and hg == 0:
                        ev_d = None
                        for dc in range(2):
                            for half in range(2):
                                VE.op("tensor_copy", junk[:, :], Kt[:, dc, half * D:(half + 1) * D],
                                      waits=ev_proj + [ev_d])
                                ev_d = d_dbg.start(YE, dbg_t["d_k"][:, dc, half * D:(half + 1) * D],
                                                   junk[:, :], waits=[VE.ev()])
                            VE.op("tensor_copy", junk[:, :], Qt[:, dc, :], waits=ev_proj + [ev_d])
                            ev_d = d_dbg.start(YE, dbg_t["d_q"][:, dc, :], junk[:, :], waits=[VE.ev()])
                        for kt in range(NT):
                            VE.op("tensor_copy", junk[:, 0:258], Vh[:, kt, :], waits=ev_proj + [ev_d])
                            ev_d = d_dbg.start(YE, dbg_t["d_v"][:, kt, :], junk[:, 0:258], waits=[VE.ev()])
                        VE.op("memset", junk[:, 0:1], 0.0, waits=[ev_d])

                    with ExitStack() as actx2:
                        Sps = [actx2.enter_context(nc.psum_tensor(f"Sps{hg}_{i}", [128, 512], F32)) for i in range(2)]
                        Ops = [actx2.enter_context(nc.psum_tensor(f"Ops{hg}_{i}", [128, 129], F32)) for i in range(4)]
                        ev_exp = {0: None, 1: None}
                        ev_pvd = {0: None, 1: None}
                        ev_o_cons = None
                        ev_att_st = None
                        for dc in range(2):
                            h = h0 + dc
                            for qg in range(4):
                                i_lo = qg * 4
                                i_hi = i_lo + 4
                                kts = [(0, k) for k in range(i_hi)] + [(1, k) for k in range(i_hi)]
                                for it, (side, k) in enumerate(kts):
                                    vlo = max(i_lo, k)
                                    off = (vlo - i_lo) * 128
                                    b = it % 2
                                    kcol = (k if side == 0 else 16 + k) * 128
                                    first = [ev_proj[0], ev_proj[1], ev_proj[4],
                                             ev_negc[0], ev_negc[1]]
                                    if ev_exp[b] is not None:
                                        first.append(ev_exp[b])
                                    if ev_o_cons is not None:
                                        first.append(ev_o_cons)
                                    ev_sc = TE.op("matmul", Sps[b][:, off:512],
                                                  Kt[:, dc, kcol:kcol + 128],
                                                  Qt[:, dc, i_lo * 128 + off: (i_lo + 4) * 128],
                                                  start=True, stop=True, waits=first)
                                    ev_e = SE.op("activation", out=Pt[b][:, off:512],
                                                 in_=Sps[b][:, off:512], func=AF.Exp,
                                                 bias=negc[dc][:, 0:1],
                                                 waits=[ev_sc, ev_negc[dc], ev_pvd[b]])
                                    ev_m = ev_e
                                    if side == 0 and k >= i_lo:
                                        c0 = (k - i_lo) * 128
                                        ev_m = VE.op("tensor_mul", Pt[b][:, c0:c0 + 128],
                                                     Pt[b][:, c0:c0 + 128], maskd[:, :],
                                                     waits=[ev_e, ev_consts])
                                    elif side == 1 and k >= i_lo:
                                        c0 = (k - i_lo) * 128
                                        ev_m = VE.op("tensor_scalar", out=Pt[b][:, c0:c0 + 128],
                                                     in0=Pt[b][:, c0:c0 + 128],
                                                     scalar1=msel[:, k:k + 1], scalar2=None,
                                                     op0=ALU.mult, waits=[ev_e, ev_consts])
                                    for i in range(vlo, i_hi):
                                        li = i - i_lo
                                        w = [ev_m, ev_onescol, ev_proj[2], ev_proj[3]] if i == vlo else []
                                        TE.op("matmul", Ops[li][:, :],
                                              Pt[b][:, li * 128:(li + 1) * 128],
                                              Vh[:, (k if side == 0 else 16 + k), dc * 129:(dc + 1) * 129],
                                              start=(side == 0 and k == 0),
                                              stop=(side == 1 and k == i),
                                              inc=(i == i_hi - 1), waits=w)
                                    ev_pvd[b] = TE.ev()
                                    if ("att2" in dbg and hg == 0 and dc == 0 and qg == 0
                                            and side == 0 and k == 0):
                                        VE.op("tensor_copy", junk[:, 0:512], Pt[b][:, :],
                                              waits=[ev_m])
                                        d_dbg.start(YE, dbg_t["d_p0"][:, :], junk[:, 0:512],
                                                    waits=[VE.ev()])
                                ev_pv_all = TE.ev()
                                if "att2" in dbg and hg == 0 and dc == 0 and qg == 0:
                                    for li4 in range(4):
                                        VE.op("tensor_copy", junk2[:, li4 * 129:(li4 + 1) * 129],
                                              Ops[li4][:, :], waits=[ev_pv_all, d_dbg.ev()])
                                    VE.op("tensor_copy", junk2[:, 516:517], negc[0][:, :],
                                          waits=[ev_negc[0], ev_negc[1]])
                                    VE.op("tensor_copy", junk2[:, 517:518], negc[1][:, :])
                                    ev_dd = VE.ev()
                                    for li4 in range(4):
                                        d_dbg.start(YE, dbg_t["d_o0"][:, li4, :],
                                                    junk2[:, li4 * 129:(li4 + 1) * 129],
                                                    waits=[ev_dd])
                                    d_dbg.start(YE, dbg_t["d_nc"][:, :], junk2[:, 516:518],
                                                waits=[ev_dd])
                                    VE.op("memset", junk2[:, 0:1], 0.0, waits=[d_dbg.ev()])
                                for li in range(4):
                                    i = i_lo + li
                                    VE.op("reciprocal", st_ra[:, :], Ops[li][:, 128:129],
                                          waits=[ev_pv_all])
                                    ev_at = VE.op("tensor_scalar", out=attn_sb[:, :],
                                                  in0=Ops[li][:, 0:128],
                                                  scalar1=st_ra[:, 0:1], scalar2=None, op0=ALU.mult,
                                                  waits=[ev_att_st] if ev_att_st else [])
                                    ev_tr = TE.op("matmul", Sps[0][:, 0:128], attn_sb[:, :],
                                                  identb[:, :], start=True, stop=True,
                                                  waits=[ev_at, ev_exp[0], ev_exp[1]])
                                    ev_att_st = SE.op("copy", attnT[:, dc, i * 128:(i + 1) * 128],
                                                      Sps[0][:, 0:128], waits=[ev_tr])
                                    ev_exp[0] = ev_att_st
                                ev_o_cons = ev_att_st
                            d_atst.start(YE, s_attnT[h, :, :], attnT[:, dc, :], waits=[ev_att_st])
                            if "attn" in dbg:
                                VE.op("tensor_copy", junk[:, :], attnT[:, dc, :],
                                      waits=[ev_att_st, d_dbg.ev()])
                                d_dbg.start(YE, dbg_t["d_at"][h, :, :], junk[:, :], waits=[VE.ev()])

                barrier()

        if upto >= 'C':
            with ExitStack() as cctx:
                u2 = cctx.enter_context(nc.sbuf_tensor("u2", [128, NIC * D], BF16))
                u2_wo = u2.ap().rearrange("p (c n) -> p c n", n=D)
                u1c = cctx.enter_context(nc.sbuf_tensor("u1c", [128, NIC * D], BF16))
                u1_wq = u1c.ap().rearrange("p (c n) -> p c n", n=D)
                d_wqld = DB("dm_wqld")
                YE.wait(d_wqst.ev())
                ev_wq_ready = d_wqld.start(YE, u1c.ap()[:, :], s_wqT[:, :])
                atb_0 = cctx.enter_context(nc.sbuf_tensor("atb_0", [128, NIC, 128], BF16))
                atb = [atb_0, atb_0]
                y1s_0 = cctx.enter_context(nc.sbuf_tensor("y1s_0", [128, D], F32))
                y1s = [y1s_0, y1s_0]
                y2 = cctx.enter_context(nc.sbuf_tensor("y2", [128, D], F32))
                y3_0 = cctx.enter_context(nc.sbuf_tensor("y3_0", [128, D], F32))
                y3 = [y3_0, y3_0]
                xiC = cctx.enter_context(nc.sbuf_tensor("xiC", [128, D], mybir.dt.int32))
                ln_quant.xi[0] = xiC
                ops_ps = cctx.enter_context(nc.psum_tensor("ops_ps", [128, D], F32))
                fps = cctx.enter_context(nc.psum_tensor("fps", [128, D], F32))

                YE.wait(d_atst.ev())
                YE.wait(d_y1st[0].ev())
                YE.wait(d_y1st[1].ev())
                d_wcd.start(GE, u2_wo[:, :, :], woT.rearrange("(c p) o -> p c o", p=128))
                ev_wo_ld = d_wcd.ev()

                ev_at = [None, None]
                ev_atfree = [None, None]
                ev_y1l = [None, None]
                ev_y1lfree = [None, None]
                ev_y2_cons = None
                ev_y3free = [None, None]
                ev_xqT2_cons = None
                ev_ffn2_done = None
                s_at_v = s_attnT.rearrange("h p q -> p h q")
                for oi in range(OWN):
                    b = oi % 2
                    for half in range(2):
                        ev_at[b] = d_atld[0].start(
                            YE, atb[b][:, half * 8:(half + 1) * 8, :],
                            s_at_v[:, half * 8:(half + 1) * 8, oi * 128:(oi + 1) * 128],
                            waits=[ev_atfree[1 - b]] if (ev_atfree[1 - b] and half == 0) else [])
                    ev_y1l[b] = d_y1ld[0].start(YE, y1s[b][:, :], s_y1own[oi, :, :],
                                                waits=[ev_y1lfree[1 - b]] if ev_y1lfree[1 - b] else [])
                    ln_quant.qdbg[0] = (dbg_t["d_qd"] if ("qdbg" in dbg and oi == 8) else None)
                    for oc in range(4):
                        for ic in range(NIC):
                            w = []
                            if ic == 0 and oc == 0:
                                w = [ev_at[b], ev_wo_ld]
                                if ev_y2_cons is not None:
                                    w.append(ev_y2_cons)
                                if ev_xqT2_cons is not None:
                                    w.append(ev_xqT2_cons)
                            TE.op("matmul", ops_ps[:, oc * 512:(oc + 1) * 512], atb[b][:, ic, :],
                                  u2_wo[:, ic, oc * 512:(oc + 1) * 512],
                                  start=(ic == 0), stop=(ic == NIC - 1),
                                  inc=(ic == NIC - 1 and oc == 3), waits=w)
                    ev_oproj = TE.ev()
                    ev_atfree[b] = ev_oproj
                    VE.op("scalar_tensor_tensor", out=y2[:, :], in0=ops_ps[:, :], scalar=1.0,
                          in1=y1s[b][:, :], op0=ALU.mult, op1=ALU.add,
                          waits=[ev_oproj, ev_y1l[b]])
                    ev_y2 = VE.ev()
                    if "y2" in dbg:
                        ev_y2 = d_dbgY[b].start(YE, dbg_t["d_y2"][oi, :, :], y2[:, :],
                                                waits=[ev_y2])
                    ev_y1lfree[b] = ev_y2
                    ev_xq2 = ln_quant(y2[:, :], [ev_y2])
                    if "xq2" in dbg:
                        ev_d1 = d_dbgG2[oi % 2].start(GE, dbg_t["d_xq2"][oi, :, :], xq[:, :],
                                                      waits=[ev_xq2])
                        VE.op("tensor_copy", junk[:, 0:1], st_a[:, :],
                              waits=[ev_xq2] if oi == 0 else [ev_xq2, ev_d2])
                        VE.op("tensor_copy", junk[:, 1:2], st_st[:, :])
                        ev_d2 = d_dbgY[0].start(YE, dbg_t["d_st2"][:, :], junk[:, 0:2],
                                                waits=[VE.ev()])
                        VE.op("memset", junk[:, 0:1], 0.0, waits=[ev_d1, ev_d2])
                        ln_quant.ge_guard[0] = ev_d1
                    ev_evac = {0: None, 1: None}
                    ev_xqT2 = None
                    for g in range(4):
                        pb = g % 2
                        for kk in range(4):
                            ic = g * 4 + kk
                            w = []
                            if kk == 0:
                                w = [ev_xq2, ev_consts, ev_evac[pb]]
                                if g == 0:
                                    w.append(ev_y2)
                            TE.op("matmul",
                                  ops_ps[:, pb * 512 + kk * 128: pb * 512 + (kk + 1) * 128],
                                  xq[:, ic * 128:(ic + 1) * 128], identb[:, :],
                                  start=True, stop=True, inc=(kk == 3), waits=w)
                        ev_mm = TE.ev()
                        ev_xqT2 = SE.op("copy",
                                        xqT.ap()[:, g * 4:(g + 1) * 4, :].rearrange("p c n -> p (c n)"),
                                        ops_ps[:, pb * 512:(pb + 1) * 512], waits=[ev_mm])
                        ev_evac[pb] = ev_xqT2
                    ev_y2_cons = ev_xqT2
                    for oc in range(4):
                        for ic in range(NIC):
                            w = []
                            if ic == 0 and oc == 0:
                                w = [ev_xqT2, ev_wq_ready]
                                if ev_ffn2_done is not None:
                                    w.append(ev_ffn2_done)
                            TE.op("matmul", fps[:, oc * 512:(oc + 1) * 512], xqT[:, ic, :],
                                  u1_wq[:, ic, oc * 512:(oc + 1) * 512],
                                  start=(ic == 0), stop=(ic == NIC - 1),
                                  inc=(ic == NIC - 1 and oc == 3), waits=w)
                    ev_ffn2 = TE.ev()
                    ev_xqT2_cons = ev_ffn2
                    VE.op("scalar_tensor_tensor", out=y3[b][:, :], in0=fps[:, :],
                          scalar=st_st[:, 0:1], in1=y2[:, :], op0=ALU.mult, op1=ALU.add,
                          waits=[ev_ffn2] + ([ev_y3free[1 - b]] if ev_y3free[1 - b] else []))
                    ev_y3 = VE.ev()
                    ev_ffn2_done = ev_y3
                    ev_y3free[b] = d_out.start(YE, y_out[oi, :, :], y3[b][:, :], waits=[ev_y3])

        YE.wait(d_out.ev())
        if dbg:
            YE.wait(d_dbg.ev())
            for dd in d_dbgY:
                YE.wait(dd.ev())
            for dd in d_dbgG2:
                GE.wait(dd.ev())

        blk = ctx.enter_context(nc.Block())

        @blk.tensor
        def _(e):
            for f in TE.todo:
                f(e)

        @blk.vector
        def _(e):
            for f in VE.todo:
                f(e)

        @blk.scalar
        def _(e):
            for f in SE.todo:
                f(e)

        @blk.gpsimd
        def _(e):
            for f in GE.todo:
                f(e)

        @blk.sync
        def _(e):
            for f in YE.todo:
                f(e)

    return nc


def host_inputs(x, wq, wk, wv, wo, wf):
    x = np.asarray(x, dtype=np.float32)
    wts = {n: np.ascontiguousarray(np.asarray(w, dtype=np.float32).T)
           for n, w in (("wqT", wq), ("wkT", wk), ("wvT", wv), ("woT", wo), ("wfT", wf))}
    ident = np.eye(128, dtype=np.float32)
    kk, qq = np.meshgrid(np.arange(128), np.arange(128), indexing="ij")
    maskd = (kk <= qq).astype(np.float32)
    in_maps = []
    for c in range(8):
        b, j = c // 2, c % 2
        ow = own_tiles(j)
        ot = [t for t in range(NT) if t not in ow]
        perm = ow + ot
        xb = x[b].reshape(NT, 128, D)[perm].reshape(T, D)
        msel = np.zeros((128, OWN), dtype=np.float32)
        for i in range(OWN):
            msel[:, i] = 1.0 if ot[i] < ow[i] else 0.0
        schs = np.full((128, 1), np.float32(1.0 / np.sqrt(np.float64(128.0))),
                       dtype=np.float32)
        m = {"x0": np.ascontiguousarray(xb), "ident_in": ident, "maskd_in": maskd,
             "msel_in": msel, "schs_in": schs}
        m.update(wts)
        in_maps.append(m)
    return in_maps


def kernel(L0_attn_x, L0_attn_wq, L0_attn_wk, L0_attn_wv, L0_attn_wo, L0_ffn_weight):
    in_maps = host_inputs(L0_attn_x, L0_attn_wq, L0_attn_wk, L0_attn_wv,
                          L0_attn_wo, L0_ffn_weight)
    nc = build()
    res = run_bass_kernel_spmd(nc, in_maps, core_ids=list(range(8)))
    out = np.empty((4, T, D), dtype=np.float32)
    for c in range(8):
        b, j = c // 2, c % 2
        ow = own_tiles(j)
        shard = res.results[c]["y_out"]
        for i, t in enumerate(ow):
            out[b, t * 128:(t + 1) * 128, :] = shard[i]
    return out


# version 10
# speedup vs baseline: 1.3687x; 1.0227x over previous
import numpy as np
import concourse.bass as bass
import concourse.mybir as mybir
from concourse.bass_utils import run_bass_kernel_spmd
from contextlib import ExitStack

F32 = mybir.dt.float32
BF16 = mybir.dt.bfloat16
AF = mybir.ActivationFunctionType
ALU = mybir.AluOpType
AX = mybir.AxisListType

D = 2048
T = 4096
NT = 32
NIC = 16
H = 16
HS = 128
OWN = 16
NEG_SQRT_HS = -float(np.sqrt(np.float64(HS)))
INV_SQRT_HS = float(1.0 / np.sqrt(np.float64(HS)))
EPS = 1e-5
SH = 256.5
WMEAN_INV = 1.0 / (D * D)


def own_tiles(j):
    return [t for t in range(NT) if (0 if (t % 4) in (0, 3) else 1) == j]


class Ev:
    __slots__ = ("sem", "val")

    def __init__(self, sem, val):
        self.sem = sem
        self.val = val


class Eng:
    def __init__(self, sem, auto_serial):
        self.sem = sem
        self.cnt = 0
        self.auto = auto_serial
        self.waited = {}
        self.todo = []

    def wait(self, ev):
        if ev is None:
            return
        key = id(ev.sem)
        if self.waited.get(key, 0) >= ev.val:
            return
        self.waited[key] = ev.val
        sem, val = ev.sem, ev.val
        self.todo.append(lambda e: e.wait_ge(sem, val))

    def waits(self, evs):
        for ev in evs:
            self.wait(ev)

    def op(self, name, *args, inc=True, waits=(), **kw):
        self.waits(waits)
        if self.auto and self.cnt > 0:
            self.wait(Ev(self.sem, self.cnt))
        sem = self.sem
        if inc:
            self.cnt += 1

            def emit(e, name=name, args=args, kw=kw, sem=sem):
                getattr(e, name)(*args, **kw).then_inc(sem, 1)
        else:

            def emit(e, name=name, args=args, kw=kw):
                getattr(e, name)(*args, **kw)
        self.todo.append(emit)
        return Ev(self.sem, self.cnt) if inc else None

    def ev(self):
        return Ev(self.sem, self.cnt)


class DmaBuf:
    def __init__(self, nc, ctx, name):
        self.sem = ctx.enter_context(nc.semaphore(name))
        self.cum = 0

    def start(self, qeng, dst, src, waits=()):
        qeng.waits(waits)
        sem = self.sem
        self.cum += 16
        qeng.todo.append(lambda e, d=dst, s=src, sem=sem: e.dma_start(d, s).then_inc(sem, 16))
        return Ev(sem, self.cum)

    def ev(self):
        return Ev(self.sem, self.cum)


def build(dbg=(), upto='Z'):
    dbg = set(dbg)
    nc = bass.Bass("TRN2", target_bir_lowering=False)

    x0 = nc.dram_tensor("x0", [T, D], F32, kind="ExternalInput").ap()
    wqT = nc.dram_tensor("wqT", [D, D], F32, kind="ExternalInput").ap()
    wkT = nc.dram_tensor("wkT", [D, D], F32, kind="ExternalInput").ap()
    wvT = nc.dram_tensor("wvT", [D, D], F32, kind="ExternalInput").ap()
    woT = nc.dram_tensor("woT", [D, D], F32, kind="ExternalInput").ap()
    wfT = nc.dram_tensor("wfT", [D, D], F32, kind="ExternalInput").ap()
    ident_in = nc.dram_tensor("ident_in", [128, 128], F32, kind="ExternalInput").ap()
    maskd_in = nc.dram_tensor("maskd_in", [128, 128], F32, kind="ExternalInput").ap()
    msel_in = nc.dram_tensor("msel_in", [128, OWN], F32, kind="ExternalInput").ap()
    schs_in = nc.dram_tensor("schs_in", [128, 1], F32, kind="ExternalInput").ap()

    y_out = nc.dram_tensor("y_out", [OWN, 128, D], F32, kind="ExternalOutput").ap()
    dbg_t = {}
    if "y1" in dbg:
        dbg_t["d_y1"] = nc.dram_tensor("d_y1", [NT, 128, D], F32, kind="ExternalOutput").ap()
    if "y1ln" in dbg:
        dbg_t["d_y1ln"] = nc.dram_tensor("d_y1ln", [NT, 128, D], F32, kind="ExternalOutput").ap()
    if "kvq" in dbg:
        dbg_t["d_k"] = nc.dram_tensor("d_k", [128, 2, T], F32, kind="ExternalOutput").ap()
        dbg_t["d_v"] = nc.dram_tensor("d_v", [128, NT, 258], F32, kind="ExternalOutput").ap()
        dbg_t["d_q"] = nc.dram_tensor("d_q", [128, 2, OWN * 128], F32, kind="ExternalOutput").ap()
    if "attn" in dbg:
        dbg_t["d_at"] = nc.dram_tensor("d_at", [H, 128, OWN * 128], F32, kind="ExternalOutput").ap()
    if "y2" in dbg:
        dbg_t["d_y2"] = nc.dram_tensor("d_y2", [OWN, 128, D], F32, kind="ExternalOutput").ap()
    if "qdbg" in dbg:
        dbg_t["d_qd"] = nc.dram_tensor("d_qd", [4, 128, D], F32, kind="ExternalOutput").ap()
    if "xq2" in dbg:
        dbg_t["d_xq2"] = nc.dram_tensor("d_xq2", [OWN, 128, D], F32, kind="ExternalOutput").ap()
        dbg_t["d_st2"] = nc.dram_tensor("d_st2", [128, 2], F32, kind="ExternalOutput").ap()
    if "att2" in dbg:
        dbg_t["d_o0"] = nc.dram_tensor("d_o0", [128, 4, 129], F32, kind="ExternalOutput").ap()
        dbg_t["d_nc"] = nc.dram_tensor("d_nc", [128, 2], F32, kind="ExternalOutput").ap()
        dbg_t["d_p0"] = nc.dram_tensor("d_p0", [128, 512], F32, kind="ExternalOutput").ap()
        dbg_t["d_ch"] = nc.dram_tensor("d_ch", [128, 12], F32, kind="ExternalOutput").ap()

    s_y1ln = nc.dram_tensor("s_y1ln", [NT, 128, D], BF16, kind="Internal").ap()
    s_y1own = nc.dram_tensor("s_y1own", [OWN, 128, D], F32, kind="Internal").ap()
    s_ylT = nc.dram_tensor("s_ylT", [128, NIC, T], BF16, kind="Internal").ap()
    s_wqT = nc.dram_tensor("s_wqT", [128, NIC * D], BF16, kind="Internal").ap()
    s_attnT = nc.dram_tensor("s_attnT", [H, 128, OWN * 128], BF16, kind="Internal").ap()

    with ExitStack() as ctx:
        TE = Eng(ctx.enter_context(nc.semaphore("sem_t")), False)
        VE = Eng(ctx.enter_context(nc.semaphore("sem_v")), True)
        SE = Eng(ctx.enter_context(nc.semaphore("sem_s")), True)
        GE = Eng(ctx.enter_context(nc.semaphore("sem_g")), False)
        YE = Eng(ctx.enter_context(nc.semaphore("sem_y")), False)
        ALL = [TE, VE, SE, GE, YE]

        ALL_DMABUFS = []

        def barrier():
            evs = [g.ev() for g in ALL]
            for g in ALL:
                for ev in evs:
                    if ev.sem is not g.sem:
                        g.wait(ev)
                for db in ALL_DMABUFS:
                    if db.cum:
                        g.wait(db.ev())

        def DB(name):
            b = DmaBuf(nc, ctx, name)
            ALL_DMABUFS.append(b)
            return b

        d_x0 = [DB(f"dm_x{i}") for i in range(2)]
        d_wld = [DB(f"dm_w{i}") for i in range(2)]
        d_y1lnst = [DB(f"dm_ylst{i}") for i in range(2)]
        d_y1st = [DB(f"dm_y1st{i}") for i in range(2)]
        d_const = DB("dm_const")
        d_ylld = [DB(f"dm_ylld{i}") for i in range(2)]
        d_ylTst = [DB(f"dm_ylTst{i}") for i in range(2)]
        d_ylTld = [DB(f"dm_ylTld{i}") for i in range(2)]
        d_wsl = DB("dm_wsl")
        d_atst = DB("dm_atst")
        d_atld = [DB(f"dm_atld{i}") for i in range(2)]
        d_y1ld = [DB(f"dm_y1ld{i}") for i in range(2)]
        d_wcd = DB("dm_wcd")
        d_out = DB("dm_out")
        d_dbg = DB("dm_dbg")
        d_dbgY = [DB(f"dm_dbgY{i}") for i in range(2)]
        d_dbgG2 = [DB(f"dm_dbgG{i}") for i in range(2)]

        identb = ctx.enter_context(nc.sbuf_tensor("identb", [128, 128], BF16))
        identf = ctx.enter_context(nc.sbuf_tensor("identf", [128, 128], F32))
        maskd = ctx.enter_context(nc.sbuf_tensor("maskd", [128, 128], BF16))
        msel = ctx.enter_context(nc.sbuf_tensor("msel", [128, OWN], F32))
        ones_f = ctx.enter_context(nc.sbuf_tensor("ones_f", [128, 128], F32))
        pad_f = ctx.enter_context(nc.sbuf_tensor("pad_f", [128, 128], F32))

        xm = ctx.enter_context(nc.sbuf_tensor("xm", [128, D], F32))
        xqT = ctx.enter_context(nc.sbuf_tensor("xqT", [128, NIC, 128], BF16))
        junk = ctx.enter_context(nc.sbuf_tensor("junk", [128, D], F32))
        junk2 = ctx.enter_context(nc.sbuf_tensor("junk2", [128, D], F32))
        junk_s = ctx.enter_context(nc.sbuf_tensor("junk_s", [128, D], F32))

        def vec(name):
            return ctx.enter_context(nc.sbuf_tensor(name, [128, 1], F32))

        st_cs = vec("st_cs")
        st_sq = vec("st_sq")
        st_m = vec("st_m")
        st_msq = vec("st_msq")
        st_var = vec("st_var")
        st_sig = vec("st_sig")
        st_inv = vec("st_inv")
        st_am = vec("st_am")
        st_a = vec("st_a")
        st_ra = vec("st_ra")
        st_g = vec("st_g")
        st_st = vec("st_st")
        wsc = vec("wsc")
        wsc_inv = vec("wsc_inv")
        wscB = vec("wscB")
        st_mx = vec("st_mx")
        st_gm = vec("st_gm")
        negc = [vec("negc0"), vec("negc1")]
        wabs = ctx.enter_context(nc.sbuf_tensor("wabs", [128, NIC], F32))

        ev_c = None
        for dst, src in ((identb, ident_in), (identf, ident_in), (maskd, maskd_in),
                         (msel, msel_in)):
            ev_c = d_const.start(GE, dst[:, :], src[:, :])
        ev_consts = ev_c
        ev_ones = VE.op("memset", ones_f[:, :], 1.0)
        ones_b = ctx.enter_context(nc.sbuf_tensor("ones_b", [128, 128], BF16))
        ev_onesb = VE.op("memset", ones_b[:, :], 1.0)
        st_eps = vec("st_eps")
        ev_eps = VE.op("memset", st_eps[:, :], EPS)
        st_schs = vec("st_schs")
        st_r127 = vec("st_r127")
        schs_sb = ctx.enter_context(nc.sbuf_tensor("schs_sb", [128, 1], F32))
        ev_schs_ld = d_const.start(GE, schs_sb[:, :], schs_in[:, :])
        ev_consts = ev_schs_ld
        VE.op("tensor_copy", st_schs[:, :], schs_sb[:, :], waits=[ev_schs_ld])
        VE.op("memset", st_r127[:, :], 127.0)
        ev_r127 = VE.op("reciprocal", st_r127[:, :], st_r127[:, :])

        def ln_quant(src_ap, waits_src, xq_out=None, st_st_out=None):
            VE.op("reduce_sum", st_cs[:, :], src_ap, axis=AX.X, waits=waits_src)
            SE.op("activation", out=junk_s[:, :], in_=src_ap, func=AF.Square,
                  accum_out=st_sq[:, :], waits=list(waits_src) + [VE.ev()])
            ev_sq = SE.ev()
            VE.op("tensor_scalar", out=st_m[:, :], in0=st_cs[:, :], scalar1=1.0 / D,
                  scalar2=None, op0=ALU.mult)
            VE.op("tensor_scalar", out=xm[:, :], in0=src_ap, scalar1=st_m[:, 0:1],
                  scalar2=None, op0=ALU.subtract)
            VE.op("reduce_max", st_am[:, :], xm[:, :], axis=AX.X, apply_absolute_value=True)
            VE.op("tensor_mul", st_msq[:, :], st_m[:, :], st_m[:, :])
            VE.op("scalar_tensor_tensor", out=st_var[:, :], in0=st_sq[:, :], scalar=1.0 / D,
                  in1=st_msq[:, :], op0=ALU.mult, op1=ALU.subtract, waits=[ev_sq])
            ev_var = VE.ev()
            ev_sig = SE.op("activation", out=st_sig[:, :], in_=st_var[:, :], func=AF.Sqrt,
                           bias=st_eps[:, 0:1], waits=[ev_var, ev_eps])
            VE.op("reciprocal", st_inv[:, :], st_sig[:, :], waits=[ev_sig])
            VE.op("tensor_mul", st_a[:, :], st_am[:, :], st_inv[:, :])
            VE.op("tensor_scalar_max", st_a[:, :], st_a[:, :], 1e-8)
            VE.op("reciprocal", st_ra[:, :], st_a[:, :])
            VE.op("tensor_scalar", out=st_ra[:, :], in0=st_ra[:, :], scalar1=127.0,
                  scalar2=None, op0=ALU.mult)
            VE.op("tensor_mul", st_g[:, :], st_inv[:, :], st_ra[:, :])
            xi = ln_quant.xi[0]
            VE.op("tensor_scalar", out=junk[:, :], in0=xm[:, :], scalar1=st_g[:, 0:1],
                  scalar2=SH, op0=ALU.mult, op1=ALU.add)
            VE.op("tensor_copy", xi[:, :], junk[:, :])
            VE.op("tensor_copy", junk2[:, :], xi[:, :])
            if ln_quant.qdbg[0] is not None:
                qd = ln_quant.qdbg[0]
                d_dbg.start(YE, qd[0], junk[:, :], waits=[VE.ev()])
                d_dbg.start(YE, qd[1], junk2[:, :], waits=[VE.ev()])
            VE.op("tensor_tensor", xm[:, :], junk[:, :], junk2[:, :], op=ALU.subtract)
            if ln_quant.qdbg[0] is not None:
                qd = ln_quant.qdbg[0]
                d_dbg.start(YE, qd[2], xm[:, :], waits=[VE.ev()])
            VE.op("tensor_scalar", out=junk[:, :], in0=xm[:, :], scalar1=1.0,
                  scalar2=None, op0=ALU.is_ge, waits=[d_dbg.ev()])
            VE.op("scalar_tensor_tensor", out=junk[:, :], in0=xm[:, :], scalar=2.0,
                  in1=junk[:, :], op0=ALU.is_ge, op1=ALU.add)
            VE.op("scalar_tensor_tensor", out=junk[:, :], in0=xm[:, :], scalar=0.0,
                  in1=junk[:, :], op0=ALU.is_lt, op1=ALU.subtract)
            VE.op("scalar_tensor_tensor", out=junk[:, :], in0=xm[:, :], scalar=-1.0,
                  in1=junk[:, :], op0=ALU.is_lt, op1=ALU.add)
            VE.op("tensor_tensor", junk[:, :], junk2[:, :], junk[:, :], op=ALU.subtract)
            VE.op("tensor_scalar", out=junk[:, :], in0=junk[:, :], scalar1=383.0,
                  scalar2=129.0, op0=ALU.min, op1=ALU.max)
            xq_o = xq_out
            st_o = st_st if st_st_out is None else st_st_out
            ev_xq = VE.op("tensor_scalar", out=xq_o[:, :], in0=junk[:, :], scalar1=256.0,
                          scalar2=None, op0=ALU.subtract, waits=[ln_quant.ge_guard[0]])
            VE.op("tensor_mul", st_o[:, :], st_a[:, :], wscB[:, :])
            return ev_xq

        ln_quant.ge_guard = [None]
        ln_quant.xi = [None]
        ln_quant.qdbg = [None]

        tr_state = {0: None, 1: None}

        def transpose16(trps, src_fn, dst_fn, ev_src, ev_prev_cons):
            ev_out = None
            for g in range(4):
                b = g % 2
                for kk in range(4):
                    ic = g * 4 + kk
                    w = []
                    if kk == 0:
                        w = [ev_src, ev_consts, tr_state[b]]
                        if g == 0 and ev_prev_cons is not None:
                            w.append(ev_prev_cons)
                    TE.op("matmul", trps[b][:, kk * 128:(kk + 1) * 128], src_fn(ic),
                          identb[:, :], start=True, stop=True, inc=(kk == 3), waits=w)
                ev_mm = TE.ev()
                ev_out = SE.op("copy", dst_fn(g), trps[b][:, :], waits=[ev_mm])
                tr_state[b] = ev_out
            return ev_out

        def ln2_normalize(src_ap, dst_ap, ev_src, extra_waits=()):
            VE.op("reduce_sum", st_cs[:, :], src_ap, axis=AX.X, waits=[ev_src])
            SE.op("activation", out=junk_s[:, :], in_=src_ap, func=AF.Square,
                  accum_out=st_sq[:, :], waits=[ev_src, VE.ev()])
            ev_sq = SE.ev()
            VE.op("tensor_scalar", out=st_m[:, :], in0=st_cs[:, :], scalar1=1.0 / D,
                  scalar2=None, op0=ALU.mult)
            VE.op("tensor_mul", st_msq[:, :], st_m[:, :], st_m[:, :])
            VE.op("scalar_tensor_tensor", out=st_var[:, :], in0=st_sq[:, :], scalar=1.0 / D,
                  in1=st_msq[:, :], op0=ALU.mult, op1=ALU.subtract, waits=[ev_sq])
            ev_var = VE.ev()
            ev_sig = SE.op("activation", out=st_sig[:, :], in_=st_var[:, :], func=AF.Sqrt,
                           bias=st_eps[:, 0:1], waits=[ev_var, ev_eps])
            VE.op("reciprocal", st_inv[:, :], st_sig[:, :], waits=[ev_sig])
            return VE.op("tensor_scalar", out=dst_ap, in0=src_ap, scalar1=st_m[:, 0:1],
                         scalar2=st_inv[:, 0:1], op0=ALU.subtract, op1=ALU.mult,
                         waits=list(extra_waits))

        with ExitStack() as actx:
            wbuf = [actx.enter_context(nc.sbuf_tensor(f"wbuf{i}", [128, D], F32)) for i in range(2)]
            xbuf = [actx.enter_context(nc.sbuf_tensor(f"xbuf{i}", [128, D], F32)) for i in range(2)]
            y1b = [actx.enter_context(nc.sbuf_tensor(f"y1b{i}", [128, D], F32)) for i in range(2)]
            y1ln = [actx.enter_context(nc.sbuf_tensor(f"y1ln{i}", [128, D], BF16)) for i in range(2)]
            u1 = actx.enter_context(nc.sbuf_tensor("u1", [128, NIC * D], BF16))
            u1_wq = u1.ap().rearrange("p (c n) -> p c n", n=D)
            xiA = actx.enter_context(nc.sbuf_tensor("xiA", [128, D], mybir.dt.int32))
            xqb = [actx.enter_context(nc.sbuf_tensor(f"xqb{i}", [128, D], BF16)) for i in range(2)]
            xqTb = [actx.enter_context(nc.sbuf_tensor(f"xqTb{i}", [128, NIC, 128], BF16)) for i in range(2)]
            ylnT = [actx.enter_context(nc.sbuf_tensor(f"ylnT{i}", [128, NIC, 128], BF16)) for i in range(2)]
            st_stb = [actx.enter_context(nc.sbuf_tensor(f"st_stb{i}", [128, 1], F32)) for i in range(2)]
            ln_quant.xi[0] = xiA
            ypsum = actx.enter_context(nc.psum_tensor("ypsum", [128, D], F32))
            trpsA = [actx.enter_context(nc.psum_tensor(f"trpsA{i}", [128, 512], F32)) for i in range(2)]

            ev_w = [None, None]
            ev_wfree = [None, None]
            for k in range(NIC):
                b = k % 2
                ev_w[b] = d_wld[b].start(GE, wbuf[b][:, :], wfT[k * 128:(k + 1) * 128, :],
                                         waits=[ev_wfree[b]] if ev_wfree[b] else [])
                ev_wfree[b] = SE.op("activation", out=junk_s[:, :], in_=wbuf[b][:, :],
                                    func=AF.Abs, accum_out=wabs[:, k:k + 1], waits=[ev_w[b]])
            ev_ws = VE.op("reduce_sum", st_cs[:, :], wabs[:, :], axis=AX.X,
                          waits=[ev_wfree[0], ev_wfree[1]])
            VE.op("tensor_copy", st_mx[:, :], st_cs[:, :])
            ev_wsv = VE.ev()
            ev_wsum = TE.op("matmul", trpsA[0][:, 0:1], ones_f[:, :], st_mx[:, :],
                            start=True, stop=True, waits=[ev_ones, ev_wsv])
            VE.op("tensor_scalar", out=wsc[:, :], in0=trpsA[0][:, 0:1],
                  scalar1=WMEAN_INV, scalar2=None, op0=ALU.mult, waits=[ev_wsum])
            VE.op("tensor_scalar_max", wsc[:, :], wsc[:, :], 1e-6)
            ev_wsci = VE.op("reciprocal", wsc_inv[:, :], wsc[:, :])
            VE.op("tensor_scalar", out=wscB[:, :], in0=wsc[:, :],
                  scalar1=0.5, scalar2=None, op0=ALU.mult)
            ev_wscB = VE.op("tensor_mul", wscB[:, :], wscB[:, :], st_r127[:, 0:1],
                            waits=[ev_r127])
            ev_wsum_read = ev_wscB

            ev_w = [None, None]
            for k in range(NIC):
                b = k % 2
                ev_w[b] = d_wld[b].start(GE, wbuf[b][:, :], wfT[k * 128:(k + 1) * 128, :],
                                         waits=[ev_wfree[b]] if ev_wfree[b] else [])
                xi = ln_quant.xi[0]
                VE.op("tensor_scalar", out=junk[:, :], in0=wbuf[b][:, :],
                      scalar1=wsc_inv[:, 0:1], scalar2=SH, op0=ALU.mult, op1=ALU.add,
                      waits=[ev_w[b], ev_wsci])
                ev_wfree[b] = VE.ev()
                VE.op("tensor_copy", xi[:, :], junk[:, :])
                VE.op("tensor_copy", junk2[:, :], xi[:, :])
                VE.op("tensor_tensor", xm[:, :], junk[:, :], junk2[:, :], op=ALU.subtract)
                VE.op("tensor_scalar", out=junk[:, :], in0=xm[:, :], scalar1=1.0,
                      scalar2=None, op0=ALU.is_ge)
                VE.op("tensor_scalar", out=xm[:, :], in0=xm[:, :], scalar1=0.0,
                      scalar2=None, op0=ALU.is_lt)
                VE.op("tensor_tensor", junk[:, :], junk[:, :], xm[:, :], op=ALU.subtract)
                VE.op("tensor_tensor", junk[:, :], junk[:, :], junk2[:, :], op=ALU.add)
                VE.op("tensor_scalar", out=junk[:, :], in0=junk[:, :], scalar1=257.0,
                      scalar2=255.0, op0=ALU.min, op1=ALU.max)
                VE.op("tensor_scalar", out=u1_wq[:, k, :], in0=junk[:, :], scalar1=256.0,
                      scalar2=None, op0=ALU.subtract)
            ev_wq_ready = VE.ev()
            d_wqst = DB("dm_wqst")
            ev_wq_st = d_wqst.start(YE, s_wqT[:, :], u1.ap()[:, :], waits=[ev_wq_ready])
            if "wq" in dbg:
                for k in range(NIC):
                    VE.op("tensor_copy", junk[:, :], u1_wq[:, k, :], waits=[ev_wq_ready])
                    d_dbg.start(YE, dbg_t["d_wq"][:, k, :], junk[:, :], waits=[VE.ev()])

            ev_x = [None, None]
            ev_xfree = [None, None]
            ev_y1free = [[], []]
            ev_ylnfree = [[], []]
            ev_ffn = [None, None]
            ev_trans = [None, None]
            ev_ylnT_store = [None, None]
            ev_stt_prev = None
            ev_xq_e = [None, None]
            ev_xqT_e = [None, None]

            def quant_stage(t):
                b = t % 2
                ev_x[b] = d_x0[b].start(YE, xbuf[b][:, :], x0[t * 128:(t + 1) * 128, :],
                                        waits=[ev_xfree[b]] if ev_xfree[b] else [])
                w = [ev_x[b], ev_wscB]
                if ev_trans[b] is not None:
                    w.append(ev_trans[b])
                ev_xq_e[b] = ln_quant(xbuf[b][:, :], w, xq_out=xqb[b], st_st_out=st_stb[b])
                ev_xqT = transpose16(
                    trpsA, lambda ic, bb=b: xqb[bb][:, ic * 128:(ic + 1) * 128],
                    lambda g, bb=b: xqTb[bb].ap()[:, g * 4:(g + 1) * 4, :].rearrange("p c n -> p (c n)"),
                    ev_xq_e[b], ev_ffn[b])
                ev_trans[b] = TE.ev()
                ev_xqT_e[b] = ev_xqT

            def ffn_stage(t):
                b = t % 2
                for oc in range(4):
                    for ic in range(NIC):
                        w = []
                        if ic == 0 and oc == 0:
                            w = [ev_xqT_e[b], ev_wq_ready]
                            if ev_stt_prev is not None:
                                w.append(ev_stt_prev)
                        TE.op("matmul", ypsum[:, oc * 512:(oc + 1) * 512], xqTb[b][:, ic, :],
                              u1_wq[:, ic, oc * 512:(oc + 1) * 512],
                              start=(ic == 0), stop=(ic == NIC - 1),
                              inc=(ic == NIC - 1 and oc == 3), waits=w)
                ev_ffn[b] = TE.ev()

            def finish_stage(t):
                nonlocal ev_stt_prev
                b = t % 2
                yb = y1b[b]
                VE.op("scalar_tensor_tensor", out=yb[:, :], in0=ypsum[:, :],
                      scalar=st_stb[b][:, 0:1], in1=xbuf[b][:, :], op0=ALU.mult, op1=ALU.add,
                      waits=[ev_ffn[b]] + ev_y1free[b])
                ev_y1 = VE.ev()
                ev_stt_prev = ev_y1
                ev_xfree[b] = ev_y1
                ev_yln = ln2_normalize(yb[:, :], y1ln[b][:, :], ev_y1, ev_ylnfree[b])
                ev_tr = transpose16(
                    trpsA, lambda ic, bb=b: y1ln[bb][:, ic * 128:(ic + 1) * 128],
                    lambda g, bb=b: ylnT[bb].ap()[:, g * 4:(g + 1) * 4, :].rearrange("p c n -> p (c n)"),
                    ev_yln, ev_ylnT_store[b])
                for ic in range(NIC):
                    d_ylTst[b].start(YE, s_ylT[:, ic, t * 128:(t + 1) * 128], ylnT[b][:, ic, :],
                                     waits=[ev_tr] if ic == 0 else [])
                ev_ylnT_store[b] = d_ylTst[b].ev()
                ev_ylnfree[b] = [ev_tr]
                ev_y1free[b] = [ev_y1]
                if t < OWN:
                    ev_y1free[b].append(d_y1st[b].start(YE, s_y1own[t, :, :], yb[:, :],
                                                        waits=[ev_y1]))
                if "y1" in dbg:
                    ev_y1free[b].append(d_dbgY[b].start(YE, dbg_t["d_y1"][t, :, :], yb[:, :],
                                                        waits=[ev_y1]))
                if "y1ln" in dbg:
                    ev_ylnfree[b].append(d_dbgG2[b].start(GE, dbg_t["d_y1ln"][t, :, :],
                                                          y1ln[b][:, :], waits=[ev_yln]))

            quant_stage(0)
            ffn_stage(0)
            for t in range(1, NT):
                quant_stage(t)
                finish_stage(t - 1)
                ffn_stage(t)
            finish_stage(NT - 1)

            barrier()

        if upto >= 'B':
            with ExitStack() as bctx:
                Kt = bctx.enter_context(nc.sbuf_tensor("Kt", [128, 2, T], BF16))
                Vh = bctx.enter_context(nc.sbuf_tensor("Vh", [128, NT, 258], BF16))
                Qt = bctx.enter_context(nc.sbuf_tensor("Qt", [128, 2, OWN * 128], BF16))
                Pt = [bctx.enter_context(nc.sbuf_tensor(f"Pt{i}", [128, 512], BF16)) for i in range(2)]
                wk_sl = bctx.enter_context(nc.sbuf_tensor("wk_sl", [128, NIC, 256], BF16))
                wv_sl = bctx.enter_context(nc.sbuf_tensor("wv_sl", [128, NIC, 256], BF16))
                wq_sl = bctx.enter_context(nc.sbuf_tensor("wq_sl", [128, NIC, 256], BF16))
                ylstr = [bctx.enter_context(nc.sbuf_tensor(f"ylstr{i}", [128, NIC, 512], BF16)) for i in range(2)]
                attn_sb = bctx.enter_context(nc.sbuf_tensor("attn_sb", [128, 128], BF16))
                attnT = bctx.enter_context(nc.sbuf_tensor("attnT", [128, 2, OWN * 128], BF16))

                for hg in range(H // 2):
                    barrier()
                    SE.wait(d_atst.ev())
                    h0 = 2 * hg
                    d_wsl.start(GE, wk_sl[:, :, :],
                                wkT.rearrange("(c p) o -> p c o", p=128)[:, :, h0 * 128:(h0 + 2) * 128])
                    d_wsl.start(GE, wv_sl[:, :, :],
                                wvT.rearrange("(c p) o -> p c o", p=128)[:, :, h0 * 128:(h0 + 2) * 128])
                    d_wsl.start(GE, wq_sl[:, :, :],
                                wqT.rearrange("(c p) o -> p c o", p=128)[:, :, h0 * 128:(h0 + 2) * 128])
                    ev_wall = d_wsl.ev()
                    VE.op("memset", Vh.ap()[:, :, 128:129], 1.0)
                    ev_onescol = VE.op("memset", Vh.ap()[:, :, 257:258], 1.0)

                    with ExitStack() as pctx:
                        psK = [pctx.enter_context(nc.psum_tensor(f"psK{hg}_{i}", [128, 512], F32)) for i in range(2)]
                        psV = [pctx.enter_context(nc.psum_tensor(f"psV{hg}_{i}", [128, 256], F32)) for i in range(2)]
                        psQ = pctx.enter_context(nc.psum_tensor(f"psQ{hg}", [128, 512], F32))
                        ev_str = [None, None]
                        ev_strfree = [None, None]
                        evK = {0: None, 1: None}
                        evV = {0: None, 1: None}
                        evQ = None
                        for t5 in range(8):
                            b = t5 % 2
                            ev_str[b] = d_ylTld[b].start(YE, ylstr[b][:, :, :],
                                                         s_ylT[:, :, t5 * 512:(t5 + 1) * 512],
                                                         waits=[ev_strfree[b]] if ev_strfree[b] else [])
                            for dc in range(2):
                                for ic in range(NIC):
                                    w = [ev_wall, ev_str[b], evK[dc]] if ic == 0 else []
                                    TE.op("matmul", psK[dc][:, :], wk_sl[:, ic, dc * 128:(dc + 1) * 128],
                                          ylstr[b][:, ic, :], start=(ic == 0), stop=(ic == NIC - 1),
                                          inc=(ic == NIC - 1), waits=w)
                                evK[dc] = SE.op("copy", Kt[:, dc, t5 * 512:(t5 + 1) * 512],
                                                psK[dc][:, :], waits=[TE.ev()])
                            for tc in range(4):
                                kt = t5 * 4 + tc
                                pb = tc % 2
                                for ic in range(NIC):
                                    w = [ev_wall, ev_str[b], evV[pb]] if ic == 0 else []
                                    TE.op("matmul", psV[pb][:, :],
                                          ylstr[b][:, ic, tc * 128:(tc + 1) * 128],
                                          wv_sl[:, ic, :], start=(ic == 0), stop=(ic == NIC - 1),
                                          inc=(ic == NIC - 1), waits=w)
                                ev_mm = TE.ev()
                                SE.op("copy", Vh[:, kt, 0:128], psV[pb][:, 0:128], waits=[ev_mm])
                                evV[pb] = SE.op("copy", Vh[:, kt, 129:257], psV[pb][:, 128:256])
                            if t5 < 4:
                                for dc in range(2):
                                    for ic in range(NIC):
                                        w = [ev_wall, ev_str[b], evQ] if ic == 0 else []
                                        TE.op("matmul", psQ[:, :], wq_sl[:, ic, dc * 128:(dc + 1) * 128],
                                              ylstr[b][:, ic, :], start=(ic == 0), stop=(ic == NIC - 1),
                                              inc=(ic == NIC - 1), waits=w)
                                    evQ = SE.op("activation",
                                                out=Qt[:, dc, t5 * 512:(t5 + 1) * 512],
                                                in_=psQ[:, :], func=AF.Copy,
                                                scale=st_schs[:, 0:1], waits=[TE.ev()])
                            ev_strfree[b] = TE.ev()
                        ev_proj = [evK[0], evK[1], evV[0], evV[1], evQ]

                        ev_negc = [None, None]
                        ev_sqr = None
                        for dc in range(2):
                            chsc = ylstr[0].ap().rearrange("p c n -> p (c n)")[:, 0:D]
                            ev_sq = SE.op("activation", out=chsc, in_=Qt[:, dc, :],
                                          func=AF.Square,
                                          waits=ev_proj + [ev_negc[0], ev_sqr, TE.ev()])
                            for c4 in range(4):
                                ev_mm = TE.op("matmul", psQ[:, :], ones_b[:, :],
                                              chsc[:, c4 * 512:(c4 + 1) * 512], start=True,
                                              stop=True, waits=[ev_sq, ev_onesb, evQ,
                                                                ev_negc[0], ev_negc[1],
                                                                VE.ev()])
                                VE.op("reduce_max", st_gm[:, :], psQ[:, :], axis=AX.X,
                                      waits=[ev_mm])
                                if c4 == 0:
                                    VE.op("tensor_copy", st_am[:, :], st_gm[:, :])
                                else:
                                    VE.op("tensor_max", st_am[:, :], st_am[:, :], st_gm[:, :])
                            ev_q2 = VE.ev()
                            ev_first = None
                            for half in range(2):
                                ev_sqr = SE.op("activation", out=chsc,
                                               in_=Kt[:, dc, half * D:(half + 1) * D],
                                               func=AF.Square, waits=[ev_q2, TE.ev()])
                                for c4 in range(4):
                                    ev_mm = TE.op("matmul", psQ[:, :], ones_b[:, :],
                                                  chsc[:, c4 * 512:(c4 + 1) * 512], start=True,
                                                  stop=True, waits=[ev_sqr, VE.ev()])
                                    VE.op("reduce_max", st_gm[:, :], psQ[:, :], axis=AX.X,
                                          waits=[ev_mm])
                                    if half == 0 and c4 == 0:
                                        VE.op("tensor_copy", st_mx[:, :], st_gm[:, :])
                                    else:
                                        VE.op("tensor_max", st_mx[:, :], st_mx[:, :], st_gm[:, :])
                            VE.op("tensor_mul", st_gm[:, :], st_am[:, :], st_mx[:, :])
                            ev_m2 = VE.ev()
                            ev_rt = SE.op("activation", out=st_gm[:, :], in_=st_gm[:, :],
                                          func=AF.Sqrt, waits=[ev_m2])
                            ev_negc[dc] = VE.op("tensor_scalar", out=negc[dc][:, :],
                                                in0=st_gm[:, :], scalar1=-1.0,
                                                scalar2=None, op0=ALU.mult, waits=[ev_rt])
                            if "att2" in dbg and hg == 0 and dc == 0:
                                VE.op("tensor_copy", junk[:, 0:1], st_am[:, :])
                                VE.op("tensor_copy", junk[:, 1:2], st_mx[:, :])
                                VE.op("tensor_copy", junk[:, 2:3], negc[0][:, :])
                                d_dbg.start(YE, dbg_t["d_ch"][:, :], junk[:, 0:12], waits=[VE.ev()])
                                VE.op("memset", junk[:, 0:1], 0.0, waits=[d_dbg.ev()])

                    if "kvq" in dbg and hg == 0:
                        ev_d = None
                        for dc in range(2):
                            for half in range(2):
                                VE.op("tensor_copy", junk[:, :], Kt[:, dc, half * D:(half + 1) * D],
                                      waits=ev_proj + [ev_d])
                                ev_d = d_dbg.start(YE, dbg_t["d_k"][:, dc, half * D:(half + 1) * D],
                                                   junk[:, :], waits=[VE.ev()])
                            VE.op("tensor_copy", junk[:, :], Qt[:, dc, :], waits=ev_proj + [ev_d])
                            ev_d = d_dbg.start(YE, dbg_t["d_q"][:, dc, :], junk[:, :], waits=[VE.ev()])
                        for kt in range(NT):
                            VE.op("tensor_copy", junk[:, 0:258], Vh[:, kt, :], waits=ev_proj + [ev_d])
                            ev_d = d_dbg.start(YE, dbg_t["d_v"][:, kt, :], junk[:, 0:258], waits=[VE.ev()])
                        VE.op("memset", junk[:, 0:1], 0.0, waits=[ev_d])

                    with ExitStack() as actx2:
                        Sps = [actx2.enter_context(nc.psum_tensor(f"Sps{hg}_{i}", [128, 512], F32)) for i in range(2)]
                        Ops = [actx2.enter_context(nc.psum_tensor(f"Ops{hg}_{i}", [128, 129], F32)) for i in range(4)]
                        ev_exp = {0: None, 1: None}
                        ev_pvd = {0: None, 1: None}
                        ev_o_cons = None
                        ev_att_st = None
                        for dc in range(2):
                            h = h0 + dc
                            for qg in range(4):
                                i_lo = qg * 4
                                i_hi = i_lo + 4
                                kts = [(0, k) for k in range(i_hi)] + [(1, k) for k in range(i_hi)]
                                for it, (side, k) in enumerate(kts):
                                    vlo = max(i_lo, k)
                                    off = (vlo - i_lo) * 128
                                    b = it % 2
                                    kcol = (k if side == 0 else 16 + k) * 128
                                    first = [ev_proj[0], ev_proj[1], ev_proj[4],
                                             ev_negc[0], ev_negc[1]]
                                    if ev_exp[b] is not None:
                                        first.append(ev_exp[b])
                                    if ev_o_cons is not None:
                                        first.append(ev_o_cons)
                                    ev_sc = TE.op("matmul", Sps[b][:, off:512],
                                                  Kt[:, dc, kcol:kcol + 128],
                                                  Qt[:, dc, i_lo * 128 + off: (i_lo + 4) * 128],
                                                  start=True, stop=True, waits=first)
                                    ev_e = SE.op("activation", out=Pt[b][:, off:512],
                                                 in_=Sps[b][:, off:512], func=AF.Exp,
                                                 bias=negc[dc][:, 0:1],
                                                 waits=[ev_sc, ev_negc[dc], ev_pvd[b]])
                                    ev_m = ev_e
                                    if side == 0 and k >= i_lo:
                                        c0 = (k - i_lo) * 128
                                        ev_m = VE.op("tensor_mul", Pt[b][:, c0:c0 + 128],
                                                     Pt[b][:, c0:c0 + 128], maskd[:, :],
                                                     waits=[ev_e, ev_consts])
                                    elif side == 1 and k >= i_lo:
                                        c0 = (k - i_lo) * 128
                                        ev_m = VE.op("tensor_scalar", out=Pt[b][:, c0:c0 + 128],
                                                     in0=Pt[b][:, c0:c0 + 128],
                                                     scalar1=msel[:, k:k + 1], scalar2=None,
                                                     op0=ALU.mult, waits=[ev_e, ev_consts])
                                    for i in range(vlo, i_hi):
                                        li = i - i_lo
                                        w = [ev_m, ev_onescol, ev_proj[2], ev_proj[3]] if i == vlo else []
                                        TE.op("matmul", Ops[li][:, :],
                                              Pt[b][:, li * 128:(li + 1) * 128],
                                              Vh[:, (k if side == 0 else 16 + k), dc * 129:(dc + 1) * 129],
                                              start=(side == 0 and k == 0),
                                              stop=(side == 1 and k == i),
                                              inc=(i == i_hi - 1), waits=w)
                                    ev_pvd[b] = TE.ev()
                                    if ("att2" in dbg and hg == 0 and dc == 0 and qg == 0
                                            and side == 0 and k == 0):
                                        VE.op("tensor_copy", junk[:, 0:512], Pt[b][:, :],
                                              waits=[ev_m])
                                        d_dbg.start(YE, dbg_t["d_p0"][:, :], junk[:, 0:512],
                                                    waits=[VE.ev()])
                                ev_pv_all = TE.ev()
                                if "att2" in dbg and hg == 0 and dc == 0 and qg == 0:
                                    for li4 in range(4):
                                        VE.op("tensor_copy", junk2[:, li4 * 129:(li4 + 1) * 129],
                                              Ops[li4][:, :], waits=[ev_pv_all, d_dbg.ev()])
                                    VE.op("tensor_copy", junk2[:, 516:517], negc[0][:, :],
                                          waits=[ev_negc[0], ev_negc[1]])
                                    VE.op("tensor_copy", junk2[:, 517:518], negc[1][:, :])
                                    ev_dd = VE.ev()
                                    for li4 in range(4):
                                        d_dbg.start(YE, dbg_t["d_o0"][:, li4, :],
                                                    junk2[:, li4 * 129:(li4 + 1) * 129],
                                                    waits=[ev_dd])
                                    d_dbg.start(YE, dbg_t["d_nc"][:, :], junk2[:, 516:518],
                                                waits=[ev_dd])
                                    VE.op("memset", junk2[:, 0:1], 0.0, waits=[d_dbg.ev()])
                                for li in range(4):
                                    i = i_lo + li
                                    VE.op("reciprocal", st_ra[:, :], Ops[li][:, 128:129],
                                          waits=[ev_pv_all])
                                    ev_at = VE.op("tensor_scalar", out=attn_sb[:, :],
                                                  in0=Ops[li][:, 0:128],
                                                  scalar1=st_ra[:, 0:1], scalar2=None, op0=ALU.mult,
                                                  waits=[ev_att_st] if ev_att_st else [])
                                    ev_tr = TE.op("matmul", Sps[0][:, 0:128], attn_sb[:, :],
                                                  identb[:, :], start=True, stop=True,
                                                  waits=[ev_at, ev_exp[0], ev_exp[1]])
                                    ev_att_st = SE.op("copy", attnT[:, dc, i * 128:(i + 1) * 128],
                                                      Sps[0][:, 0:128], waits=[ev_tr])
                                    ev_exp[0] = ev_att_st
                                ev_o_cons = ev_att_st
                            d_atst.start(YE, s_attnT[h, :, :], attnT[:, dc, :], waits=[ev_att_st])
                            if "attn" in dbg:
                                VE.op("tensor_copy", junk[:, :], attnT[:, dc, :],
                                      waits=[ev_att_st, d_dbg.ev()])
                                d_dbg.start(YE, dbg_t["d_at"][h, :, :], junk[:, :], waits=[VE.ev()])

                barrier()

        if upto >= 'C':
            with ExitStack() as cctx:
                u2 = cctx.enter_context(nc.sbuf_tensor("u2", [128, NIC * D], BF16))
                u2_wo = u2.ap().rearrange("p (c n) -> p c n", n=D)
                u1c = cctx.enter_context(nc.sbuf_tensor("u1c", [128, NIC * D], BF16))
                u1_wq = u1c.ap().rearrange("p (c n) -> p c n", n=D)
                d_wqld = DB("dm_wqld")
                YE.wait(d_wqst.ev())
                ev_wq_ready = d_wqld.start(YE, u1c.ap()[:, :], s_wqT[:, :])
                atb_0 = cctx.enter_context(nc.sbuf_tensor("atb_0", [128, NIC, 128], BF16))
                atb = [atb_0, atb_0]
                y1s_0 = cctx.enter_context(nc.sbuf_tensor("y1s_0", [128, D], F32))
                y1s = [y1s_0, y1s_0]
                y2 = [cctx.enter_context(nc.sbuf_tensor(f"y2_{i}", [128, D], F32)) for i in range(2)]
                xq2b_0 = cctx.enter_context(nc.sbuf_tensor("xq2b_0", [128, D], BF16))
                xq2b = [xq2b_0, xq2b_0]
                st_stc = [cctx.enter_context(nc.sbuf_tensor(f"st_stc{i}", [128, 1], F32)) for i in range(2)]
                xiC = cctx.enter_context(nc.sbuf_tensor("xiC", [128, D], mybir.dt.int32))
                ln_quant.xi[0] = xiC
                ops_ps = [cctx.enter_context(nc.psum_tensor(f"ops_ps{i}", [128, D], F32)) for i in range(1)]
                ops_ps = [ops_ps[0], ops_ps[0]]
                fps = cctx.enter_context(nc.psum_tensor("fps", [128, D], F32))

                YE.wait(d_atst.ev())
                YE.wait(d_y1st[0].ev())
                YE.wait(d_y1st[1].ev())
                d_wcd.start(GE, u2_wo[:, :, :], woT.rearrange("(c p) o -> p c o", p=128))
                ev_wo_ld = d_wcd.ev()

                ev_at = [None, None]
                ev_atfree = [None, None]
                ev_y1l = [None, None]
                ev_y1lfree = [None, None]
                ev_y2_cons = None
                ev_y3free = [None, None]
                ev_xqT2_cons = None
                ev_ffn2_done = None
                s_at_v = s_attnT.rearrange("h p q -> p h q")
                evs = {"at": [None, None], "atfree": [None, None],
                       "y1l": [None, None], "y1lfree": [None, None],
                       "oproj": [None, None], "y2": [None, None],
                       "xq2": [None, None], "xqT2": None, "ffn2": None,
                       "y3": None, "y3free": None, "trans_done": [None, None]}

                def cd_dma(t):
                    b = t % 2
                    w = [evs["atfree"][1 - b]] if evs["atfree"][1 - b] else []
                    for half in range(2):
                        evs["at"][b] = d_atld[b].start(
                            YE, atb[b][:, half * 8:(half + 1) * 8, :],
                            s_at_v[:, half * 8:(half + 1) * 8, t * 128:(t + 1) * 128],
                            waits=w if half == 0 else [])

                def cd_dma_y1(t):
                    b = t % 2
                    evs["y1l"][b] = d_y1ld[0].start(
                        YE, y1s[b][:, :], s_y1own[t, :, :],
                        waits=[evs["y1lfree"][1 - b]] if evs["y1lfree"][1 - b] else [])

                def cd_oproj(t):
                    b = t % 2
                    for oc in range(4):
                        for ic in range(NIC):
                            w = []
                            if ic == 0 and oc == 0:
                                w = [evs["at"][b], ev_wo_ld]
                                if evs["trans_done"][1 - b] is not None:
                                    w.append(evs["trans_done"][1 - b])
                                if evs["y2"][1 - b] is not None:
                                    w.append(evs["y2"][1 - b])
                            TE.op("matmul", ops_ps[b][:, oc * 512:(oc + 1) * 512],
                                  atb[b][:, ic, :],
                                  u2_wo[:, ic, oc * 512:(oc + 1) * 512],
                                  start=(ic == 0), stop=(ic == NIC - 1),
                                  inc=(ic == NIC - 1 and oc == 3), waits=w)
                    evs["oproj"][b] = TE.ev()
                    evs["atfree"][b] = evs["oproj"][b]

                def cd_y2_quant(t):
                    b = t % 2
                    VE.op("scalar_tensor_tensor", out=y2[b][:, :], in0=ops_ps[b][:, :],
                          scalar=1.0, in1=y1s[b][:, :], op0=ALU.mult, op1=ALU.add,
                          waits=[evs["oproj"][b], evs["y1l"][b]] +
                                ([evs["xq2"][b]] if evs["xq2"][b] else []))
                    ev_y2 = VE.ev()
                    evs["y2"][b] = ev_y2
                    evs["y1lfree"][b] = ev_y2
                    ln_quant.qdbg[0] = (dbg_t["d_qd"] if ("qdbg" in dbg and t == 8) else None)
                    w = [ev_y2]
                    if evs["xqT2"] is not None:
                        w.append(evs["xqT2"])
                    for tb in range(2):
                        if evs["trans_done"][tb] is not None:
                            w.append(evs["trans_done"][tb])
                    if evs["y3free"] is not None:
                        w.append(evs["y3free"])
                    evs["xq2"][b] = ln_quant(y2[b][:, :], w, xq_out=xq2b[b], st_st_out=st_stc[b])
                    if "xq2" in dbg and t == 0:
                        ev_d1 = d_dbgG2[0].start(GE, dbg_t["d_xq2"][0, :, :], xq2b[b][:, :],
                                                 waits=[evs["xq2"][b]])
                        ln_quant.ge_guard[0] = ev_d1

                def cd_trans(t):
                    b = t % 2
                    ev_evac = None
                    for g in range(4):
                        pb = g % 2
                        for kk in range(4):
                            ic = g * 4 + kk
                            w = []
                            if kk == 0:
                                w = [evs["xq2"][b], ev_consts, evs["y2"][b]]
                                if g >= 2:
                                    w.append(ev_evac)
                            TE.op("matmul",
                                  ops_ps[b][:, pb * 512 + kk * 128: pb * 512 + (kk + 1) * 128],
                                  xq2b[b][:, ic * 128:(ic + 1) * 128], identb[:, :],
                                  start=True, stop=True, inc=(kk == 3), waits=w)
                        ev_mm = TE.ev()
                        ev_evac = SE.op(
                            "copy",
                            xqT.ap()[:, g * 4:(g + 1) * 4, :].rearrange("p c n -> p (c n)"),
                            ops_ps[b][:, pb * 512:(pb + 1) * 512], waits=[ev_mm])
                    evs["xqT2"] = ev_evac
                    evs["trans_done"][b] = ev_evac

                def cd_y3(t):
                    b = t % 2
                    VE.op("scalar_tensor_tensor", out=xm[:, :], in0=fps[:, :],
                          scalar=st_stc[b][:, 0:1], in1=y2[b][:, :], op0=ALU.mult, op1=ALU.add,
                          waits=[evs["ffn2"], d_dbg.ev()] +
                                ([evs["y3free"]] if evs["y3free"] else []))
                    ev_y3 = VE.ev()
                    evs["y3"] = ev_y3
                    evs["y3free"] = d_out.start(YE, y_out[t, :, :], xm[:, :], waits=[ev_y3])

                def cd_ffn2(t):
                    b = t % 2
                    for oc in range(4):
                        for ic in range(NIC):
                            w = []
                            if ic == 0 and oc == 0:
                                w = [evs["xqT2"], ev_wq_ready]
                                if evs["y3"] is not None:
                                    w.append(evs["y3"])
                            TE.op("matmul", fps[:, oc * 512:(oc + 1) * 512], xqT[:, ic, :],
                                  u1_wq[:, ic, oc * 512:(oc + 1) * 512],
                                  start=(ic == 0), stop=(ic == NIC - 1),
                                  inc=(ic == NIC - 1 and oc == 3), waits=w)
                    evs["ffn2"] = TE.ev()

                cd_dma(0)
                cd_dma_y1(0)
                cd_oproj(0)
                for t in range(OWN):
                    if t + 1 < OWN:
                        cd_dma(t + 1)
                    cd_y2_quant(t)
                    if t + 1 < OWN:
                        cd_dma_y1(t + 1)
                    cd_trans(t)
                    if t >= 1:
                        cd_y3(t - 1)
                    cd_ffn2(t)
                    if t + 1 < OWN:
                        cd_oproj(t + 1)
                cd_y3(OWN - 1)


        YE.wait(d_out.ev())
        if dbg:
            YE.wait(d_dbg.ev())
            for dd in d_dbgY:
                YE.wait(dd.ev())
            for dd in d_dbgG2:
                GE.wait(dd.ev())

        blk = ctx.enter_context(nc.Block())

        @blk.tensor
        def _(e):
            for f in TE.todo:
                f(e)

        @blk.vector
        def _(e):
            for f in VE.todo:
                f(e)

        @blk.scalar
        def _(e):
            for f in SE.todo:
                f(e)

        @blk.gpsimd
        def _(e):
            for f in GE.todo:
                f(e)

        @blk.sync
        def _(e):
            for f in YE.todo:
                f(e)

    return nc


def host_inputs(x, wq, wk, wv, wo, wf):
    x = np.asarray(x, dtype=np.float32)
    wts = {n: np.ascontiguousarray(np.asarray(w, dtype=np.float32).T)
           for n, w in (("wqT", wq), ("wkT", wk), ("wvT", wv), ("woT", wo), ("wfT", wf))}
    ident = np.eye(128, dtype=np.float32)
    kk, qq = np.meshgrid(np.arange(128), np.arange(128), indexing="ij")
    maskd = (kk <= qq).astype(np.float32)
    in_maps = []
    for c in range(8):
        b, j = c // 2, c % 2
        ow = own_tiles(j)
        ot = [t for t in range(NT) if t not in ow]
        perm = ow + ot
        xb = x[b].reshape(NT, 128, D)[perm].reshape(T, D)
        msel = np.zeros((128, OWN), dtype=np.float32)
        for i in range(OWN):
            msel[:, i] = 1.0 if ot[i] < ow[i] else 0.0
        schs = np.full((128, 1), np.float32(1.0 / np.sqrt(np.float64(128.0))),
                       dtype=np.float32)
        m = {"x0": np.ascontiguousarray(xb), "ident_in": ident, "maskd_in": maskd,
             "msel_in": msel, "schs_in": schs}
        m.update(wts)
        in_maps.append(m)
    return in_maps


def kernel(L0_attn_x, L0_attn_wq, L0_attn_wk, L0_attn_wv, L0_attn_wo, L0_ffn_weight):
    in_maps = host_inputs(L0_attn_x, L0_attn_wq, L0_attn_wk, L0_attn_wv,
                          L0_attn_wo, L0_ffn_weight)
    nc = build()
    res = run_bass_kernel_spmd(nc, in_maps, core_ids=list(range(8)))
    out = np.empty((4, T, D), dtype=np.float32)
    for c in range(8):
        b, j = c // 2, c % 2
        ow = own_tiles(j)
        shard = res.results[c]["y_out"]
        for i, t in enumerate(ow):
            out[b, t * 128:(t + 1) * 128, :] = shard[i]
    return out


# version 11
# speedup vs baseline: 1.6056x; 1.1731x over previous
import numpy as np
import concourse.bass as bass
import concourse.mybir as mybir
from concourse.bass_utils import run_bass_kernel_spmd
from contextlib import ExitStack

F32 = mybir.dt.float32
BF16 = mybir.dt.bfloat16
AF = mybir.ActivationFunctionType
ALU = mybir.AluOpType
AX = mybir.AxisListType

D = 2048
T = 4096
NT = 32
NIC = 16
H = 16
HS = 128
OWN = 16
NEG_SQRT_HS = -float(np.sqrt(np.float64(HS)))
INV_SQRT_HS = float(1.0 / np.sqrt(np.float64(HS)))
EPS = 1e-5
SH = 256.5
WMEAN_INV = 1.0 / (D * D)


def own_tiles(j):
    return [t for t in range(NT) if (0 if (t % 4) in (0, 3) else 1) == j]


class Ev:
    __slots__ = ("sem", "val")

    def __init__(self, sem, val):
        self.sem = sem
        self.val = val


class Eng:
    def __init__(self, sem, auto_serial):
        self.sem = sem
        self.cnt = 0
        self.auto = auto_serial
        self.waited = {}
        self.todo = []

    def wait(self, ev):
        if ev is None:
            return
        key = id(ev.sem)
        if self.waited.get(key, 0) >= ev.val:
            return
        self.waited[key] = ev.val
        sem, val = ev.sem, ev.val
        self.todo.append(lambda e: e.wait_ge(sem, val))

    def waits(self, evs):
        for ev in evs:
            self.wait(ev)

    def op(self, name, *args, inc=True, waits=(), **kw):
        self.waits(waits)
        if self.auto and self.cnt > 0:
            self.wait(Ev(self.sem, self.cnt))
        sem = self.sem
        if inc:
            self.cnt += 1

            def emit(e, name=name, args=args, kw=kw, sem=sem):
                getattr(e, name)(*args, **kw).then_inc(sem, 1)
        else:

            def emit(e, name=name, args=args, kw=kw):
                getattr(e, name)(*args, **kw)
        self.todo.append(emit)
        return Ev(self.sem, self.cnt) if inc else None

    def ev(self):
        return Ev(self.sem, self.cnt)


class DmaBuf:
    def __init__(self, nc, ctx, name):
        self.sem = ctx.enter_context(nc.semaphore(name))
        self.cum = 0

    def start(self, qeng, dst, src, waits=()):
        qeng.waits(waits)
        sem = self.sem
        self.cum += 16
        qeng.todo.append(lambda e, d=dst, s=src, sem=sem: e.dma_start(d, s).then_inc(sem, 16))
        return Ev(sem, self.cum)

    def ev(self):
        return Ev(self.sem, self.cum)


def build(dbg=(), upto='Z'):
    dbg = set(dbg)
    nc = bass.Bass("TRN2", target_bir_lowering=False)

    x0 = nc.dram_tensor("x0", [T, D], F32, kind="ExternalInput").ap()
    wqT = nc.dram_tensor("wqT", [D, D], F32, kind="ExternalInput").ap()
    wkT = nc.dram_tensor("wkT", [D, D], F32, kind="ExternalInput").ap()
    wvT = nc.dram_tensor("wvT", [D, D], F32, kind="ExternalInput").ap()
    woT = nc.dram_tensor("woT", [D, D], F32, kind="ExternalInput").ap()
    wfT = nc.dram_tensor("wfT", [D, D], F32, kind="ExternalInput").ap()
    ident_in = nc.dram_tensor("ident_in", [128, 128], F32, kind="ExternalInput").ap()
    maskd_in = nc.dram_tensor("maskd_in", [128, 128], F32, kind="ExternalInput").ap()
    msel_in = nc.dram_tensor("msel_in", [128, OWN], F32, kind="ExternalInput").ap()
    schs_in = nc.dram_tensor("schs_in", [128, 1], F32, kind="ExternalInput").ap()

    y_out = nc.dram_tensor("y_out", [OWN, 128, D], F32, kind="ExternalOutput").ap()
    dbg_t = {}
    if "y1" in dbg:
        dbg_t["d_y1"] = nc.dram_tensor("d_y1", [NT, 128, D], F32, kind="ExternalOutput").ap()
    if "y1ln" in dbg:
        dbg_t["d_y1ln"] = nc.dram_tensor("d_y1ln", [NT, 128, D], F32, kind="ExternalOutput").ap()
    if "kvq" in dbg:
        dbg_t["d_k"] = nc.dram_tensor("d_k", [128, 2, T], F32, kind="ExternalOutput").ap()
        dbg_t["d_v"] = nc.dram_tensor("d_v", [128, NT, 258], F32, kind="ExternalOutput").ap()
        dbg_t["d_q"] = nc.dram_tensor("d_q", [128, 2, OWN * 128], F32, kind="ExternalOutput").ap()
    if "attn" in dbg:
        dbg_t["d_at"] = nc.dram_tensor("d_at", [H, 128, OWN * 128], F32, kind="ExternalOutput").ap()
    if "y2" in dbg:
        dbg_t["d_y2"] = nc.dram_tensor("d_y2", [OWN, 128, D], F32, kind="ExternalOutput").ap()
    if "qdbg" in dbg:
        dbg_t["d_qd"] = nc.dram_tensor("d_qd", [4, 128, D], F32, kind="ExternalOutput").ap()
    if "xq2" in dbg:
        dbg_t["d_xq2"] = nc.dram_tensor("d_xq2", [OWN, 128, D], F32, kind="ExternalOutput").ap()
        dbg_t["d_st2"] = nc.dram_tensor("d_st2", [128, 2], F32, kind="ExternalOutput").ap()
    if "att2" in dbg:
        dbg_t["d_o0"] = nc.dram_tensor("d_o0", [128, 4, 129], F32, kind="ExternalOutput").ap()
        dbg_t["d_nc"] = nc.dram_tensor("d_nc", [128, 2], F32, kind="ExternalOutput").ap()
        dbg_t["d_p0"] = nc.dram_tensor("d_p0", [128, 512], F32, kind="ExternalOutput").ap()
        dbg_t["d_ch"] = nc.dram_tensor("d_ch", [128, 12], F32, kind="ExternalOutput").ap()

    s_y1ln = nc.dram_tensor("s_y1ln", [NT, 128, D], BF16, kind="Internal").ap()
    s_y1own = nc.dram_tensor("s_y1own", [OWN, 128, D], F32, kind="Internal").ap()
    s_ylT = nc.dram_tensor("s_ylT", [128, NIC, T], BF16, kind="Internal").ap()
    s_wqT = nc.dram_tensor("s_wqT", [128, NIC * D], BF16, kind="Internal").ap()
    s_attnT = nc.dram_tensor("s_attnT", [H, 128, OWN * 128], BF16, kind="Internal").ap()

    with ExitStack() as ctx:
        TE = Eng(ctx.enter_context(nc.semaphore("sem_t")), False)
        VE = Eng(ctx.enter_context(nc.semaphore("sem_v")), True)
        SE = Eng(ctx.enter_context(nc.semaphore("sem_s")), True)
        GE = Eng(ctx.enter_context(nc.semaphore("sem_g")), False)
        YE = Eng(ctx.enter_context(nc.semaphore("sem_y")), False)
        ALL = [TE, VE, SE, GE, YE]

        ALL_DMABUFS = []

        def barrier():
            evs = [g.ev() for g in ALL]
            for g in ALL:
                for ev in evs:
                    if ev.sem is not g.sem:
                        g.wait(ev)
                for db in ALL_DMABUFS:
                    if db.cum:
                        g.wait(db.ev())

        def DB(name):
            b = DmaBuf(nc, ctx, name)
            ALL_DMABUFS.append(b)
            return b

        d_x0 = [DB(f"dm_x{i}") for i in range(2)]
        d_wld = [DB(f"dm_w{i}") for i in range(2)]
        d_y1lnst = [DB(f"dm_ylst{i}") for i in range(2)]
        d_y1st = [DB(f"dm_y1st{i}") for i in range(2)]
        d_const = DB("dm_const")
        d_ylld = [DB(f"dm_ylld{i}") for i in range(2)]
        d_ylTst = [DB(f"dm_ylTst{i}") for i in range(2)]
        d_ylTld = [DB(f"dm_ylTld{i}") for i in range(2)]
        d_wsl = DB("dm_wsl")
        d_atst = DB("dm_atst")
        d_atld = [DB(f"dm_atld{i}") for i in range(2)]
        d_y1ld = [DB(f"dm_y1ld{i}") for i in range(2)]
        d_wcd = DB("dm_wcd")
        d_out = DB("dm_out")
        d_dbg = DB("dm_dbg")
        d_dbgY = [DB(f"dm_dbgY{i}") for i in range(2)]
        d_dbgG2 = [DB(f"dm_dbgG{i}") for i in range(2)]

        identb = ctx.enter_context(nc.sbuf_tensor("identb", [128, 128], BF16))
        identf = ctx.enter_context(nc.sbuf_tensor("identf", [128, 128], F32))
        maskd = ctx.enter_context(nc.sbuf_tensor("maskd", [128, 128], BF16))
        msel = ctx.enter_context(nc.sbuf_tensor("msel", [128, OWN], F32))
        ones_f = ctx.enter_context(nc.sbuf_tensor("ones_f", [128, 128], F32))
        pad_f = ctx.enter_context(nc.sbuf_tensor("pad_f", [128, 128], F32))

        xm = ctx.enter_context(nc.sbuf_tensor("xm", [128, D], F32))
        xqT = ctx.enter_context(nc.sbuf_tensor("xqT", [128, NIC, 128], BF16))
        junk = ctx.enter_context(nc.sbuf_tensor("junk", [128, D], F32))
        junk2 = ctx.enter_context(nc.sbuf_tensor("junk2", [128, D], F32))
        junk_s = ctx.enter_context(nc.sbuf_tensor("junk_s", [128, D], F32))

        def vec(name):
            return ctx.enter_context(nc.sbuf_tensor(name, [128, 1], F32))

        st_cs = vec("st_cs")
        st_sq = vec("st_sq")
        st_m = vec("st_m")
        st_msq = vec("st_msq")
        st_var = vec("st_var")
        st_sig = vec("st_sig")
        st_inv = vec("st_inv")
        st_am = vec("st_am")
        st_a = vec("st_a")
        st_ra = vec("st_ra")
        st_g = vec("st_g")
        st_st = vec("st_st")
        wsc = vec("wsc")
        wsc_inv = vec("wsc_inv")
        wscB = vec("wscB")
        st_mx = vec("st_mx")
        st_gm = vec("st_gm")
        negc = [vec("negc0"), vec("negc1")]
        wabs = ctx.enter_context(nc.sbuf_tensor("wabs", [128, NIC], F32))

        ev_c = None
        for dst, src in ((identb, ident_in), (identf, ident_in), (maskd, maskd_in),
                         (msel, msel_in)):
            ev_c = d_const.start(GE, dst[:, :], src[:, :])
        ev_consts = ev_c
        ev_ones = VE.op("memset", ones_f[:, :], 1.0)
        ones_b = ctx.enter_context(nc.sbuf_tensor("ones_b", [128, 128], BF16))
        ev_onesb = VE.op("memset", ones_b[:, :], 1.0)
        st_eps = vec("st_eps")
        ev_eps = VE.op("memset", st_eps[:, :], EPS)
        st_schs = vec("st_schs")
        st_r127 = vec("st_r127")
        schs_sb = ctx.enter_context(nc.sbuf_tensor("schs_sb", [128, 1], F32))
        ev_schs_ld = d_const.start(GE, schs_sb[:, :], schs_in[:, :])
        ev_consts = ev_schs_ld
        VE.op("tensor_copy", st_schs[:, :], schs_sb[:, :], waits=[ev_schs_ld])
        VE.op("memset", st_r127[:, :], 127.0)
        ev_r127 = VE.op("reciprocal", st_r127[:, :], st_r127[:, :])

        def ln_quant(src_ap, waits_src, xq_out=None, st_st_out=None):
            VE.op("reduce_sum", st_cs[:, :], src_ap, axis=AX.X, waits=waits_src)
            SE.op("activation", out=junk_s[:, :], in_=src_ap, func=AF.Square,
                  accum_out=st_sq[:, :], waits=list(waits_src) + [VE.ev()])
            ev_sq = SE.ev()
            VE.op("tensor_scalar", out=st_m[:, :], in0=st_cs[:, :], scalar1=1.0 / D,
                  scalar2=None, op0=ALU.mult)
            VE.op("tensor_scalar", out=xm[:, :], in0=src_ap, scalar1=st_m[:, 0:1],
                  scalar2=None, op0=ALU.subtract)
            VE.op("reduce_max", st_am[:, :], xm[:, :], axis=AX.X, apply_absolute_value=True)
            VE.op("tensor_mul", st_msq[:, :], st_m[:, :], st_m[:, :])
            VE.op("scalar_tensor_tensor", out=st_var[:, :], in0=st_sq[:, :], scalar=1.0 / D,
                  in1=st_msq[:, :], op0=ALU.mult, op1=ALU.subtract, waits=[ev_sq])
            ev_var = VE.ev()
            ev_sig = SE.op("activation", out=st_sig[:, :], in_=st_var[:, :], func=AF.Sqrt,
                           bias=st_eps[:, 0:1], waits=[ev_var, ev_eps])
            VE.op("reciprocal", st_inv[:, :], st_sig[:, :], waits=[ev_sig])
            VE.op("tensor_mul", st_a[:, :], st_am[:, :], st_inv[:, :])
            VE.op("tensor_scalar_max", st_a[:, :], st_a[:, :], 1e-8)
            VE.op("reciprocal", st_ra[:, :], st_a[:, :])
            VE.op("tensor_scalar", out=st_ra[:, :], in0=st_ra[:, :], scalar1=127.0,
                  scalar2=None, op0=ALU.mult)
            VE.op("tensor_mul", st_g[:, :], st_inv[:, :], st_ra[:, :])
            xi = ln_quant.xi[0]
            VE.op("tensor_scalar", out=junk[:, :], in0=xm[:, :], scalar1=st_g[:, 0:1],
                  scalar2=SH, op0=ALU.mult, op1=ALU.add)
            VE.op("tensor_copy", xi[:, :], junk[:, :])
            VE.op("tensor_copy", junk2[:, :], xi[:, :])
            if ln_quant.qdbg[0] is not None:
                qd = ln_quant.qdbg[0]
                d_dbg.start(YE, qd[0], junk[:, :], waits=[VE.ev()])
                d_dbg.start(YE, qd[1], junk2[:, :], waits=[VE.ev()])
            VE.op("tensor_tensor", xm[:, :], junk[:, :], junk2[:, :], op=ALU.subtract)
            if ln_quant.qdbg[0] is not None:
                qd = ln_quant.qdbg[0]
                d_dbg.start(YE, qd[2], xm[:, :], waits=[VE.ev()])
            VE.op("tensor_scalar", out=junk[:, :], in0=xm[:, :], scalar1=1.0,
                  scalar2=None, op0=ALU.is_ge, waits=[d_dbg.ev()])
            VE.op("scalar_tensor_tensor", out=junk[:, :], in0=xm[:, :], scalar=2.0,
                  in1=junk[:, :], op0=ALU.is_ge, op1=ALU.add)
            VE.op("scalar_tensor_tensor", out=junk[:, :], in0=xm[:, :], scalar=0.0,
                  in1=junk[:, :], op0=ALU.is_lt, op1=ALU.subtract)
            VE.op("scalar_tensor_tensor", out=junk[:, :], in0=xm[:, :], scalar=-1.0,
                  in1=junk[:, :], op0=ALU.is_lt, op1=ALU.add)
            VE.op("tensor_tensor", junk[:, :], junk2[:, :], junk[:, :], op=ALU.subtract)
            VE.op("tensor_scalar", out=junk[:, :], in0=junk[:, :], scalar1=383.0,
                  scalar2=129.0, op0=ALU.min, op1=ALU.max)
            xq_o = xq_out
            st_o = st_st if st_st_out is None else st_st_out
            ev_xq = VE.op("tensor_scalar", out=xq_o[:, :], in0=junk[:, :], scalar1=256.0,
                          scalar2=None, op0=ALU.subtract, waits=[ln_quant.ge_guard[0]])
            VE.op("tensor_mul", st_o[:, :], st_a[:, :], wscB[:, :])
            return ev_xq

        ln_quant.ge_guard = [None]
        ln_quant.xi = [None]
        ln_quant.qdbg = [None]

        tr_state = {0: None, 1: None}

        def transpose16(trps, src_fn, dst_fn, ev_src, ev_prev_cons):
            ev_out = None
            for g in range(4):
                b = g % 2
                for kk in range(4):
                    ic = g * 4 + kk
                    w = []
                    if kk == 0:
                        w = [ev_src, ev_consts, tr_state[b]]
                        if g == 0 and ev_prev_cons is not None:
                            w.append(ev_prev_cons)
                    TE.op("matmul", trps[b][:, kk * 128:(kk + 1) * 128], src_fn(ic),
                          identb[:, :], start=True, stop=True, inc=(kk == 3), waits=w)
                ev_mm = TE.ev()
                ev_out = SE.op("copy", dst_fn(g), trps[b][:, :], waits=[ev_mm])
                tr_state[b] = ev_out
            return ev_out

        def ln2_normalize(src_ap, dst_ap, ev_src, extra_waits=()):
            VE.op("reduce_sum", st_cs[:, :], src_ap, axis=AX.X, waits=[ev_src])
            SE.op("activation", out=junk_s[:, :], in_=src_ap, func=AF.Square,
                  accum_out=st_sq[:, :], waits=[ev_src, VE.ev()])
            ev_sq = SE.ev()
            VE.op("tensor_scalar", out=st_m[:, :], in0=st_cs[:, :], scalar1=1.0 / D,
                  scalar2=None, op0=ALU.mult)
            VE.op("tensor_mul", st_msq[:, :], st_m[:, :], st_m[:, :])
            VE.op("scalar_tensor_tensor", out=st_var[:, :], in0=st_sq[:, :], scalar=1.0 / D,
                  in1=st_msq[:, :], op0=ALU.mult, op1=ALU.subtract, waits=[ev_sq])
            ev_var = VE.ev()
            ev_sig = SE.op("activation", out=st_sig[:, :], in_=st_var[:, :], func=AF.Sqrt,
                           bias=st_eps[:, 0:1], waits=[ev_var, ev_eps])
            VE.op("reciprocal", st_inv[:, :], st_sig[:, :], waits=[ev_sig])
            return VE.op("tensor_scalar", out=dst_ap, in0=src_ap, scalar1=st_m[:, 0:1],
                         scalar2=st_inv[:, 0:1], op0=ALU.subtract, op1=ALU.mult,
                         waits=list(extra_waits))

        with ExitStack() as actx:
            wbuf = [actx.enter_context(nc.sbuf_tensor(f"wbuf{i}", [128, D], F32)) for i in range(2)]
            xbuf = [actx.enter_context(nc.sbuf_tensor(f"xbuf{i}", [128, D], F32)) for i in range(2)]
            y1b = [actx.enter_context(nc.sbuf_tensor(f"y1b{i}", [128, D], F32)) for i in range(2)]
            y1ln = [actx.enter_context(nc.sbuf_tensor(f"y1ln{i}", [128, D], BF16)) for i in range(2)]
            u1 = actx.enter_context(nc.sbuf_tensor("u1", [128, NIC * D], BF16))
            u1_wq = u1.ap().rearrange("p (c n) -> p c n", n=D)
            xiA = actx.enter_context(nc.sbuf_tensor("xiA", [128, D], mybir.dt.int32))
            xqb = [actx.enter_context(nc.sbuf_tensor(f"xqb{i}", [128, D], BF16)) for i in range(2)]
            xqTb = [actx.enter_context(nc.sbuf_tensor(f"xqTb{i}", [128, NIC, 128], BF16)) for i in range(2)]
            ylnT = [actx.enter_context(nc.sbuf_tensor(f"ylnT{i}", [128, NIC, 128], BF16)) for i in range(2)]
            st_stb = [actx.enter_context(nc.sbuf_tensor(f"st_stb{i}", [128, 1], F32)) for i in range(2)]
            ln_quant.xi[0] = xiA
            ypsum = actx.enter_context(nc.psum_tensor("ypsum", [128, D], F32))
            trpsA = [actx.enter_context(nc.psum_tensor(f"trpsA{i}", [128, 512], F32)) for i in range(2)]

            ev_w = [None, None]
            ev_wfree = [None, None]
            for k in range(NIC):
                b = k % 2
                ev_w[b] = d_wld[b].start(GE, wbuf[b][:, :], wfT[k * 128:(k + 1) * 128, :],
                                         waits=[ev_wfree[b]] if ev_wfree[b] else [])
                ev_wfree[b] = SE.op("activation", out=junk_s[:, :], in_=wbuf[b][:, :],
                                    func=AF.Abs, accum_out=wabs[:, k:k + 1], waits=[ev_w[b]])
            ev_ws = VE.op("reduce_sum", st_cs[:, :], wabs[:, :], axis=AX.X,
                          waits=[ev_wfree[0], ev_wfree[1]])
            VE.op("tensor_copy", st_mx[:, :], st_cs[:, :])
            ev_wsv = VE.ev()
            ev_wsum = TE.op("matmul", trpsA[0][:, 0:1], ones_f[:, :], st_mx[:, :],
                            start=True, stop=True, waits=[ev_ones, ev_wsv])
            VE.op("tensor_scalar", out=wsc[:, :], in0=trpsA[0][:, 0:1],
                  scalar1=WMEAN_INV, scalar2=None, op0=ALU.mult, waits=[ev_wsum])
            VE.op("tensor_scalar_max", wsc[:, :], wsc[:, :], 1e-6)
            ev_wsci = VE.op("reciprocal", wsc_inv[:, :], wsc[:, :])
            VE.op("tensor_scalar", out=wscB[:, :], in0=wsc[:, :],
                  scalar1=0.5, scalar2=None, op0=ALU.mult)
            ev_wscB = VE.op("tensor_mul", wscB[:, :], wscB[:, :], st_r127[:, 0:1],
                            waits=[ev_r127])
            ev_wsum_read = ev_wscB

            ev_w = [None, None]
            for k in range(NIC):
                b = k % 2
                ev_w[b] = d_wld[b].start(GE, wbuf[b][:, :], wfT[k * 128:(k + 1) * 128, :],
                                         waits=[ev_wfree[b]] if ev_wfree[b] else [])
                xi = ln_quant.xi[0]
                VE.op("tensor_scalar", out=junk[:, :], in0=wbuf[b][:, :],
                      scalar1=wsc_inv[:, 0:1], scalar2=SH, op0=ALU.mult, op1=ALU.add,
                      waits=[ev_w[b], ev_wsci])
                ev_wfree[b] = VE.ev()
                VE.op("tensor_copy", xi[:, :], junk[:, :])
                VE.op("tensor_copy", junk2[:, :], xi[:, :])
                VE.op("tensor_tensor", xm[:, :], junk[:, :], junk2[:, :], op=ALU.subtract)
                VE.op("tensor_scalar", out=junk[:, :], in0=xm[:, :], scalar1=1.0,
                      scalar2=None, op0=ALU.is_ge)
                VE.op("tensor_scalar", out=xm[:, :], in0=xm[:, :], scalar1=0.0,
                      scalar2=None, op0=ALU.is_lt)
                VE.op("tensor_tensor", junk[:, :], junk[:, :], xm[:, :], op=ALU.subtract)
                VE.op("tensor_tensor", junk[:, :], junk[:, :], junk2[:, :], op=ALU.add)
                VE.op("tensor_scalar", out=junk[:, :], in0=junk[:, :], scalar1=257.0,
                      scalar2=255.0, op0=ALU.min, op1=ALU.max)
                VE.op("tensor_scalar", out=u1_wq[:, k, :], in0=junk[:, :], scalar1=256.0,
                      scalar2=None, op0=ALU.subtract)
            ev_wq_ready = VE.ev()
            d_wqst = DB("dm_wqst")
            ev_wq_st = d_wqst.start(YE, s_wqT[:, :], u1.ap()[:, :], waits=[ev_wq_ready])
            if "wq" in dbg:
                for k in range(NIC):
                    VE.op("tensor_copy", junk[:, :], u1_wq[:, k, :], waits=[ev_wq_ready])
                    d_dbg.start(YE, dbg_t["d_wq"][:, k, :], junk[:, :], waits=[VE.ev()])

            ev_x = [None, None]
            ev_xfree = [None, None]
            ev_y1free = [[], []]
            ev_ylnfree = [[], []]
            ev_ffn = [None, None]
            ev_trans = [None, None]
            ev_ylnT_store = [None, None]
            ev_stt_prev = None
            ev_xq_e = [None, None]
            ev_xqT_e = [None, None]

            def quant_stage(t):
                b = t % 2
                ev_x[b] = d_x0[b].start(YE, xbuf[b][:, :], x0[t * 128:(t + 1) * 128, :],
                                        waits=[ev_xfree[b]] if ev_xfree[b] else [])
                w = [ev_x[b], ev_wscB]
                if ev_trans[b] is not None:
                    w.append(ev_trans[b])
                ev_xq_e[b] = ln_quant(xbuf[b][:, :], w, xq_out=xqb[b], st_st_out=st_stb[b])
                ev_xqT = transpose16(
                    trpsA, lambda ic, bb=b: xqb[bb][:, ic * 128:(ic + 1) * 128],
                    lambda g, bb=b: xqTb[bb].ap()[:, g * 4:(g + 1) * 4, :].rearrange("p c n -> p (c n)"),
                    ev_xq_e[b], ev_ffn[b])
                ev_trans[b] = TE.ev()
                ev_xqT_e[b] = ev_xqT

            def ffn_stage(t):
                b = t % 2
                for oc in range(4):
                    for ic in range(NIC):
                        w = []
                        if ic == 0 and oc == 0:
                            w = [ev_xqT_e[b], ev_wq_ready]
                            if ev_stt_prev is not None:
                                w.append(ev_stt_prev)
                        TE.op("matmul", ypsum[:, oc * 512:(oc + 1) * 512], xqTb[b][:, ic, :],
                              u1_wq[:, ic, oc * 512:(oc + 1) * 512],
                              start=(ic == 0), stop=(ic == NIC - 1),
                              inc=(ic == NIC - 1 and oc == 3), waits=w)
                ev_ffn[b] = TE.ev()

            def finish_stage(t):
                nonlocal ev_stt_prev
                b = t % 2
                yb = y1b[b]
                VE.op("scalar_tensor_tensor", out=yb[:, :], in0=ypsum[:, :],
                      scalar=st_stb[b][:, 0:1], in1=xbuf[b][:, :], op0=ALU.mult, op1=ALU.add,
                      waits=[ev_ffn[b]] + ev_y1free[b])
                ev_y1 = VE.ev()
                ev_stt_prev = ev_y1
                ev_xfree[b] = ev_y1
                ev_yln = ln2_normalize(yb[:, :], y1ln[b][:, :], ev_y1, ev_ylnfree[b])
                ev_tr = transpose16(
                    trpsA, lambda ic, bb=b: y1ln[bb][:, ic * 128:(ic + 1) * 128],
                    lambda g, bb=b: ylnT[bb].ap()[:, g * 4:(g + 1) * 4, :].rearrange("p c n -> p (c n)"),
                    ev_yln, ev_ylnT_store[b])
                for ic in range(NIC):
                    d_ylTst[b].start(YE, s_ylT[:, ic, t * 128:(t + 1) * 128], ylnT[b][:, ic, :],
                                     waits=[ev_tr] if ic == 0 else [])
                ev_ylnT_store[b] = d_ylTst[b].ev()
                ev_ylnfree[b] = [ev_tr]
                ev_y1free[b] = [ev_y1]
                if t < OWN:
                    ev_y1free[b].append(d_y1st[b].start(YE, s_y1own[t, :, :], yb[:, :],
                                                        waits=[ev_y1]))
                if "y1" in dbg:
                    ev_y1free[b].append(d_dbgY[b].start(YE, dbg_t["d_y1"][t, :, :], yb[:, :],
                                                        waits=[ev_y1]))
                if "y1ln" in dbg:
                    ev_ylnfree[b].append(d_dbgG2[b].start(GE, dbg_t["d_y1ln"][t, :, :],
                                                          y1ln[b][:, :], waits=[ev_yln]))

            quant_stage(0)
            ffn_stage(0)
            for t in range(1, NT):
                quant_stage(t)
                finish_stage(t - 1)
                ffn_stage(t)
            finish_stage(NT - 1)

            barrier()

        if upto >= 'B':
            with ExitStack() as bctx:
                Kt = bctx.enter_context(nc.sbuf_tensor("Kt", [128, 2, T], BF16))
                Vh = bctx.enter_context(nc.sbuf_tensor("Vh", [128, NT, 258], BF16))
                Qt = bctx.enter_context(nc.sbuf_tensor("Qt", [128, 2, OWN * 128], BF16))
                Pt = [bctx.enter_context(nc.sbuf_tensor(f"Pt{i}", [128, 512], BF16)) for i in range(2)]
                wk_sl = bctx.enter_context(nc.sbuf_tensor("wk_sl", [128, NIC, 256], BF16))
                wv_sl = bctx.enter_context(nc.sbuf_tensor("wv_sl", [128, NIC, 256], BF16))
                wq_sl = bctx.enter_context(nc.sbuf_tensor("wq_sl", [128, NIC, 256], BF16))
                ylstr = [bctx.enter_context(nc.sbuf_tensor(f"ylstr{i}", [128, NIC, 512], BF16)) for i in range(2)]
                attn_sb = bctx.enter_context(nc.sbuf_tensor("attn_sb", [128, 128], BF16))
                attnT = bctx.enter_context(nc.sbuf_tensor("attnT", [128, 2, OWN * 128], BF16))

                for hg in range(H // 2):
                    barrier()
                    SE.wait(d_atst.ev())
                    h0 = 2 * hg
                    d_wsl.start(GE, wk_sl[:, :, :],
                                wkT.rearrange("(c p) o -> p c o", p=128)[:, :, h0 * 128:(h0 + 2) * 128])
                    d_wsl.start(GE, wv_sl[:, :, :],
                                wvT.rearrange("(c p) o -> p c o", p=128)[:, :, h0 * 128:(h0 + 2) * 128])
                    d_wsl.start(GE, wq_sl[:, :, :],
                                wqT.rearrange("(c p) o -> p c o", p=128)[:, :, h0 * 128:(h0 + 2) * 128])
                    ev_wall = d_wsl.ev()
                    VE.op("memset", Vh.ap()[:, :, 128:129], 1.0)
                    ev_onescol = VE.op("memset", Vh.ap()[:, :, 257:258], 1.0)

                    with ExitStack() as pctx:
                        psK = [pctx.enter_context(nc.psum_tensor(f"psK{hg}_{i}", [128, 512], F32)) for i in range(2)]
                        psV = [pctx.enter_context(nc.psum_tensor(f"psV{hg}_{i}", [128, 256], F32)) for i in range(2)]
                        psQ = pctx.enter_context(nc.psum_tensor(f"psQ{hg}", [128, 512], F32))
                        ev_str = [None, None]
                        ev_strfree = [None, None]
                        evK = {0: None, 1: None}
                        evV = {0: None, 1: None}
                        evQ = None
                        for t5 in range(8):
                            b = t5 % 2
                            ev_str[b] = d_ylTld[b].start(YE, ylstr[b][:, :, :],
                                                         s_ylT[:, :, t5 * 512:(t5 + 1) * 512],
                                                         waits=[ev_strfree[b]] if ev_strfree[b] else [])
                            for dc in range(2):
                                for ic in range(NIC):
                                    w = [ev_wall, ev_str[b], evK[dc]] if ic == 0 else []
                                    TE.op("matmul", psK[dc][:, :], wk_sl[:, ic, dc * 128:(dc + 1) * 128],
                                          ylstr[b][:, ic, :], start=(ic == 0), stop=(ic == NIC - 1),
                                          inc=(ic == NIC - 1), waits=w)
                                evK[dc] = SE.op("copy", Kt[:, dc, t5 * 512:(t5 + 1) * 512],
                                                psK[dc][:, :], waits=[TE.ev()])
                            for tc in range(4):
                                kt = t5 * 4 + tc
                                pb = tc % 2
                                for ic in range(NIC):
                                    w = [ev_wall, ev_str[b], evV[pb]] if ic == 0 else []
                                    TE.op("matmul", psV[pb][:, :],
                                          ylstr[b][:, ic, tc * 128:(tc + 1) * 128],
                                          wv_sl[:, ic, :], start=(ic == 0), stop=(ic == NIC - 1),
                                          inc=(ic == NIC - 1), waits=w)
                                ev_mm = TE.ev()
                                SE.op("copy", Vh[:, kt, 0:128], psV[pb][:, 0:128], waits=[ev_mm])
                                evV[pb] = SE.op("copy", Vh[:, kt, 129:257], psV[pb][:, 128:256])
                            if t5 < 4:
                                for dc in range(2):
                                    for ic in range(NIC):
                                        w = [ev_wall, ev_str[b], evQ] if ic == 0 else []
                                        TE.op("matmul", psQ[:, :], wq_sl[:, ic, dc * 128:(dc + 1) * 128],
                                              ylstr[b][:, ic, :], start=(ic == 0), stop=(ic == NIC - 1),
                                              inc=(ic == NIC - 1), waits=w)
                                    evQ = SE.op("activation",
                                                out=Qt[:, dc, t5 * 512:(t5 + 1) * 512],
                                                in_=psQ[:, :], func=AF.Copy,
                                                scale=st_schs[:, 0:1], waits=[TE.ev()])
                            ev_strfree[b] = TE.ev()
                        ev_proj = [evK[0], evK[1], evV[0], evV[1], evQ]

                        ev_negc = [None, None]
                        ev_sqr = None
                        for dc in range(2):
                            chsc = ylstr[0].ap().rearrange("p c n -> p (c n)")[:, 0:D]
                            ev_sq = SE.op("activation", out=chsc, in_=Qt[:, dc, :],
                                          func=AF.Square,
                                          waits=ev_proj + [ev_negc[0], ev_sqr, TE.ev()])
                            for c4 in range(4):
                                ev_mm = TE.op("matmul", psQ[:, :], ones_b[:, :],
                                              chsc[:, c4 * 512:(c4 + 1) * 512], start=True,
                                              stop=True, waits=[ev_sq, ev_onesb, evQ,
                                                                ev_negc[0], ev_negc[1],
                                                                VE.ev()])
                                VE.op("reduce_max", st_gm[:, :], psQ[:, :], axis=AX.X,
                                      waits=[ev_mm])
                                if c4 == 0:
                                    VE.op("tensor_copy", st_am[:, :], st_gm[:, :])
                                else:
                                    VE.op("tensor_max", st_am[:, :], st_am[:, :], st_gm[:, :])
                            ev_q2 = VE.ev()
                            ev_first = None
                            for half in range(2):
                                ev_sqr = SE.op("activation", out=chsc,
                                               in_=Kt[:, dc, half * D:(half + 1) * D],
                                               func=AF.Square, waits=[ev_q2, TE.ev()])
                                for c4 in range(4):
                                    ev_mm = TE.op("matmul", psQ[:, :], ones_b[:, :],
                                                  chsc[:, c4 * 512:(c4 + 1) * 512], start=True,
                                                  stop=True, waits=[ev_sqr, VE.ev()])
                                    VE.op("reduce_max", st_gm[:, :], psQ[:, :], axis=AX.X,
                                          waits=[ev_mm])
                                    if half == 0 and c4 == 0:
                                        VE.op("tensor_copy", st_mx[:, :], st_gm[:, :])
                                    else:
                                        VE.op("tensor_max", st_mx[:, :], st_mx[:, :], st_gm[:, :])
                            VE.op("tensor_mul", st_gm[:, :], st_am[:, :], st_mx[:, :])
                            ev_m2 = VE.ev()
                            ev_rt = SE.op("activation", out=st_gm[:, :], in_=st_gm[:, :],
                                          func=AF.Sqrt, waits=[ev_m2])
                            ev_negc[dc] = VE.op("tensor_scalar", out=negc[dc][:, :],
                                                in0=st_gm[:, :], scalar1=-1.0,
                                                scalar2=None, op0=ALU.mult, waits=[ev_rt])
                            if "att2" in dbg and hg == 0 and dc == 0:
                                VE.op("tensor_copy", junk[:, 0:1], st_am[:, :])
                                VE.op("tensor_copy", junk[:, 1:2], st_mx[:, :])
                                VE.op("tensor_copy", junk[:, 2:3], negc[0][:, :])
                                d_dbg.start(YE, dbg_t["d_ch"][:, :], junk[:, 0:12], waits=[VE.ev()])
                                VE.op("memset", junk[:, 0:1], 0.0, waits=[d_dbg.ev()])

                    if "kvq" in dbg and hg == 0:
                        ev_d = None
                        for dc in range(2):
                            for half in range(2):
                                VE.op("tensor_copy", junk[:, :], Kt[:, dc, half * D:(half + 1) * D],
                                      waits=ev_proj + [ev_d])
                                ev_d = d_dbg.start(YE, dbg_t["d_k"][:, dc, half * D:(half + 1) * D],
                                                   junk[:, :], waits=[VE.ev()])
                            VE.op("tensor_copy", junk[:, :], Qt[:, dc, :], waits=ev_proj + [ev_d])
                            ev_d = d_dbg.start(YE, dbg_t["d_q"][:, dc, :], junk[:, :], waits=[VE.ev()])
                        for kt in range(NT):
                            VE.op("tensor_copy", junk[:, 0:258], Vh[:, kt, :], waits=ev_proj + [ev_d])
                            ev_d = d_dbg.start(YE, dbg_t["d_v"][:, kt, :], junk[:, 0:258], waits=[VE.ev()])
                        VE.op("memset", junk[:, 0:1], 0.0, waits=[ev_d])

                    with ExitStack() as actx2:
                        Sps = [actx2.enter_context(nc.psum_tensor(f"Sps{hg}_{i}", [128, 512], F32)) for i in range(2)]
                        Ops = [actx2.enter_context(nc.psum_tensor(f"Ops{hg}_{i}", [128, 129], F32)) for i in range(4)]
                        ev_exp = {0: None, 1: None}
                        ev_pvd = {0: None, 1: None}
                        ev_o_cons = None
                        ev_att_st = None
                        for dc in range(2):
                            h = h0 + dc
                            for qg in range(4):
                                i_lo = qg * 4
                                i_hi = i_lo + 4
                                kts = [(0, k) for k in range(i_hi)] + [(1, k) for k in range(i_hi)]
                                m_ev = {}

                                def sc_stage(it, i_lo=i_lo, dc=dc):
                                    side, k = kts[it]
                                    vlo = max(i_lo, k)
                                    off = (vlo - i_lo) * 128
                                    b = it % 2
                                    kcol = (k if side == 0 else 16 + k) * 128
                                    first = [ev_proj[0], ev_proj[1], ev_proj[4],
                                             ev_negc[0], ev_negc[1]]
                                    if ev_exp[b] is not None:
                                        first.append(ev_exp[b])
                                    if ev_o_cons is not None:
                                        first.append(ev_o_cons)
                                    ev_sc = TE.op("matmul", Sps[b][:, off:512],
                                                  Kt[:, dc, kcol:kcol + 128],
                                                  Qt[:, dc, i_lo * 128 + off: (i_lo + 4) * 128],
                                                  start=True, stop=True, waits=first)
                                    ev_e = SE.op("activation", out=Pt[b][:, off:512],
                                                 in_=Sps[b][:, off:512], func=AF.Exp,
                                                 bias=negc[dc][:, 0:1],
                                                 waits=[ev_sc, ev_negc[dc], ev_pvd[b]])
                                    ev_m = ev_e
                                    if side == 0 and k >= i_lo:
                                        c0 = (k - i_lo) * 128
                                        ev_m = VE.op("tensor_mul", Pt[b][:, c0:c0 + 128],
                                                     Pt[b][:, c0:c0 + 128], maskd[:, :],
                                                     waits=[ev_e, ev_consts])
                                    elif side == 1 and k >= i_lo:
                                        c0 = (k - i_lo) * 128
                                        ev_m = VE.op("tensor_scalar", out=Pt[b][:, c0:c0 + 128],
                                                     in0=Pt[b][:, c0:c0 + 128],
                                                     scalar1=msel[:, k:k + 1], scalar2=None,
                                                     op0=ALU.mult, waits=[ev_e, ev_consts])
                                    m_ev[it] = ev_m

                                def pv_stage(it, i_lo=i_lo, i_hi=i_hi, dc=dc):
                                    side, k = kts[it]
                                    vlo = max(i_lo, k)
                                    b = it % 2
                                    for i in range(vlo, i_hi):
                                        li = i - i_lo
                                        w = [m_ev[it], ev_onescol, ev_proj[2], ev_proj[3]] if i == vlo else []
                                        TE.op("matmul", Ops[li][:, :],
                                              Pt[b][:, li * 128:(li + 1) * 128],
                                              Vh[:, (k if side == 0 else 16 + k), dc * 129:(dc + 1) * 129],
                                              start=(side == 0 and k == 0),
                                              stop=(side == 1 and k == i),
                                              inc=(i == i_hi - 1), waits=w)
                                    ev_pvd[b] = TE.ev()
                                    if ("att2" in dbg and hg == 0 and dc == 0 and qg == 0
                                            and side == 0 and k == 0):
                                        VE.op("tensor_copy", junk[:, 0:512], Pt[b][:, :],
                                              waits=[m_ev[it]])
                                        d_dbg.start(YE, dbg_t["d_p0"][:, :], junk[:, 0:512],
                                                    waits=[VE.ev()])

                                sc_stage(0)
                                for it in range(len(kts)):
                                    if it + 1 < len(kts):
                                        sc_stage(it + 1)
                                    pv_stage(it)
                                ev_pv_all = TE.ev()
                                if "att2" in dbg and hg == 0 and dc == 0 and qg == 0:
                                    for li4 in range(4):
                                        VE.op("tensor_copy", junk2[:, li4 * 129:(li4 + 1) * 129],
                                              Ops[li4][:, :], waits=[ev_pv_all, d_dbg.ev()])
                                    VE.op("tensor_copy", junk2[:, 516:517], negc[0][:, :],
                                          waits=[ev_negc[0], ev_negc[1]])
                                    VE.op("tensor_copy", junk2[:, 517:518], negc[1][:, :])
                                    ev_dd = VE.ev()
                                    for li4 in range(4):
                                        d_dbg.start(YE, dbg_t["d_o0"][:, li4, :],
                                                    junk2[:, li4 * 129:(li4 + 1) * 129],
                                                    waits=[ev_dd])
                                    d_dbg.start(YE, dbg_t["d_nc"][:, :], junk2[:, 516:518],
                                                waits=[ev_dd])
                                    VE.op("memset", junk2[:, 0:1], 0.0, waits=[d_dbg.ev()])
                                for li in range(4):
                                    i = i_lo + li
                                    VE.op("reciprocal", st_ra[:, :], Ops[li][:, 128:129],
                                          waits=[ev_pv_all])
                                    ev_at = VE.op("tensor_scalar", out=attn_sb[:, :],
                                                  in0=Ops[li][:, 0:128],
                                                  scalar1=st_ra[:, 0:1], scalar2=None, op0=ALU.mult,
                                                  waits=[ev_att_st] if ev_att_st else [])
                                    ev_tr = TE.op("matmul", Sps[0][:, 0:128], attn_sb[:, :],
                                                  identb[:, :], start=True, stop=True,
                                                  waits=[ev_at, ev_exp[0], ev_exp[1]])
                                    ev_att_st = SE.op("copy", attnT[:, dc, i * 128:(i + 1) * 128],
                                                      Sps[0][:, 0:128], waits=[ev_tr])
                                    ev_exp[0] = ev_att_st
                                ev_o_cons = ev_att_st
                            d_atst.start(YE, s_attnT[h, :, :], attnT[:, dc, :], waits=[ev_att_st])
                            if "attn" in dbg:
                                VE.op("tensor_copy", junk[:, :], attnT[:, dc, :],
                                      waits=[ev_att_st, d_dbg.ev()])
                                d_dbg.start(YE, dbg_t["d_at"][h, :, :], junk[:, :], waits=[VE.ev()])

                barrier()

        if upto >= 'C':
            with ExitStack() as cctx:
                u2 = cctx.enter_context(nc.sbuf_tensor("u2", [128, NIC * D], BF16))
                u2_wo = u2.ap().rearrange("p (c n) -> p c n", n=D)
                u1c = cctx.enter_context(nc.sbuf_tensor("u1c", [128, NIC * D], BF16))
                u1_wq = u1c.ap().rearrange("p (c n) -> p c n", n=D)
                d_wqld = DB("dm_wqld")
                YE.wait(d_wqst.ev())
                ev_wq_ready = d_wqld.start(YE, u1c.ap()[:, :], s_wqT[:, :])
                atb_0 = cctx.enter_context(nc.sbuf_tensor("atb_0", [128, NIC, 128], BF16))
                atb = [atb_0, atb_0]
                y1s_0 = cctx.enter_context(nc.sbuf_tensor("y1s_0", [128, D], F32))
                y1s = [y1s_0, y1s_0]
                y2 = [cctx.enter_context(nc.sbuf_tensor(f"y2_{i}", [128, D], F32)) for i in range(2)]
                xq2b_0 = cctx.enter_context(nc.sbuf_tensor("xq2b_0", [128, D], BF16))
                xq2b = [xq2b_0, xq2b_0]
                st_stc = [cctx.enter_context(nc.sbuf_tensor(f"st_stc{i}", [128, 1], F32)) for i in range(2)]
                xiC = cctx.enter_context(nc.sbuf_tensor("xiC", [128, D], mybir.dt.int32))
                ln_quant.xi[0] = xiC
                ops_ps = [cctx.enter_context(nc.psum_tensor(f"ops_ps{i}", [128, D], F32)) for i in range(1)]
                ops_ps = [ops_ps[0], ops_ps[0]]
                fps = cctx.enter_context(nc.psum_tensor("fps", [128, D], F32))

                YE.wait(d_atst.ev())
                YE.wait(d_y1st[0].ev())
                YE.wait(d_y1st[1].ev())
                d_wcd.start(GE, u2_wo[:, :, :], woT.rearrange("(c p) o -> p c o", p=128))
                ev_wo_ld = d_wcd.ev()

                ev_at = [None, None]
                ev_atfree = [None, None]
                ev_y1l = [None, None]
                ev_y1lfree = [None, None]
                ev_y2_cons = None
                ev_y3free = [None, None]
                ev_xqT2_cons = None
                ev_ffn2_done = None
                s_at_v = s_attnT.rearrange("h p q -> p h q")
                evs = {"at": [None, None], "atfree": [None, None],
                       "y1l": [None, None], "y1lfree": [None, None],
                       "oproj": [None, None], "y2": [None, None],
                       "xq2": [None, None], "xqT2": None, "ffn2": None,
                       "y3": None, "y3free": None, "trans_done": [None, None]}

                def cd_dma(t):
                    b = t % 2
                    w = [evs["atfree"][1 - b]] if evs["atfree"][1 - b] else []
                    for half in range(2):
                        evs["at"][b] = d_atld[b].start(
                            YE, atb[b][:, half * 8:(half + 1) * 8, :],
                            s_at_v[:, half * 8:(half + 1) * 8, t * 128:(t + 1) * 128],
                            waits=w if half == 0 else [])

                def cd_dma_y1(t):
                    b = t % 2
                    evs["y1l"][b] = d_y1ld[0].start(
                        YE, y1s[b][:, :], s_y1own[t, :, :],
                        waits=[evs["y1lfree"][1 - b]] if evs["y1lfree"][1 - b] else [])

                def cd_oproj(t):
                    b = t % 2
                    for oc in range(4):
                        for ic in range(NIC):
                            w = []
                            if ic == 0 and oc == 0:
                                w = [evs["at"][b], ev_wo_ld]
                                if evs["trans_done"][1 - b] is not None:
                                    w.append(evs["trans_done"][1 - b])
                                if evs["y2"][1 - b] is not None:
                                    w.append(evs["y2"][1 - b])
                            TE.op("matmul", ops_ps[b][:, oc * 512:(oc + 1) * 512],
                                  atb[b][:, ic, :],
                                  u2_wo[:, ic, oc * 512:(oc + 1) * 512],
                                  start=(ic == 0), stop=(ic == NIC - 1),
                                  inc=(ic == NIC - 1 and oc == 3), waits=w)
                    evs["oproj"][b] = TE.ev()
                    evs["atfree"][b] = evs["oproj"][b]

                def cd_y2_quant(t):
                    b = t % 2
                    VE.op("scalar_tensor_tensor", out=y2[b][:, :], in0=ops_ps[b][:, :],
                          scalar=1.0, in1=y1s[b][:, :], op0=ALU.mult, op1=ALU.add,
                          waits=[evs["oproj"][b], evs["y1l"][b]] +
                                ([evs["xq2"][b]] if evs["xq2"][b] else []))
                    ev_y2 = VE.ev()
                    evs["y2"][b] = ev_y2
                    evs["y1lfree"][b] = ev_y2
                    ln_quant.qdbg[0] = (dbg_t["d_qd"] if ("qdbg" in dbg and t == 8) else None)
                    w = [ev_y2]
                    if evs["xqT2"] is not None:
                        w.append(evs["xqT2"])
                    for tb in range(2):
                        if evs["trans_done"][tb] is not None:
                            w.append(evs["trans_done"][tb])
                    if evs["y3free"] is not None:
                        w.append(evs["y3free"])
                    evs["xq2"][b] = ln_quant(y2[b][:, :], w, xq_out=xq2b[b], st_st_out=st_stc[b])
                    if "xq2" in dbg and t == 0:
                        ev_d1 = d_dbgG2[0].start(GE, dbg_t["d_xq2"][0, :, :], xq2b[b][:, :],
                                                 waits=[evs["xq2"][b]])
                        ln_quant.ge_guard[0] = ev_d1

                def cd_trans(t):
                    b = t % 2
                    ev_evac = None
                    for g in range(4):
                        pb = g % 2
                        for kk in range(4):
                            ic = g * 4 + kk
                            w = []
                            if kk == 0:
                                w = [evs["xq2"][b], ev_consts, evs["y2"][b]]
                                if g >= 2:
                                    w.append(ev_evac)
                            TE.op("matmul",
                                  ops_ps[b][:, pb * 512 + kk * 128: pb * 512 + (kk + 1) * 128],
                                  xq2b[b][:, ic * 128:(ic + 1) * 128], identb[:, :],
                                  start=True, stop=True, inc=(kk == 3), waits=w)
                        ev_mm = TE.ev()
                        ev_evac = SE.op(
                            "copy",
                            xqT.ap()[:, g * 4:(g + 1) * 4, :].rearrange("p c n -> p (c n)"),
                            ops_ps[b][:, pb * 512:(pb + 1) * 512], waits=[ev_mm])
                    evs["xqT2"] = ev_evac
                    evs["trans_done"][b] = ev_evac

                def cd_y3(t):
                    b = t % 2
                    VE.op("scalar_tensor_tensor", out=xm[:, :], in0=fps[:, :],
                          scalar=st_stc[b][:, 0:1], in1=y2[b][:, :], op0=ALU.mult, op1=ALU.add,
                          waits=[evs["ffn2"], d_dbg.ev()] +
                                ([evs["y3free"]] if evs["y3free"] else []))
                    ev_y3 = VE.ev()
                    evs["y3"] = ev_y3
                    evs["y3free"] = d_out.start(YE, y_out[t, :, :], xm[:, :], waits=[ev_y3])

                def cd_ffn2(t):
                    b = t % 2
                    for oc in range(4):
                        for ic in range(NIC):
                            w = []
                            if ic == 0 and oc == 0:
                                w = [evs["xqT2"], ev_wq_ready]
                                if evs["y3"] is not None:
                                    w.append(evs["y3"])
                            TE.op("matmul", fps[:, oc * 512:(oc + 1) * 512], xqT[:, ic, :],
                                  u1_wq[:, ic, oc * 512:(oc + 1) * 512],
                                  start=(ic == 0), stop=(ic == NIC - 1),
                                  inc=(ic == NIC - 1 and oc == 3), waits=w)
                    evs["ffn2"] = TE.ev()

                cd_dma(0)
                cd_dma_y1(0)
                cd_oproj(0)
                for t in range(OWN):
                    if t + 1 < OWN:
                        cd_dma(t + 1)
                    cd_y2_quant(t)
                    if t + 1 < OWN:
                        cd_dma_y1(t + 1)
                    cd_trans(t)
                    if t >= 1:
                        cd_y3(t - 1)
                    cd_ffn2(t)
                    if t + 1 < OWN:
                        cd_oproj(t + 1)
                cd_y3(OWN - 1)


        YE.wait(d_out.ev())
        if dbg:
            YE.wait(d_dbg.ev())
            for dd in d_dbgY:
                YE.wait(dd.ev())
            for dd in d_dbgG2:
                GE.wait(dd.ev())

        blk = ctx.enter_context(nc.Block())

        @blk.tensor
        def _(e):
            for f in TE.todo:
                f(e)

        @blk.vector
        def _(e):
            for f in VE.todo:
                f(e)

        @blk.scalar
        def _(e):
            for f in SE.todo:
                f(e)

        @blk.gpsimd
        def _(e):
            for f in GE.todo:
                f(e)

        @blk.sync
        def _(e):
            for f in YE.todo:
                f(e)

    return nc


def host_inputs(x, wq, wk, wv, wo, wf):
    x = np.asarray(x, dtype=np.float32)
    wts = {n: np.ascontiguousarray(np.asarray(w, dtype=np.float32).T)
           for n, w in (("wqT", wq), ("wkT", wk), ("wvT", wv), ("woT", wo), ("wfT", wf))}
    ident = np.eye(128, dtype=np.float32)
    kk, qq = np.meshgrid(np.arange(128), np.arange(128), indexing="ij")
    maskd = (kk <= qq).astype(np.float32)
    in_maps = []
    for c in range(8):
        b, j = c // 2, c % 2
        ow = own_tiles(j)
        ot = [t for t in range(NT) if t not in ow]
        perm = ow + ot
        xb = x[b].reshape(NT, 128, D)[perm].reshape(T, D)
        msel = np.zeros((128, OWN), dtype=np.float32)
        for i in range(OWN):
            msel[:, i] = 1.0 if ot[i] < ow[i] else 0.0
        schs = np.full((128, 1), np.float32(1.0 / np.sqrt(np.float64(128.0))),
                       dtype=np.float32)
        m = {"x0": np.ascontiguousarray(xb), "ident_in": ident, "maskd_in": maskd,
             "msel_in": msel, "schs_in": schs}
        m.update(wts)
        in_maps.append(m)
    return in_maps


def kernel(L0_attn_x, L0_attn_wq, L0_attn_wk, L0_attn_wv, L0_attn_wo, L0_ffn_weight):
    in_maps = host_inputs(L0_attn_x, L0_attn_wq, L0_attn_wk, L0_attn_wv,
                          L0_attn_wo, L0_ffn_weight)
    nc = build()
    res = run_bass_kernel_spmd(nc, in_maps, core_ids=list(range(8)))
    out = np.empty((4, T, D), dtype=np.float32)
    for c in range(8):
        b, j = c // 2, c % 2
        ow = own_tiles(j)
        shard = res.results[c]["y_out"]
        for i, t in enumerate(ow):
            out[b, t * 128:(t + 1) * 128, :] = shard[i]
    return out


# version 12 (final)
# speedup vs baseline: 1.6630x; 1.0358x over previous
import numpy as np
import concourse.bass as bass
import concourse.mybir as mybir
from concourse.bass_utils import run_bass_kernel_spmd
from contextlib import ExitStack

F32 = mybir.dt.float32
BF16 = mybir.dt.bfloat16
AF = mybir.ActivationFunctionType
ALU = mybir.AluOpType
AX = mybir.AxisListType

D = 2048
T = 4096
NT = 32
NIC = 16
H = 16
HS = 128
OWN = 16
NEG_SQRT_HS = -float(np.sqrt(np.float64(HS)))
INV_SQRT_HS = float(1.0 / np.sqrt(np.float64(HS)))
EPS = 1e-5
SH = 256.5
WMEAN_INV = 1.0 / (D * D)


def own_tiles(j):
    return [t for t in range(NT) if (0 if (t % 4) in (0, 3) else 1) == j]


class Ev:
    __slots__ = ("sem", "val")

    def __init__(self, sem, val):
        self.sem = sem
        self.val = val


class Eng:
    def __init__(self, sem, auto_serial):
        self.sem = sem
        self.cnt = 0
        self.auto = auto_serial
        self.waited = {}
        self.todo = []

    def wait(self, ev):
        if ev is None:
            return
        key = id(ev.sem)
        if self.waited.get(key, 0) >= ev.val:
            return
        self.waited[key] = ev.val
        sem, val = ev.sem, ev.val
        self.todo.append(lambda e: e.wait_ge(sem, val))

    def waits(self, evs):
        for ev in evs:
            self.wait(ev)

    def op(self, name, *args, inc=True, waits=(), **kw):
        self.waits(waits)
        if self.auto and self.cnt > 0:
            self.wait(Ev(self.sem, self.cnt))
        sem = self.sem
        if inc:
            self.cnt += 1

            def emit(e, name=name, args=args, kw=kw, sem=sem):
                getattr(e, name)(*args, **kw).then_inc(sem, 1)
        else:

            def emit(e, name=name, args=args, kw=kw):
                getattr(e, name)(*args, **kw)
        self.todo.append(emit)
        return Ev(self.sem, self.cnt) if inc else None

    def ev(self):
        return Ev(self.sem, self.cnt)


class DmaBuf:
    def __init__(self, nc, ctx, name):
        self.sem = ctx.enter_context(nc.semaphore(name))
        self.cum = 0

    def start(self, qeng, dst, src, waits=()):
        qeng.waits(waits)
        sem = self.sem
        self.cum += 16
        qeng.todo.append(lambda e, d=dst, s=src, sem=sem: e.dma_start(d, s).then_inc(sem, 16))
        return Ev(sem, self.cum)

    def ev(self):
        return Ev(self.sem, self.cum)


def build(dbg=(), upto='Z'):
    dbg = set(dbg)
    nc = bass.Bass("TRN2", target_bir_lowering=False)

    x0 = nc.dram_tensor("x0", [T, D], F32, kind="ExternalInput").ap()
    wqT = nc.dram_tensor("wqT", [D, D], F32, kind="ExternalInput").ap()
    wkT = nc.dram_tensor("wkT", [D, D], F32, kind="ExternalInput").ap()
    wvT = nc.dram_tensor("wvT", [D, D], F32, kind="ExternalInput").ap()
    woT = nc.dram_tensor("woT", [D, D], F32, kind="ExternalInput").ap()
    wfT = nc.dram_tensor("wfT", [D, D], F32, kind="ExternalInput").ap()
    ident_in = nc.dram_tensor("ident_in", [128, 128], F32, kind="ExternalInput").ap()
    maskd_in = nc.dram_tensor("maskd_in", [128, 128], F32, kind="ExternalInput").ap()
    msel_in = nc.dram_tensor("msel_in", [128, OWN], F32, kind="ExternalInput").ap()
    schs_in = nc.dram_tensor("schs_in", [128, 1], F32, kind="ExternalInput").ap()

    y_out = nc.dram_tensor("y_out", [OWN, 128, D], F32, kind="ExternalOutput").ap()
    dbg_t = {}
    if "y1" in dbg:
        dbg_t["d_y1"] = nc.dram_tensor("d_y1", [NT, 128, D], F32, kind="ExternalOutput").ap()
    if "y1ln" in dbg:
        dbg_t["d_y1ln"] = nc.dram_tensor("d_y1ln", [NT, 128, D], F32, kind="ExternalOutput").ap()
    if "kvq" in dbg:
        dbg_t["d_k"] = nc.dram_tensor("d_k", [128, 2, T], F32, kind="ExternalOutput").ap()
        dbg_t["d_v"] = nc.dram_tensor("d_v", [128, NT, 258], F32, kind="ExternalOutput").ap()
        dbg_t["d_q"] = nc.dram_tensor("d_q", [128, 2, OWN * 128], F32, kind="ExternalOutput").ap()
    if "attn" in dbg:
        dbg_t["d_at"] = nc.dram_tensor("d_at", [H, 128, OWN * 128], F32, kind="ExternalOutput").ap()
    if "y2" in dbg:
        dbg_t["d_y2"] = nc.dram_tensor("d_y2", [OWN, 128, D], F32, kind="ExternalOutput").ap()
    if "qdbg" in dbg:
        dbg_t["d_qd"] = nc.dram_tensor("d_qd", [4, 128, D], F32, kind="ExternalOutput").ap()
    if "xq2" in dbg:
        dbg_t["d_xq2"] = nc.dram_tensor("d_xq2", [OWN, 128, D], F32, kind="ExternalOutput").ap()
        dbg_t["d_st2"] = nc.dram_tensor("d_st2", [128, 2], F32, kind="ExternalOutput").ap()
    if "att2" in dbg:
        dbg_t["d_o0"] = nc.dram_tensor("d_o0", [128, 4, 129], F32, kind="ExternalOutput").ap()
        dbg_t["d_nc"] = nc.dram_tensor("d_nc", [128, 2], F32, kind="ExternalOutput").ap()
        dbg_t["d_p0"] = nc.dram_tensor("d_p0", [128, 512], F32, kind="ExternalOutput").ap()
        dbg_t["d_ch"] = nc.dram_tensor("d_ch", [128, 12], F32, kind="ExternalOutput").ap()

    s_y1ln = nc.dram_tensor("s_y1ln", [NT, 128, D], BF16, kind="Internal").ap()
    s_y1own = nc.dram_tensor("s_y1own", [OWN, 128, D], F32, kind="Internal").ap()
    s_ylT = nc.dram_tensor("s_ylT", [128, NIC, T], BF16, kind="Internal").ap()
    s_wqT = nc.dram_tensor("s_wqT", [128, NIC * D], BF16, kind="Internal").ap()
    s_attnT = nc.dram_tensor("s_attnT", [H, 128, OWN * 128], BF16, kind="Internal").ap()

    with ExitStack() as ctx:
        TE = Eng(ctx.enter_context(nc.semaphore("sem_t")), False)
        VE = Eng(ctx.enter_context(nc.semaphore("sem_v")), True)
        SE = Eng(ctx.enter_context(nc.semaphore("sem_s")), True)
        GE = Eng(ctx.enter_context(nc.semaphore("sem_g")), False)
        YE = Eng(ctx.enter_context(nc.semaphore("sem_y")), False)
        ALL = [TE, VE, SE, GE, YE]

        ALL_DMABUFS = []

        def barrier():
            evs = [g.ev() for g in ALL]
            for g in ALL:
                for ev in evs:
                    if ev.sem is not g.sem:
                        g.wait(ev)
                for db in ALL_DMABUFS:
                    if db.cum:
                        g.wait(db.ev())

        def DB(name):
            b = DmaBuf(nc, ctx, name)
            ALL_DMABUFS.append(b)
            return b

        d_x0 = [DB(f"dm_x{i}") for i in range(2)]
        d_wld = [DB(f"dm_w{i}") for i in range(2)]
        d_y1lnst = [DB(f"dm_ylst{i}") for i in range(2)]
        d_y1st = [DB(f"dm_y1st{i}") for i in range(2)]
        d_const = DB("dm_const")
        d_ylld = [DB(f"dm_ylld{i}") for i in range(2)]
        d_ylTst = [DB(f"dm_ylTst{i}") for i in range(2)]
        d_ylTld = [DB(f"dm_ylTld{i}") for i in range(2)]
        d_wsl = DB("dm_wsl")
        d_atst = DB("dm_atst")
        d_atld = [DB(f"dm_atld{i}") for i in range(2)]
        d_y1ld = [DB(f"dm_y1ld{i}") for i in range(2)]
        d_wcd = DB("dm_wcd")
        d_out = DB("dm_out")
        d_dbg = DB("dm_dbg")
        d_dbgY = [DB(f"dm_dbgY{i}") for i in range(2)]
        d_dbgG2 = [DB(f"dm_dbgG{i}") for i in range(2)]

        identb = ctx.enter_context(nc.sbuf_tensor("identb", [128, 128], BF16))
        identf = ctx.enter_context(nc.sbuf_tensor("identf", [128, 128], F32))
        maskd = ctx.enter_context(nc.sbuf_tensor("maskd", [128, 128], BF16))
        msel = ctx.enter_context(nc.sbuf_tensor("msel", [128, OWN], F32))
        ones_f = ctx.enter_context(nc.sbuf_tensor("ones_f", [128, 128], F32))
        pad_f = ctx.enter_context(nc.sbuf_tensor("pad_f", [128, 128], F32))

        xm = ctx.enter_context(nc.sbuf_tensor("xm", [128, D], F32))
        xqT = ctx.enter_context(nc.sbuf_tensor("xqT", [128, NIC, 128], BF16))
        junk = ctx.enter_context(nc.sbuf_tensor("junk", [128, D], F32))
        junk2 = ctx.enter_context(nc.sbuf_tensor("junk2", [128, D], F32))
        junk_s = ctx.enter_context(nc.sbuf_tensor("junk_s", [128, D], F32))

        def vec(name):
            return ctx.enter_context(nc.sbuf_tensor(name, [128, 1], F32))

        st_cs = vec("st_cs")
        st_sq = vec("st_sq")
        st_m = vec("st_m")
        st_msq = vec("st_msq")
        st_var = vec("st_var")
        st_sig = vec("st_sig")
        st_inv = vec("st_inv")
        st_am = vec("st_am")
        st_a = vec("st_a")
        st_ra = vec("st_ra")
        st_g = vec("st_g")
        st_st = vec("st_st")
        wsc = vec("wsc")
        wsc_inv = vec("wsc_inv")
        wscB = vec("wscB")
        st_mx = vec("st_mx")
        st_gm = vec("st_gm")
        negc = [vec("negc0"), vec("negc1")]
        wabs = ctx.enter_context(nc.sbuf_tensor("wabs", [128, NIC], F32))

        ev_c = None
        for dst, src in ((identb, ident_in), (identf, ident_in), (maskd, maskd_in),
                         (msel, msel_in)):
            ev_c = d_const.start(GE, dst[:, :], src[:, :])
        ev_consts = ev_c
        ev_ones = VE.op("memset", ones_f[:, :], 1.0)
        ones_b = ctx.enter_context(nc.sbuf_tensor("ones_b", [128, 128], BF16))
        ev_onesb = VE.op("memset", ones_b[:, :], 1.0)
        st_eps = vec("st_eps")
        ev_eps = VE.op("memset", st_eps[:, :], EPS)
        st_schs = vec("st_schs")
        st_r127 = vec("st_r127")
        schs_sb = ctx.enter_context(nc.sbuf_tensor("schs_sb", [128, 1], F32))
        ev_schs_ld = d_const.start(GE, schs_sb[:, :], schs_in[:, :])
        ev_consts = ev_schs_ld
        VE.op("tensor_copy", st_schs[:, :], schs_sb[:, :], waits=[ev_schs_ld])
        VE.op("memset", st_r127[:, :], 127.0)
        ev_r127 = VE.op("reciprocal", st_r127[:, :], st_r127[:, :])

        def ln_quant(src_ap, waits_src, xq_out=None, st_st_out=None):
            VE.op("reduce_sum", st_cs[:, :], src_ap, axis=AX.X, waits=waits_src)
            SE.op("activation", out=junk_s[:, :], in_=src_ap, func=AF.Square,
                  accum_out=st_sq[:, :], waits=list(waits_src) + [VE.ev()])
            ev_sq = SE.ev()
            VE.op("tensor_scalar", out=st_m[:, :], in0=st_cs[:, :], scalar1=1.0 / D,
                  scalar2=None, op0=ALU.mult)
            VE.op("tensor_scalar", out=xm[:, :], in0=src_ap, scalar1=st_m[:, 0:1],
                  scalar2=None, op0=ALU.subtract)
            VE.op("reduce_max", st_am[:, :], xm[:, :], axis=AX.X, apply_absolute_value=True)
            VE.op("tensor_mul", st_msq[:, :], st_m[:, :], st_m[:, :])
            VE.op("scalar_tensor_tensor", out=st_var[:, :], in0=st_sq[:, :], scalar=1.0 / D,
                  in1=st_msq[:, :], op0=ALU.mult, op1=ALU.subtract, waits=[ev_sq])
            ev_var = VE.ev()
            ev_sig = SE.op("activation", out=st_sig[:, :], in_=st_var[:, :], func=AF.Sqrt,
                           bias=st_eps[:, 0:1], waits=[ev_var, ev_eps])
            VE.op("reciprocal", st_inv[:, :], st_sig[:, :], waits=[ev_sig])
            VE.op("tensor_mul", st_a[:, :], st_am[:, :], st_inv[:, :])
            VE.op("tensor_scalar_max", st_a[:, :], st_a[:, :], 1e-8)
            VE.op("reciprocal", st_ra[:, :], st_a[:, :])
            VE.op("tensor_scalar", out=st_ra[:, :], in0=st_ra[:, :], scalar1=127.0,
                  scalar2=None, op0=ALU.mult)
            VE.op("tensor_mul", st_g[:, :], st_inv[:, :], st_ra[:, :])
            xi = ln_quant.xi[0]
            VE.op("tensor_scalar", out=junk[:, :], in0=xm[:, :], scalar1=st_g[:, 0:1],
                  scalar2=SH, op0=ALU.mult, op1=ALU.add)
            VE.op("tensor_copy", xi[:, :], junk[:, :])
            VE.op("tensor_copy", junk2[:, :], xi[:, :])
            if ln_quant.qdbg[0] is not None:
                qd = ln_quant.qdbg[0]
                d_dbg.start(YE, qd[0], junk[:, :], waits=[VE.ev()])
                d_dbg.start(YE, qd[1], junk2[:, :], waits=[VE.ev()])
            VE.op("tensor_tensor", xm[:, :], junk[:, :], junk2[:, :], op=ALU.subtract)
            if ln_quant.qdbg[0] is not None:
                qd = ln_quant.qdbg[0]
                d_dbg.start(YE, qd[2], xm[:, :], waits=[VE.ev()])
            VE.op("tensor_scalar", out=junk[:, :], in0=xm[:, :], scalar1=1.0,
                  scalar2=None, op0=ALU.is_ge, waits=[d_dbg.ev()])
            VE.op("scalar_tensor_tensor", out=junk[:, :], in0=xm[:, :], scalar=2.0,
                  in1=junk[:, :], op0=ALU.is_ge, op1=ALU.add)
            VE.op("scalar_tensor_tensor", out=junk[:, :], in0=xm[:, :], scalar=0.0,
                  in1=junk[:, :], op0=ALU.is_lt, op1=ALU.subtract)
            VE.op("scalar_tensor_tensor", out=junk[:, :], in0=xm[:, :], scalar=-1.0,
                  in1=junk[:, :], op0=ALU.is_lt, op1=ALU.add)
            VE.op("tensor_tensor", junk[:, :], junk2[:, :], junk[:, :], op=ALU.subtract)
            VE.op("tensor_scalar", out=junk[:, :], in0=junk[:, :], scalar1=383.0,
                  scalar2=129.0, op0=ALU.min, op1=ALU.max)
            xq_o = xq_out
            st_o = st_st if st_st_out is None else st_st_out
            ev_xq = VE.op("tensor_scalar", out=xq_o[:, :], in0=junk[:, :], scalar1=256.0,
                          scalar2=None, op0=ALU.subtract, waits=[ln_quant.ge_guard[0]])
            VE.op("tensor_mul", st_o[:, :], st_a[:, :], wscB[:, :])
            return ev_xq

        ln_quant.ge_guard = [None]
        ln_quant.xi = [None]
        ln_quant.qdbg = [None]

        tr_state = {0: None, 1: None}

        def transpose16(trps, src_fn, dst_fn, ev_src, ev_prev_cons):
            ev_out = None
            for g in range(4):
                b = g % 2
                for kk in range(4):
                    ic = g * 4 + kk
                    w = []
                    if kk == 0:
                        w = [ev_src, ev_consts, tr_state[b]]
                        if g == 0 and ev_prev_cons is not None:
                            w.append(ev_prev_cons)
                    TE.op("matmul", trps[b][:, kk * 128:(kk + 1) * 128], src_fn(ic),
                          identb[:, :], start=True, stop=True, inc=(kk == 3), waits=w)
                ev_mm = TE.ev()
                ev_out = SE.op("copy", dst_fn(g), trps[b][:, :], waits=[ev_mm])
                tr_state[b] = ev_out
            return ev_out

        def ln2_normalize(src_ap, dst_ap, ev_src, extra_waits=()):
            VE.op("reduce_sum", st_cs[:, :], src_ap, axis=AX.X, waits=[ev_src])
            SE.op("activation", out=junk_s[:, :], in_=src_ap, func=AF.Square,
                  accum_out=st_sq[:, :], waits=[ev_src, VE.ev()])
            ev_sq = SE.ev()
            VE.op("tensor_scalar", out=st_m[:, :], in0=st_cs[:, :], scalar1=1.0 / D,
                  scalar2=None, op0=ALU.mult)
            VE.op("tensor_mul", st_msq[:, :], st_m[:, :], st_m[:, :])
            VE.op("scalar_tensor_tensor", out=st_var[:, :], in0=st_sq[:, :], scalar=1.0 / D,
                  in1=st_msq[:, :], op0=ALU.mult, op1=ALU.subtract, waits=[ev_sq])
            ev_var = VE.ev()
            ev_sig = SE.op("activation", out=st_sig[:, :], in_=st_var[:, :], func=AF.Sqrt,
                           bias=st_eps[:, 0:1], waits=[ev_var, ev_eps])
            VE.op("reciprocal", st_inv[:, :], st_sig[:, :], waits=[ev_sig])
            return VE.op("tensor_scalar", out=dst_ap, in0=src_ap, scalar1=st_m[:, 0:1],
                         scalar2=st_inv[:, 0:1], op0=ALU.subtract, op1=ALU.mult,
                         waits=list(extra_waits))

        with ExitStack() as actx:
            wbuf = [actx.enter_context(nc.sbuf_tensor(f"wbuf{i}", [128, D], F32)) for i in range(2)]
            xbuf = [actx.enter_context(nc.sbuf_tensor(f"xbuf{i}", [128, D], F32)) for i in range(2)]
            y1b = [actx.enter_context(nc.sbuf_tensor(f"y1b{i}", [128, D], F32)) for i in range(2)]
            y1ln = [actx.enter_context(nc.sbuf_tensor(f"y1ln{i}", [128, D], BF16)) for i in range(2)]
            u1 = actx.enter_context(nc.sbuf_tensor("u1", [128, NIC * D], BF16))
            u1_wq = u1.ap().rearrange("p (c n) -> p c n", n=D)
            xiA = actx.enter_context(nc.sbuf_tensor("xiA", [128, D], mybir.dt.int32))
            xqb = [actx.enter_context(nc.sbuf_tensor(f"xqb{i}", [128, D], BF16)) for i in range(2)]
            xqTb = [actx.enter_context(nc.sbuf_tensor(f"xqTb{i}", [128, NIC, 128], BF16)) for i in range(2)]
            ylnT = [actx.enter_context(nc.sbuf_tensor(f"ylnT{i}", [128, NIC, 128], BF16)) for i in range(2)]
            st_stb = [actx.enter_context(nc.sbuf_tensor(f"st_stb{i}", [128, 1], F32)) for i in range(2)]
            ln_quant.xi[0] = xiA
            ypsum = actx.enter_context(nc.psum_tensor("ypsum", [128, D], F32))
            trpsA = [actx.enter_context(nc.psum_tensor(f"trpsA{i}", [128, 512], F32)) for i in range(2)]

            ev_w = [None, None]
            ev_wfree = [None, None]
            for k in range(NIC):
                b = k % 2
                ev_w[b] = d_wld[b].start(GE, wbuf[b][:, :], wfT[k * 128:(k + 1) * 128, :],
                                         waits=[ev_wfree[b]] if ev_wfree[b] else [])
                ev_wfree[b] = SE.op("activation", out=junk_s[:, :], in_=wbuf[b][:, :],
                                    func=AF.Abs, accum_out=wabs[:, k:k + 1], waits=[ev_w[b]])
            ev_ws = VE.op("reduce_sum", st_cs[:, :], wabs[:, :], axis=AX.X,
                          waits=[ev_wfree[0], ev_wfree[1]])
            VE.op("tensor_copy", st_mx[:, :], st_cs[:, :])
            ev_wsv = VE.ev()
            ev_wsum = TE.op("matmul", trpsA[0][:, 0:1], ones_f[:, :], st_mx[:, :],
                            start=True, stop=True, waits=[ev_ones, ev_wsv])
            VE.op("tensor_scalar", out=wsc[:, :], in0=trpsA[0][:, 0:1],
                  scalar1=WMEAN_INV, scalar2=None, op0=ALU.mult, waits=[ev_wsum])
            VE.op("tensor_scalar_max", wsc[:, :], wsc[:, :], 1e-6)
            ev_wsci = VE.op("reciprocal", wsc_inv[:, :], wsc[:, :])
            VE.op("tensor_scalar", out=wscB[:, :], in0=wsc[:, :],
                  scalar1=0.5, scalar2=None, op0=ALU.mult)
            ev_wscB = VE.op("tensor_mul", wscB[:, :], wscB[:, :], st_r127[:, 0:1],
                            waits=[ev_r127])
            ev_wsum_read = ev_wscB

            ev_w = [None, None]
            for k in range(NIC):
                b = k % 2
                ev_w[b] = d_wld[b].start(GE, wbuf[b][:, :], wfT[k * 128:(k + 1) * 128, :],
                                         waits=[ev_wfree[b]] if ev_wfree[b] else [])
                xi = ln_quant.xi[0]
                VE.op("tensor_scalar", out=junk[:, :], in0=wbuf[b][:, :],
                      scalar1=wsc_inv[:, 0:1], scalar2=SH, op0=ALU.mult, op1=ALU.add,
                      waits=[ev_w[b], ev_wsci])
                ev_wfree[b] = VE.ev()
                VE.op("tensor_copy", xi[:, :], junk[:, :])
                VE.op("tensor_copy", junk2[:, :], xi[:, :])
                VE.op("tensor_tensor", xm[:, :], junk[:, :], junk2[:, :], op=ALU.subtract)
                VE.op("tensor_scalar", out=junk[:, :], in0=xm[:, :], scalar1=1.0,
                      scalar2=None, op0=ALU.is_ge)
                VE.op("tensor_scalar", out=xm[:, :], in0=xm[:, :], scalar1=0.0,
                      scalar2=None, op0=ALU.is_lt)
                VE.op("tensor_tensor", junk[:, :], junk[:, :], xm[:, :], op=ALU.subtract)
                VE.op("tensor_tensor", junk[:, :], junk[:, :], junk2[:, :], op=ALU.add)
                VE.op("tensor_scalar", out=junk[:, :], in0=junk[:, :], scalar1=257.0,
                      scalar2=255.0, op0=ALU.min, op1=ALU.max)
                VE.op("tensor_scalar", out=u1_wq[:, k, :], in0=junk[:, :], scalar1=256.0,
                      scalar2=None, op0=ALU.subtract)
            ev_wq_ready = VE.ev()
            d_wqst = DB("dm_wqst")
            ev_wq_st = d_wqst.start(YE, s_wqT[:, :], u1.ap()[:, :], waits=[ev_wq_ready])
            if "wq" in dbg:
                for k in range(NIC):
                    VE.op("tensor_copy", junk[:, :], u1_wq[:, k, :], waits=[ev_wq_ready])
                    d_dbg.start(YE, dbg_t["d_wq"][:, k, :], junk[:, :], waits=[VE.ev()])

            ev_x = [None, None]
            ev_xfree = [None, None]
            ev_y1free = [[], []]
            ev_ylnfree = [[], []]
            ev_ffn = [None, None]
            ev_trans = [None, None]
            ev_ylnT_store = [None, None]
            ev_stt_prev = None
            ev_xq_e = [None, None]
            ev_xqT_e = [None, None]

            def quant_stage(t):
                b = t % 2
                ev_x[b] = d_x0[b].start(YE, xbuf[b][:, :], x0[t * 128:(t + 1) * 128, :],
                                        waits=[ev_xfree[b]] if ev_xfree[b] else [])
                w = [ev_x[b], ev_wscB]
                if ev_trans[b] is not None:
                    w.append(ev_trans[b])
                ev_xq_e[b] = ln_quant(xbuf[b][:, :], w, xq_out=xqb[b], st_st_out=st_stb[b])
                ev_xqT = transpose16(
                    trpsA, lambda ic, bb=b: xqb[bb][:, ic * 128:(ic + 1) * 128],
                    lambda g, bb=b: xqTb[bb].ap()[:, g * 4:(g + 1) * 4, :].rearrange("p c n -> p (c n)"),
                    ev_xq_e[b], ev_ffn[b])
                ev_trans[b] = TE.ev()
                ev_xqT_e[b] = ev_xqT

            def ffn_stage(t):
                b = t % 2
                for oc in range(4):
                    for ic in range(NIC):
                        w = []
                        if ic == 0 and oc == 0:
                            w = [ev_xqT_e[b], ev_wq_ready]
                            if ev_stt_prev is not None:
                                w.append(ev_stt_prev)
                        TE.op("matmul", ypsum[:, oc * 512:(oc + 1) * 512], xqTb[b][:, ic, :],
                              u1_wq[:, ic, oc * 512:(oc + 1) * 512],
                              start=(ic == 0), stop=(ic == NIC - 1),
                              inc=(ic == NIC - 1 and oc == 3), waits=w)
                ev_ffn[b] = TE.ev()

            def finish_stage(t):
                nonlocal ev_stt_prev
                b = t % 2
                yb = y1b[b]
                VE.op("scalar_tensor_tensor", out=yb[:, :], in0=ypsum[:, :],
                      scalar=st_stb[b][:, 0:1], in1=xbuf[b][:, :], op0=ALU.mult, op1=ALU.add,
                      waits=[ev_ffn[b]] + ev_y1free[b])
                ev_y1 = VE.ev()
                ev_stt_prev = ev_y1
                ev_xfree[b] = ev_y1
                ev_yln = ln2_normalize(yb[:, :], y1ln[b][:, :], ev_y1, ev_ylnfree[b])
                ev_tr = transpose16(
                    trpsA, lambda ic, bb=b: y1ln[bb][:, ic * 128:(ic + 1) * 128],
                    lambda g, bb=b: ylnT[bb].ap()[:, g * 4:(g + 1) * 4, :].rearrange("p c n -> p (c n)"),
                    ev_yln, ev_ylnT_store[b])
                for ic in range(NIC):
                    d_ylTst[b].start(YE, s_ylT[:, ic, t * 128:(t + 1) * 128], ylnT[b][:, ic, :],
                                     waits=[ev_tr] if ic == 0 else [])
                ev_ylnT_store[b] = d_ylTst[b].ev()
                ev_ylnfree[b] = [ev_tr]
                ev_y1free[b] = [ev_y1]
                if t < OWN:
                    ev_y1free[b].append(d_y1st[b].start(YE, s_y1own[t, :, :], yb[:, :],
                                                        waits=[ev_y1]))
                if "y1" in dbg:
                    ev_y1free[b].append(d_dbgY[b].start(YE, dbg_t["d_y1"][t, :, :], yb[:, :],
                                                        waits=[ev_y1]))
                if "y1ln" in dbg:
                    ev_ylnfree[b].append(d_dbgG2[b].start(GE, dbg_t["d_y1ln"][t, :, :],
                                                          y1ln[b][:, :], waits=[ev_yln]))

            quant_stage(0)
            ffn_stage(0)
            for t in range(1, NT):
                quant_stage(t)
                finish_stage(t - 1)
                ffn_stage(t)
            finish_stage(NT - 1)

            barrier()

        if upto >= 'B':
            with ExitStack() as bctx:
                Kt = bctx.enter_context(nc.sbuf_tensor("Kt", [128, 2, T], BF16))
                Vh = bctx.enter_context(nc.sbuf_tensor("Vh", [128, NT, 258], BF16))
                Qt = bctx.enter_context(nc.sbuf_tensor("Qt", [128, 2, OWN * 128], BF16))
                Pt = [bctx.enter_context(nc.sbuf_tensor(f"Pt{i}", [128, 512], BF16)) for i in range(4)]
                wk_sl = bctx.enter_context(nc.sbuf_tensor("wk_sl", [128, NIC, 256], BF16))
                wv_sl = bctx.enter_context(nc.sbuf_tensor("wv_sl", [128, NIC, 256], BF16))
                wq_sl = bctx.enter_context(nc.sbuf_tensor("wq_sl", [128, NIC, 256], BF16))
                ylstr = [bctx.enter_context(nc.sbuf_tensor(f"ylstr{i}", [128, NIC, 512], BF16)) for i in range(2)]
                attn_sb = bctx.enter_context(nc.sbuf_tensor("attn_sb", [128, 128], BF16))
                attnT = bctx.enter_context(nc.sbuf_tensor("attnT", [128, 2, OWN * 128], BF16))

                for hg in range(H // 2):
                    barrier()
                    SE.wait(d_atst.ev())
                    h0 = 2 * hg
                    d_wsl.start(GE, wk_sl[:, :, :],
                                wkT.rearrange("(c p) o -> p c o", p=128)[:, :, h0 * 128:(h0 + 2) * 128])
                    d_wsl.start(GE, wv_sl[:, :, :],
                                wvT.rearrange("(c p) o -> p c o", p=128)[:, :, h0 * 128:(h0 + 2) * 128])
                    d_wsl.start(GE, wq_sl[:, :, :],
                                wqT.rearrange("(c p) o -> p c o", p=128)[:, :, h0 * 128:(h0 + 2) * 128])
                    ev_wall = d_wsl.ev()
                    VE.op("memset", Vh.ap()[:, :, 128:129], 1.0)
                    ev_onescol = VE.op("memset", Vh.ap()[:, :, 257:258], 1.0)

                    with ExitStack() as pctx:
                        psK = [pctx.enter_context(nc.psum_tensor(f"psK{hg}_{i}", [128, 512], F32)) for i in range(2)]
                        psV = [pctx.enter_context(nc.psum_tensor(f"psV{hg}_{i}", [128, 256], F32)) for i in range(2)]
                        psQ = pctx.enter_context(nc.psum_tensor(f"psQ{hg}", [128, 512], F32))
                        ev_str = [None, None]
                        ev_strfree = [None, None]
                        evK = {0: None, 1: None}
                        evV = {0: None, 1: None}
                        evQ = None
                        for t5 in range(8):
                            b = t5 % 2
                            ev_str[b] = d_ylTld[b].start(YE, ylstr[b][:, :, :],
                                                         s_ylT[:, :, t5 * 512:(t5 + 1) * 512],
                                                         waits=[ev_strfree[b]] if ev_strfree[b] else [])
                            for dc in range(2):
                                for ic in range(NIC):
                                    w = [ev_wall, ev_str[b], evK[dc]] if ic == 0 else []
                                    TE.op("matmul", psK[dc][:, :], wk_sl[:, ic, dc * 128:(dc + 1) * 128],
                                          ylstr[b][:, ic, :], start=(ic == 0), stop=(ic == NIC - 1),
                                          inc=(ic == NIC - 1), waits=w)
                                evK[dc] = SE.op("copy", Kt[:, dc, t5 * 512:(t5 + 1) * 512],
                                                psK[dc][:, :], waits=[TE.ev()])
                            for tc in range(4):
                                kt = t5 * 4 + tc
                                pb = tc % 2
                                for ic in range(NIC):
                                    w = [ev_wall, ev_str[b], evV[pb]] if ic == 0 else []
                                    TE.op("matmul", psV[pb][:, :],
                                          ylstr[b][:, ic, tc * 128:(tc + 1) * 128],
                                          wv_sl[:, ic, :], start=(ic == 0), stop=(ic == NIC - 1),
                                          inc=(ic == NIC - 1), waits=w)
                                ev_mm = TE.ev()
                                SE.op("copy", Vh[:, kt, 0:128], psV[pb][:, 0:128], waits=[ev_mm])
                                evV[pb] = SE.op("copy", Vh[:, kt, 129:257], psV[pb][:, 128:256])
                            if t5 < 4:
                                for dc in range(2):
                                    for ic in range(NIC):
                                        w = [ev_wall, ev_str[b], evQ] if ic == 0 else []
                                        TE.op("matmul", psQ[:, :], wq_sl[:, ic, dc * 128:(dc + 1) * 128],
                                              ylstr[b][:, ic, :], start=(ic == 0), stop=(ic == NIC - 1),
                                              inc=(ic == NIC - 1), waits=w)
                                    evQ = SE.op("activation",
                                                out=Qt[:, dc, t5 * 512:(t5 + 1) * 512],
                                                in_=psQ[:, :], func=AF.Copy,
                                                scale=st_schs[:, 0:1], waits=[TE.ev()])
                            ev_strfree[b] = TE.ev()
                        ev_proj = [evK[0], evK[1], evV[0], evV[1], evQ]

                        ev_negc = [None, None]
                        ev_sqr = None
                        for dc in range(2):
                            chsc = ylstr[0].ap().rearrange("p c n -> p (c n)")[:, 0:D]
                            ev_sq = SE.op("activation", out=chsc, in_=Qt[:, dc, :],
                                          func=AF.Square,
                                          waits=ev_proj + [ev_negc[0], ev_sqr, TE.ev()])
                            for c4 in range(4):
                                ev_mm = TE.op("matmul", psQ[:, :], ones_b[:, :],
                                              chsc[:, c4 * 512:(c4 + 1) * 512], start=True,
                                              stop=True, waits=[ev_sq, ev_onesb, evQ,
                                                                ev_negc[0], ev_negc[1],
                                                                VE.ev()])
                                VE.op("reduce_max", st_gm[:, :], psQ[:, :], axis=AX.X,
                                      waits=[ev_mm])
                                if c4 == 0:
                                    VE.op("tensor_copy", st_am[:, :], st_gm[:, :])
                                else:
                                    VE.op("tensor_max", st_am[:, :], st_am[:, :], st_gm[:, :])
                            ev_q2 = VE.ev()
                            ev_first = None
                            for half in range(2):
                                ev_sqr = SE.op("activation", out=chsc,
                                               in_=Kt[:, dc, half * D:(half + 1) * D],
                                               func=AF.Square, waits=[ev_q2, TE.ev()])
                                for c4 in range(4):
                                    ev_mm = TE.op("matmul", psQ[:, :], ones_b[:, :],
                                                  chsc[:, c4 * 512:(c4 + 1) * 512], start=True,
                                                  stop=True, waits=[ev_sqr, VE.ev()])
                                    VE.op("reduce_max", st_gm[:, :], psQ[:, :], axis=AX.X,
                                          waits=[ev_mm])
                                    if half == 0 and c4 == 0:
                                        VE.op("tensor_copy", st_mx[:, :], st_gm[:, :])
                                    else:
                                        VE.op("tensor_max", st_mx[:, :], st_mx[:, :], st_gm[:, :])
                            VE.op("tensor_mul", st_gm[:, :], st_am[:, :], st_mx[:, :])
                            ev_m2 = VE.ev()
                            ev_rt = SE.op("activation", out=st_gm[:, :], in_=st_gm[:, :],
                                          func=AF.Sqrt, waits=[ev_m2])
                            ev_negc[dc] = VE.op("tensor_scalar", out=negc[dc][:, :],
                                                in0=st_gm[:, :], scalar1=-1.0,
                                                scalar2=None, op0=ALU.mult, waits=[ev_rt])
                            if "att2" in dbg and hg == 0 and dc == 0:
                                VE.op("tensor_copy", junk[:, 0:1], st_am[:, :])
                                VE.op("tensor_copy", junk[:, 1:2], st_mx[:, :])
                                VE.op("tensor_copy", junk[:, 2:3], negc[0][:, :])
                                d_dbg.start(YE, dbg_t["d_ch"][:, :], junk[:, 0:12], waits=[VE.ev()])
                                VE.op("memset", junk[:, 0:1], 0.0, waits=[d_dbg.ev()])

                    if "kvq" in dbg and hg == 0:
                        ev_d = None
                        for dc in range(2):
                            for half in range(2):
                                VE.op("tensor_copy", junk[:, :], Kt[:, dc, half * D:(half + 1) * D],
                                      waits=ev_proj + [ev_d])
                                ev_d = d_dbg.start(YE, dbg_t["d_k"][:, dc, half * D:(half + 1) * D],
                                                   junk[:, :], waits=[VE.ev()])
                            VE.op("tensor_copy", junk[:, :], Qt[:, dc, :], waits=ev_proj + [ev_d])
                            ev_d = d_dbg.start(YE, dbg_t["d_q"][:, dc, :], junk[:, :], waits=[VE.ev()])
                        for kt in range(NT):
                            VE.op("tensor_copy", junk[:, 0:258], Vh[:, kt, :], waits=ev_proj + [ev_d])
                            ev_d = d_dbg.start(YE, dbg_t["d_v"][:, kt, :], junk[:, 0:258], waits=[VE.ev()])
                        VE.op("memset", junk[:, 0:1], 0.0, waits=[ev_d])

                    with ExitStack() as actx2:
                        Sps = [actx2.enter_context(nc.psum_tensor(f"Sps{hg}_{i}", [128, 512], F32)) for i in range(4)]
                        Ops = [actx2.enter_context(nc.psum_tensor(f"Ops{hg}_{i}", [128, 129], F32)) for i in range(4)]
                        ev_exp = {i: None for i in range(4)}
                        ev_pvd = {i: None for i in range(4)}
                        ev_o_cons = None
                        ev_att_st = None
                        for dc in range(2):
                            h = h0 + dc
                            for qg in range(4):
                                i_lo = qg * 4
                                i_hi = i_lo + 4
                                kts = [(0, k) for k in range(i_hi)] + [(1, k) for k in range(i_hi)]
                                m_ev = {}

                                def sc_stage(it, i_lo=i_lo, dc=dc):
                                    side, k = kts[it]
                                    vlo = max(i_lo, k)
                                    off = (vlo - i_lo) * 128
                                    b = it % 4
                                    kcol = (k if side == 0 else 16 + k) * 128
                                    first = [ev_proj[0], ev_proj[1], ev_proj[4],
                                             ev_negc[0], ev_negc[1]]
                                    if ev_exp[b] is not None:
                                        first.append(ev_exp[b])
                                    if ev_o_cons is not None:
                                        first.append(ev_o_cons)
                                    ev_sc = TE.op("matmul", Sps[b][:, off:512],
                                                  Kt[:, dc, kcol:kcol + 128],
                                                  Qt[:, dc, i_lo * 128 + off: (i_lo + 4) * 128],
                                                  start=True, stop=True, waits=first)
                                    ev_e = SE.op("activation", out=Pt[b][:, off:512],
                                                 in_=Sps[b][:, off:512], func=AF.Exp,
                                                 bias=negc[dc][:, 0:1],
                                                 waits=[ev_sc, ev_negc[dc], ev_pvd[b]])
                                    ev_m = ev_e
                                    if side == 0 and k >= i_lo:
                                        c0 = (k - i_lo) * 128
                                        ev_m = VE.op("tensor_mul", Pt[b][:, c0:c0 + 128],
                                                     Pt[b][:, c0:c0 + 128], maskd[:, :],
                                                     waits=[ev_e, ev_consts])
                                    elif side == 1 and k >= i_lo:
                                        c0 = (k - i_lo) * 128
                                        ev_m = VE.op("tensor_scalar", out=Pt[b][:, c0:c0 + 128],
                                                     in0=Pt[b][:, c0:c0 + 128],
                                                     scalar1=msel[:, k:k + 1], scalar2=None,
                                                     op0=ALU.mult, waits=[ev_e, ev_consts])
                                    m_ev[it] = ev_m

                                def pv_stage(it, i_lo=i_lo, i_hi=i_hi, dc=dc):
                                    side, k = kts[it]
                                    vlo = max(i_lo, k)
                                    b = it % 4
                                    for i in range(vlo, i_hi):
                                        li = i - i_lo
                                        w = [m_ev[it], ev_onescol, ev_proj[2], ev_proj[3]] if i == vlo else []
                                        TE.op("matmul", Ops[li][:, :],
                                              Pt[b][:, li * 128:(li + 1) * 128],
                                              Vh[:, (k if side == 0 else 16 + k), dc * 129:(dc + 1) * 129],
                                              start=(side == 0 and k == 0),
                                              stop=(side == 1 and k == i),
                                              inc=(i == i_hi - 1), waits=w)
                                    ev_pvd[b] = TE.ev()
                                    if ("att2" in dbg and hg == 0 and dc == 0 and qg == 0
                                            and side == 0 and k == 0):
                                        VE.op("tensor_copy", junk[:, 0:512], Pt[b][:, :],
                                              waits=[m_ev[it]])
                                        d_dbg.start(YE, dbg_t["d_p0"][:, :], junk[:, 0:512],
                                                    waits=[VE.ev()])

                                sc_stage(0)
                                sc_stage(1)
                                for it in range(len(kts)):
                                    if it + 2 < len(kts):
                                        sc_stage(it + 2)
                                    pv_stage(it)
                                ev_pv_all = TE.ev()
                                if "att2" in dbg and hg == 0 and dc == 0 and qg == 0:
                                    for li4 in range(4):
                                        VE.op("tensor_copy", junk2[:, li4 * 129:(li4 + 1) * 129],
                                              Ops[li4][:, :], waits=[ev_pv_all, d_dbg.ev()])
                                    VE.op("tensor_copy", junk2[:, 516:517], negc[0][:, :],
                                          waits=[ev_negc[0], ev_negc[1]])
                                    VE.op("tensor_copy", junk2[:, 517:518], negc[1][:, :])
                                    ev_dd = VE.ev()
                                    for li4 in range(4):
                                        d_dbg.start(YE, dbg_t["d_o0"][:, li4, :],
                                                    junk2[:, li4 * 129:(li4 + 1) * 129],
                                                    waits=[ev_dd])
                                    d_dbg.start(YE, dbg_t["d_nc"][:, :], junk2[:, 516:518],
                                                waits=[ev_dd])
                                    VE.op("memset", junk2[:, 0:1], 0.0, waits=[d_dbg.ev()])
                                for li in range(4):
                                    i = i_lo + li
                                    VE.op("reciprocal", st_ra[:, :], Ops[li][:, 128:129],
                                          waits=[ev_pv_all])
                                    ev_at = VE.op("tensor_scalar", out=attn_sb[:, :],
                                                  in0=Ops[li][:, 0:128],
                                                  scalar1=st_ra[:, 0:1], scalar2=None, op0=ALU.mult,
                                                  waits=[ev_att_st] if ev_att_st else [])
                                    ev_tr = TE.op("matmul", Sps[0][:, 0:128], attn_sb[:, :],
                                                  identb[:, :], start=True, stop=True,
                                                  waits=[ev_at, ev_exp[0], ev_exp[1],
                                                         ev_exp[2], ev_exp[3]])
                                    ev_att_st = SE.op("copy", attnT[:, dc, i * 128:(i + 1) * 128],
                                                      Sps[0][:, 0:128], waits=[ev_tr])
                                    ev_exp[0] = ev_att_st
                                ev_o_cons = ev_att_st
                            d_atst.start(YE, s_attnT[h, :, :], attnT[:, dc, :], waits=[ev_att_st])
                            if "attn" in dbg:
                                VE.op("tensor_copy", junk[:, :], attnT[:, dc, :],
                                      waits=[ev_att_st, d_dbg.ev()])
                                d_dbg.start(YE, dbg_t["d_at"][h, :, :], junk[:, :], waits=[VE.ev()])

                barrier()

        if upto >= 'C':
            with ExitStack() as cctx:
                u2 = cctx.enter_context(nc.sbuf_tensor("u2", [128, NIC * D], BF16))
                u2_wo = u2.ap().rearrange("p (c n) -> p c n", n=D)
                u1c = cctx.enter_context(nc.sbuf_tensor("u1c", [128, NIC * D], BF16))
                u1_wq = u1c.ap().rearrange("p (c n) -> p c n", n=D)
                d_wqld = DB("dm_wqld")
                YE.wait(d_wqst.ev())
                ev_wq_ready = d_wqld.start(YE, u1c.ap()[:, :], s_wqT[:, :])
                atb_0 = cctx.enter_context(nc.sbuf_tensor("atb_0", [128, NIC, 128], BF16))
                atb = [atb_0, atb_0]
                y1s_0 = cctx.enter_context(nc.sbuf_tensor("y1s_0", [128, D], F32))
                y1s = [y1s_0, y1s_0]
                y2 = [cctx.enter_context(nc.sbuf_tensor(f"y2_{i}", [128, D], F32)) for i in range(2)]
                xq2b_0 = cctx.enter_context(nc.sbuf_tensor("xq2b_0", [128, D], BF16))
                xq2b = [xq2b_0, xq2b_0]
                st_stc = [cctx.enter_context(nc.sbuf_tensor(f"st_stc{i}", [128, 1], F32)) for i in range(2)]
                xiC = cctx.enter_context(nc.sbuf_tensor("xiC", [128, D], mybir.dt.int32))
                ln_quant.xi[0] = xiC
                ops_ps = [cctx.enter_context(nc.psum_tensor(f"ops_ps{i}", [128, D], F32)) for i in range(1)]
                ops_ps = [ops_ps[0], ops_ps[0]]
                fps = cctx.enter_context(nc.psum_tensor("fps", [128, D], F32))

                YE.wait(d_atst.ev())
                YE.wait(d_y1st[0].ev())
                YE.wait(d_y1st[1].ev())
                d_wcd.start(GE, u2_wo[:, :, :], woT.rearrange("(c p) o -> p c o", p=128))
                ev_wo_ld = d_wcd.ev()

                ev_at = [None, None]
                ev_atfree = [None, None]
                ev_y1l = [None, None]
                ev_y1lfree = [None, None]
                ev_y2_cons = None
                ev_y3free = [None, None]
                ev_xqT2_cons = None
                ev_ffn2_done = None
                s_at_v = s_attnT.rearrange("h p q -> p h q")
                evs = {"at": [None, None], "atfree": [None, None],
                       "y1l": [None, None], "y1lfree": [None, None],
                       "oproj": [None, None], "y2": [None, None],
                       "xq2": [None, None], "xqT2": None, "ffn2": None,
                       "y3": None, "y3free": None, "trans_done": [None, None]}

                def cd_dma(t):
                    b = t % 2
                    w = [evs["atfree"][1 - b]] if evs["atfree"][1 - b] else []
                    for half in range(2):
                        evs["at"][b] = d_atld[b].start(
                            YE, atb[b][:, half * 8:(half + 1) * 8, :],
                            s_at_v[:, half * 8:(half + 1) * 8, t * 128:(t + 1) * 128],
                            waits=w if half == 0 else [])

                def cd_dma_y1(t):
                    b = t % 2
                    evs["y1l"][b] = d_y1ld[0].start(
                        YE, y1s[b][:, :], s_y1own[t, :, :],
                        waits=[evs["y1lfree"][1 - b]] if evs["y1lfree"][1 - b] else [])

                def cd_oproj(t):
                    b = t % 2
                    for oc in range(4):
                        for ic in range(NIC):
                            w = []
                            if ic == 0 and oc == 0:
                                w = [evs["at"][b], ev_wo_ld]
                                if evs["trans_done"][1 - b] is not None:
                                    w.append(evs["trans_done"][1 - b])
                                if evs["y2"][1 - b] is not None:
                                    w.append(evs["y2"][1 - b])
                            TE.op("matmul", ops_ps[b][:, oc * 512:(oc + 1) * 512],
                                  atb[b][:, ic, :],
                                  u2_wo[:, ic, oc * 512:(oc + 1) * 512],
                                  start=(ic == 0), stop=(ic == NIC - 1),
                                  inc=(ic == NIC - 1 and oc == 3), waits=w)
                    evs["oproj"][b] = TE.ev()
                    evs["atfree"][b] = evs["oproj"][b]

                def cd_y2_quant(t):
                    b = t % 2
                    VE.op("scalar_tensor_tensor", out=y2[b][:, :], in0=ops_ps[b][:, :],
                          scalar=1.0, in1=y1s[b][:, :], op0=ALU.mult, op1=ALU.add,
                          waits=[evs["oproj"][b], evs["y1l"][b]] +
                                ([evs["xq2"][b]] if evs["xq2"][b] else []))
                    ev_y2 = VE.ev()
                    evs["y2"][b] = ev_y2
                    evs["y1lfree"][b] = ev_y2
                    ln_quant.qdbg[0] = (dbg_t["d_qd"] if ("qdbg" in dbg and t == 8) else None)
                    w = [ev_y2]
                    if evs["xqT2"] is not None:
                        w.append(evs["xqT2"])
                    for tb in range(2):
                        if evs["trans_done"][tb] is not None:
                            w.append(evs["trans_done"][tb])
                    if evs["y3free"] is not None:
                        w.append(evs["y3free"])
                    evs["xq2"][b] = ln_quant(y2[b][:, :], w, xq_out=xq2b[b], st_st_out=st_stc[b])
                    if "xq2" in dbg and t == 0:
                        ev_d1 = d_dbgG2[0].start(GE, dbg_t["d_xq2"][0, :, :], xq2b[b][:, :],
                                                 waits=[evs["xq2"][b]])
                        ln_quant.ge_guard[0] = ev_d1

                def cd_trans(t):
                    b = t % 2
                    ev_evac = None
                    for g in range(4):
                        pb = g % 2
                        for kk in range(4):
                            ic = g * 4 + kk
                            w = []
                            if kk == 0:
                                w = [evs["xq2"][b], ev_consts, evs["y2"][b]]
                                if g >= 2:
                                    w.append(ev_evac)
                            TE.op("matmul",
                                  ops_ps[b][:, pb * 512 + kk * 128: pb * 512 + (kk + 1) * 128],
                                  xq2b[b][:, ic * 128:(ic + 1) * 128], identb[:, :],
                                  start=True, stop=True, inc=(kk == 3), waits=w)
                        ev_mm = TE.ev()
                        ev_evac = SE.op(
                            "copy",
                            xqT.ap()[:, g * 4:(g + 1) * 4, :].rearrange("p c n -> p (c n)"),
                            ops_ps[b][:, pb * 512:(pb + 1) * 512], waits=[ev_mm])
                    evs["xqT2"] = ev_evac
                    evs["trans_done"][b] = ev_evac

                def cd_y3(t):
                    b = t % 2
                    VE.op("scalar_tensor_tensor", out=xm[:, :], in0=fps[:, :],
                          scalar=st_stc[b][:, 0:1], in1=y2[b][:, :], op0=ALU.mult, op1=ALU.add,
                          waits=[evs["ffn2"], d_dbg.ev()] +
                                ([evs["y3free"]] if evs["y3free"] else []))
                    ev_y3 = VE.ev()
                    evs["y3"] = ev_y3
                    evs["y3free"] = d_out.start(YE, y_out[t, :, :], xm[:, :], waits=[ev_y3])

                def cd_ffn2(t):
                    b = t % 2
                    for oc in range(4):
                        for ic in range(NIC):
                            w = []
                            if ic == 0 and oc == 0:
                                w = [evs["xqT2"], ev_wq_ready]
                                if evs["y3"] is not None:
                                    w.append(evs["y3"])
                            TE.op("matmul", fps[:, oc * 512:(oc + 1) * 512], xqT[:, ic, :],
                                  u1_wq[:, ic, oc * 512:(oc + 1) * 512],
                                  start=(ic == 0), stop=(ic == NIC - 1),
                                  inc=(ic == NIC - 1 and oc == 3), waits=w)
                    evs["ffn2"] = TE.ev()

                cd_dma(0)
                cd_dma_y1(0)
                cd_oproj(0)
                for t in range(OWN):
                    if t + 1 < OWN:
                        cd_dma(t + 1)
                    cd_y2_quant(t)
                    if t + 1 < OWN:
                        cd_dma_y1(t + 1)
                    cd_trans(t)
                    if t >= 1:
                        cd_y3(t - 1)
                    cd_ffn2(t)
                    if t + 1 < OWN:
                        cd_oproj(t + 1)
                cd_y3(OWN - 1)


        YE.wait(d_out.ev())
        if dbg:
            YE.wait(d_dbg.ev())
            for dd in d_dbgY:
                YE.wait(dd.ev())
            for dd in d_dbgG2:
                GE.wait(dd.ev())

        blk = ctx.enter_context(nc.Block())

        @blk.tensor
        def _(e):
            for f in TE.todo:
                f(e)

        @blk.vector
        def _(e):
            for f in VE.todo:
                f(e)

        @blk.scalar
        def _(e):
            for f in SE.todo:
                f(e)

        @blk.gpsimd
        def _(e):
            for f in GE.todo:
                f(e)

        @blk.sync
        def _(e):
            for f in YE.todo:
                f(e)

    return nc


def host_inputs(x, wq, wk, wv, wo, wf):
    x = np.asarray(x, dtype=np.float32)
    wts = {n: np.ascontiguousarray(np.asarray(w, dtype=np.float32).T)
           for n, w in (("wqT", wq), ("wkT", wk), ("wvT", wv), ("woT", wo), ("wfT", wf))}
    ident = np.eye(128, dtype=np.float32)
    kk, qq = np.meshgrid(np.arange(128), np.arange(128), indexing="ij")
    maskd = (kk <= qq).astype(np.float32)
    in_maps = []
    for c in range(8):
        b, j = c // 2, c % 2
        ow = own_tiles(j)
        ot = [t for t in range(NT) if t not in ow]
        perm = ow + ot
        xb = x[b].reshape(NT, 128, D)[perm].reshape(T, D)
        msel = np.zeros((128, OWN), dtype=np.float32)
        for i in range(OWN):
            msel[:, i] = 1.0 if ot[i] < ow[i] else 0.0
        schs = np.full((128, 1), np.float32(1.0 / np.sqrt(np.float64(128.0))),
                       dtype=np.float32)
        m = {"x0": np.ascontiguousarray(xb), "ident_in": ident, "maskd_in": maskd,
             "msel_in": msel, "schs_in": schs}
        m.update(wts)
        in_maps.append(m)
    return in_maps


def kernel(L0_attn_x, L0_attn_wq, L0_attn_wk, L0_attn_wv, L0_attn_wo, L0_ffn_weight):
    in_maps = host_inputs(L0_attn_x, L0_attn_wq, L0_attn_wk, L0_attn_wv,
                          L0_attn_wo, L0_ffn_weight)
    nc = build()
    res = run_bass_kernel_spmd(nc, in_maps, core_ids=list(range(8)))
    out = np.empty((4, T, D), dtype=np.float32)
    for c in range(8):
        b, j = c // 2, c % 2
        ow = own_tiles(j)
        shard = res.results[c]["y_out"]
        for i, t in enumerate(ow):
            out[b, t * 128:(t + 1) * 128, :] = shard[i]
    return out


# version 13
# speedup vs baseline: 1.6734x; 1.0062x over previous
import numpy as np
import concourse.bass as bass
import concourse.mybir as mybir
from concourse.bass_utils import run_bass_kernel_spmd
from contextlib import ExitStack

F32 = mybir.dt.float32
BF16 = mybir.dt.bfloat16
AF = mybir.ActivationFunctionType
ALU = mybir.AluOpType
AX = mybir.AxisListType

D = 2048
T = 4096
NT = 32
NIC = 16
H = 16
HS = 128
OWN = 16
NEG_SQRT_HS = -float(np.sqrt(np.float64(HS)))
INV_SQRT_HS = float(1.0 / np.sqrt(np.float64(HS)))
EPS = 1e-5
SH = 256.5
WMEAN_INV = 1.0 / (D * D)


def own_tiles(j):
    return [t for t in range(NT) if (0 if (t % 4) in (0, 3) else 1) == j]


class Ev:
    __slots__ = ("sem", "val")

    def __init__(self, sem, val):
        self.sem = sem
        self.val = val


class Eng:
    def __init__(self, sem, auto_serial):
        self.sem = sem
        self.cnt = 0
        self.auto = auto_serial
        self.waited = {}
        self.todo = []

    def wait(self, ev):
        if ev is None:
            return
        key = id(ev.sem)
        if self.waited.get(key, 0) >= ev.val:
            return
        self.waited[key] = ev.val
        sem, val = ev.sem, ev.val
        self.todo.append(lambda e: e.wait_ge(sem, val))

    def waits(self, evs):
        for ev in evs:
            self.wait(ev)

    def op(self, name, *args, inc=True, waits=(), **kw):
        self.waits(waits)
        if self.auto and self.cnt > 0:
            self.wait(Ev(self.sem, self.cnt))
        sem = self.sem
        if inc:
            self.cnt += 1

            def emit(e, name=name, args=args, kw=kw, sem=sem):
                getattr(e, name)(*args, **kw).then_inc(sem, 1)
        else:

            def emit(e, name=name, args=args, kw=kw):
                getattr(e, name)(*args, **kw)
        self.todo.append(emit)
        return Ev(self.sem, self.cnt) if inc else None

    def ev(self):
        return Ev(self.sem, self.cnt)


class DmaBuf:
    def __init__(self, nc, ctx, name):
        self.sem = ctx.enter_context(nc.semaphore(name))
        self.cum = 0

    def start(self, qeng, dst, src, waits=()):
        qeng.waits(waits)
        sem = self.sem
        self.cum += 16
        qeng.todo.append(lambda e, d=dst, s=src, sem=sem: e.dma_start(d, s).then_inc(sem, 16))
        return Ev(sem, self.cum)

    def ev(self):
        return Ev(self.sem, self.cum)


def build(dbg=(), upto='Z'):
    dbg = set(dbg)
    nc = bass.Bass("TRN2", target_bir_lowering=False)

    x0 = nc.dram_tensor("x0", [T, D], F32, kind="ExternalInput").ap()
    wqT = nc.dram_tensor("wqT", [D, D], F32, kind="ExternalInput").ap()
    wkT = nc.dram_tensor("wkT", [D, D], F32, kind="ExternalInput").ap()
    wvT = nc.dram_tensor("wvT", [D, D], F32, kind="ExternalInput").ap()
    woT = nc.dram_tensor("woT", [D, D], F32, kind="ExternalInput").ap()
    wfT = nc.dram_tensor("wfT", [D, D], F32, kind="ExternalInput").ap()
    ident_in = nc.dram_tensor("ident_in", [128, 128], F32, kind="ExternalInput").ap()
    maskd_in = nc.dram_tensor("maskd_in", [128, 128], F32, kind="ExternalInput").ap()
    msel_in = nc.dram_tensor("msel_in", [128, OWN], F32, kind="ExternalInput").ap()
    schs_in = nc.dram_tensor("schs_in", [128, 1], F32, kind="ExternalInput").ap()

    y_out = nc.dram_tensor("y_out", [OWN, 128, D], F32, kind="ExternalOutput").ap()
    dbg_t = {}
    if "y1" in dbg:
        dbg_t["d_y1"] = nc.dram_tensor("d_y1", [NT, 128, D], F32, kind="ExternalOutput").ap()
    if "y1ln" in dbg:
        dbg_t["d_y1ln"] = nc.dram_tensor("d_y1ln", [NT, 128, D], F32, kind="ExternalOutput").ap()
    if "kvq" in dbg:
        dbg_t["d_k"] = nc.dram_tensor("d_k", [128, 2, T], F32, kind="ExternalOutput").ap()
        dbg_t["d_v"] = nc.dram_tensor("d_v", [128, NT, 258], F32, kind="ExternalOutput").ap()
        dbg_t["d_q"] = nc.dram_tensor("d_q", [128, 2, OWN * 128], F32, kind="ExternalOutput").ap()
    if "attn" in dbg:
        dbg_t["d_at"] = nc.dram_tensor("d_at", [H, 128, OWN * 128], F32, kind="ExternalOutput").ap()
    if "y2" in dbg:
        dbg_t["d_y2"] = nc.dram_tensor("d_y2", [OWN, 128, D], F32, kind="ExternalOutput").ap()
    if "qdbg" in dbg:
        dbg_t["d_qd"] = nc.dram_tensor("d_qd", [4, 128, D], F32, kind="ExternalOutput").ap()
    if "xq2" in dbg:
        dbg_t["d_xq2"] = nc.dram_tensor("d_xq2", [OWN, 128, D], F32, kind="ExternalOutput").ap()
        dbg_t["d_st2"] = nc.dram_tensor("d_st2", [128, 2], F32, kind="ExternalOutput").ap()
    if "att2" in dbg:
        dbg_t["d_o0"] = nc.dram_tensor("d_o0", [128, 4, 129], F32, kind="ExternalOutput").ap()
        dbg_t["d_nc"] = nc.dram_tensor("d_nc", [128, 2], F32, kind="ExternalOutput").ap()
        dbg_t["d_p0"] = nc.dram_tensor("d_p0", [128, 512], F32, kind="ExternalOutput").ap()
        dbg_t["d_ch"] = nc.dram_tensor("d_ch", [128, 12], F32, kind="ExternalOutput").ap()

    s_y1ln = nc.dram_tensor("s_y1ln", [NT, 128, D], BF16, kind="Internal").ap()
    s_y1own = nc.dram_tensor("s_y1own", [OWN, 128, D], F32, kind="Internal").ap()
    s_ylT = nc.dram_tensor("s_ylT", [128, NIC, T], BF16, kind="Internal").ap()
    s_wqT = nc.dram_tensor("s_wqT", [128, NIC * D], BF16, kind="Internal").ap()
    s_attnT = nc.dram_tensor("s_attnT", [H, 128, OWN * 128], BF16, kind="Internal").ap()

    with ExitStack() as ctx:
        TE = Eng(ctx.enter_context(nc.semaphore("sem_t")), False)
        VE = Eng(ctx.enter_context(nc.semaphore("sem_v")), True)
        SE = Eng(ctx.enter_context(nc.semaphore("sem_s")), True)
        GE = Eng(ctx.enter_context(nc.semaphore("sem_g")), False)
        YE = Eng(ctx.enter_context(nc.semaphore("sem_y")), False)
        ALL = [TE, VE, SE, GE, YE]

        ALL_DMABUFS = []

        def barrier():
            evs = [g.ev() for g in ALL]
            for g in ALL:
                for ev in evs:
                    if ev.sem is not g.sem:
                        g.wait(ev)
                for db in ALL_DMABUFS:
                    if db.cum:
                        g.wait(db.ev())

        def DB(name):
            b = DmaBuf(nc, ctx, name)
            ALL_DMABUFS.append(b)
            return b

        d_x0 = [DB(f"dm_x{i}") for i in range(2)]
        d_wld = [DB(f"dm_w{i}") for i in range(2)]
        d_y1lnst = [DB(f"dm_ylst{i}") for i in range(2)]
        d_y1st = [DB(f"dm_y1st{i}") for i in range(2)]
        d_const = DB("dm_const")
        d_ylld = [DB(f"dm_ylld{i}") for i in range(2)]
        d_ylTst = [DB(f"dm_ylTst{i}") for i in range(2)]
        d_ylTld = [DB(f"dm_ylTld{i}") for i in range(2)]
        d_wsl = DB("dm_wsl")
        d_atst = DB("dm_atst")
        d_atld = [DB(f"dm_atld{i}") for i in range(2)]
        d_y1ld = [DB(f"dm_y1ld{i}") for i in range(2)]
        d_wcd = DB("dm_wcd")
        d_out = DB("dm_out")
        d_dbg = DB("dm_dbg")
        d_dbgY = [DB(f"dm_dbgY{i}") for i in range(2)]
        d_dbgG2 = [DB(f"dm_dbgG{i}") for i in range(2)]

        identb = ctx.enter_context(nc.sbuf_tensor("identb", [128, 128], BF16))
        identf = ctx.enter_context(nc.sbuf_tensor("identf", [128, 128], F32))
        maskd = ctx.enter_context(nc.sbuf_tensor("maskd", [128, 128], BF16))
        msel = ctx.enter_context(nc.sbuf_tensor("msel", [128, OWN], F32))
        ones_f = ctx.enter_context(nc.sbuf_tensor("ones_f", [128, 128], F32))
        pad_f = ctx.enter_context(nc.sbuf_tensor("pad_f", [128, 128], F32))

        xm = ctx.enter_context(nc.sbuf_tensor("xm", [128, D], F32))
        xqT = ctx.enter_context(nc.sbuf_tensor("xqT", [128, NIC, 128], BF16))
        junk = ctx.enter_context(nc.sbuf_tensor("junk", [128, D], F32))
        junk2 = ctx.enter_context(nc.sbuf_tensor("junk2", [128, D], F32))
        junk_s = ctx.enter_context(nc.sbuf_tensor("junk_s", [128, D], F32))

        def vec(name):
            return ctx.enter_context(nc.sbuf_tensor(name, [128, 1], F32))

        st_cs = vec("st_cs")
        st_sq = vec("st_sq")
        st_m = vec("st_m")
        st_msq = vec("st_msq")
        st_var = vec("st_var")
        st_sig = vec("st_sig")
        st_inv = vec("st_inv")
        st_am = vec("st_am")
        st_a = vec("st_a")
        st_ra = vec("st_ra")
        st_g = vec("st_g")
        st_st = vec("st_st")
        wsc = vec("wsc")
        wsc_inv = vec("wsc_inv")
        wscB = vec("wscB")
        st_mx = vec("st_mx")
        st_gm = vec("st_gm")
        negc = [vec("negc0"), vec("negc1")]
        wabs = ctx.enter_context(nc.sbuf_tensor("wabs", [128, NIC], F32))

        ev_c = None
        for dst, src in ((identb, ident_in), (identf, ident_in), (maskd, maskd_in),
                         (msel, msel_in)):
            ev_c = d_const.start(GE, dst[:, :], src[:, :])
        ev_consts = ev_c
        ev_ones = VE.op("memset", ones_f[:, :], 1.0)
        ones_b = ctx.enter_context(nc.sbuf_tensor("ones_b", [128, 128], BF16))
        ev_onesb = VE.op("memset", ones_b[:, :], 1.0)
        st_eps = vec("st_eps")
        ev_eps = VE.op("memset", st_eps[:, :], EPS)
        st_schs = vec("st_schs")
        st_r127 = vec("st_r127")
        schs_sb = ctx.enter_context(nc.sbuf_tensor("schs_sb", [128, 1], F32))
        ev_schs_ld = d_const.start(GE, schs_sb[:, :], schs_in[:, :])
        ev_consts = ev_schs_ld
        VE.op("tensor_copy", st_schs[:, :], schs_sb[:, :], waits=[ev_schs_ld])
        VE.op("memset", st_r127[:, :], 127.0)
        ev_r127 = VE.op("reciprocal", st_r127[:, :], st_r127[:, :])

        def ln_quant(src_ap, waits_src, xq_out=None, st_st_out=None):
            VE.op("reduce_sum", st_cs[:, :], src_ap, axis=AX.X, waits=waits_src)
            SE.op("activation", out=junk_s[:, :], in_=src_ap, func=AF.Square,
                  accum_out=st_sq[:, :], waits=list(waits_src) + [VE.ev()])
            ev_sq = SE.ev()
            VE.op("tensor_scalar", out=st_m[:, :], in0=st_cs[:, :], scalar1=1.0 / D,
                  scalar2=None, op0=ALU.mult)
            VE.op("tensor_scalar", out=xm[:, :], in0=src_ap, scalar1=st_m[:, 0:1],
                  scalar2=None, op0=ALU.subtract)
            VE.op("reduce_max", st_am[:, :], xm[:, :], axis=AX.X, apply_absolute_value=True)
            VE.op("tensor_mul", st_msq[:, :], st_m[:, :], st_m[:, :])
            VE.op("scalar_tensor_tensor", out=st_var[:, :], in0=st_sq[:, :], scalar=1.0 / D,
                  in1=st_msq[:, :], op0=ALU.mult, op1=ALU.subtract, waits=[ev_sq])
            ev_var = VE.ev()
            ev_sig = SE.op("activation", out=st_sig[:, :], in_=st_var[:, :], func=AF.Sqrt,
                           bias=st_eps[:, 0:1], waits=[ev_var, ev_eps])
            VE.op("reciprocal", st_inv[:, :], st_sig[:, :], waits=[ev_sig])
            VE.op("tensor_mul", st_a[:, :], st_am[:, :], st_inv[:, :])
            VE.op("tensor_scalar_max", st_a[:, :], st_a[:, :], 1e-8)
            VE.op("reciprocal", st_ra[:, :], st_a[:, :])
            VE.op("tensor_scalar", out=st_ra[:, :], in0=st_ra[:, :], scalar1=127.0,
                  scalar2=None, op0=ALU.mult)
            VE.op("tensor_mul", st_g[:, :], st_inv[:, :], st_ra[:, :])
            xi = ln_quant.xi[0]
            VE.op("tensor_scalar", out=junk[:, :], in0=xm[:, :], scalar1=st_g[:, 0:1],
                  scalar2=SH, op0=ALU.mult, op1=ALU.add)
            VE.op("tensor_copy", xi[:, :], junk[:, :])
            VE.op("tensor_copy", junk2[:, :], xi[:, :])
            if ln_quant.qdbg[0] is not None:
                qd = ln_quant.qdbg[0]
                d_dbg.start(YE, qd[0], junk[:, :], waits=[VE.ev()])
                d_dbg.start(YE, qd[1], junk2[:, :], waits=[VE.ev()])
            VE.op("tensor_tensor", xm[:, :], junk[:, :], junk2[:, :], op=ALU.subtract)
            if ln_quant.qdbg[0] is not None:
                qd = ln_quant.qdbg[0]
                d_dbg.start(YE, qd[2], xm[:, :], waits=[VE.ev()])
            VE.op("tensor_scalar", out=junk[:, :], in0=xm[:, :], scalar1=1.0,
                  scalar2=None, op0=ALU.is_ge, waits=[d_dbg.ev()])
            VE.op("scalar_tensor_tensor", out=junk[:, :], in0=xm[:, :], scalar=2.0,
                  in1=junk[:, :], op0=ALU.is_ge, op1=ALU.add)
            VE.op("scalar_tensor_tensor", out=junk[:, :], in0=xm[:, :], scalar=0.0,
                  in1=junk[:, :], op0=ALU.is_lt, op1=ALU.subtract)
            VE.op("scalar_tensor_tensor", out=junk[:, :], in0=xm[:, :], scalar=-1.0,
                  in1=junk[:, :], op0=ALU.is_lt, op1=ALU.add)
            VE.op("tensor_tensor", junk[:, :], junk2[:, :], junk[:, :], op=ALU.subtract)
            VE.op("tensor_scalar", out=junk[:, :], in0=junk[:, :], scalar1=383.0,
                  scalar2=129.0, op0=ALU.min, op1=ALU.max)
            xq_o = xq_out
            st_o = st_st if st_st_out is None else st_st_out
            ev_xq = VE.op("tensor_scalar", out=xq_o[:, :], in0=junk[:, :], scalar1=256.0,
                          scalar2=None, op0=ALU.subtract, waits=[ln_quant.ge_guard[0]])
            VE.op("tensor_mul", st_o[:, :], st_a[:, :], wscB[:, :])
            return ev_xq

        ln_quant.ge_guard = [None]
        ln_quant.xi = [None]
        ln_quant.qdbg = [None]

        tr_state = {0: None, 1: None}

        def transpose16(trps, src_fn, dst_fn, ev_src, ev_prev_cons):
            ev_out = None
            for g in range(4):
                b = g % 2
                for kk in range(4):
                    ic = g * 4 + kk
                    w = []
                    if kk == 0:
                        w = [ev_src, ev_consts, tr_state[b]]
                        if g == 0 and ev_prev_cons is not None:
                            w.append(ev_prev_cons)
                    TE.op("matmul", trps[b][:, kk * 128:(kk + 1) * 128], src_fn(ic),
                          identb[:, :], start=True, stop=True, inc=(kk == 3), waits=w)
                ev_mm = TE.ev()
                ev_out = SE.op("copy", dst_fn(g), trps[b][:, :], waits=[ev_mm])
                tr_state[b] = ev_out
            return ev_out

        def ln2_normalize(src_ap, dst_ap, ev_src, extra_waits=()):
            VE.op("reduce_sum", st_cs[:, :], src_ap, axis=AX.X, waits=[ev_src])
            SE.op("activation", out=junk_s[:, :], in_=src_ap, func=AF.Square,
                  accum_out=st_sq[:, :], waits=[ev_src, VE.ev()])
            ev_sq = SE.ev()
            VE.op("tensor_scalar", out=st_m[:, :], in0=st_cs[:, :], scalar1=1.0 / D,
                  scalar2=None, op0=ALU.mult)
            VE.op("tensor_mul", st_msq[:, :], st_m[:, :], st_m[:, :])
            VE.op("scalar_tensor_tensor", out=st_var[:, :], in0=st_sq[:, :], scalar=1.0 / D,
                  in1=st_msq[:, :], op0=ALU.mult, op1=ALU.subtract, waits=[ev_sq])
            ev_var = VE.ev()
            ev_sig = SE.op("activation", out=st_sig[:, :], in_=st_var[:, :], func=AF.Sqrt,
                           bias=st_eps[:, 0:1], waits=[ev_var, ev_eps])
            VE.op("reciprocal", st_inv[:, :], st_sig[:, :], waits=[ev_sig])
            return VE.op("tensor_scalar", out=dst_ap, in0=src_ap, scalar1=st_m[:, 0:1],
                         scalar2=st_inv[:, 0:1], op0=ALU.subtract, op1=ALU.mult,
                         waits=list(extra_waits))

        with ExitStack() as actx:
            wbuf = [actx.enter_context(nc.sbuf_tensor(f"wbuf{i}", [128, D], F32)) for i in range(2)]
            xbuf = [actx.enter_context(nc.sbuf_tensor(f"xbuf{i}", [128, D], F32)) for i in range(2)]
            y1b = [actx.enter_context(nc.sbuf_tensor(f"y1b{i}", [128, D], F32)) for i in range(2)]
            y1ln = [actx.enter_context(nc.sbuf_tensor(f"y1ln{i}", [128, D], BF16)) for i in range(2)]
            u1 = actx.enter_context(nc.sbuf_tensor("u1", [128, NIC * D], BF16))
            u1_wq = u1.ap().rearrange("p (c n) -> p c n", n=D)
            xiA = actx.enter_context(nc.sbuf_tensor("xiA", [128, D], mybir.dt.int32))
            xqb = [actx.enter_context(nc.sbuf_tensor(f"xqb{i}", [128, D], BF16)) for i in range(2)]
            xqTb = [actx.enter_context(nc.sbuf_tensor(f"xqTb{i}", [128, NIC, 128], BF16)) for i in range(2)]
            ylnT = [actx.enter_context(nc.sbuf_tensor(f"ylnT{i}", [128, NIC, 128], BF16)) for i in range(2)]
            st_stb = [actx.enter_context(nc.sbuf_tensor(f"st_stb{i}", [128, 1], F32)) for i in range(2)]
            ln_quant.xi[0] = xiA
            ypsum = actx.enter_context(nc.psum_tensor("ypsum", [128, D], F32))
            trpsA = [actx.enter_context(nc.psum_tensor(f"trpsA{i}", [128, 512], F32)) for i in range(2)]

            ev_w = [None, None]
            ev_wfree = [None, None]
            for k in range(NIC):
                b = k % 2
                ev_w[b] = d_wld[b].start(GE, wbuf[b][:, :], wfT[k * 128:(k + 1) * 128, :],
                                         waits=[ev_wfree[b]] if ev_wfree[b] else [])
                ev_wfree[b] = SE.op("activation", out=junk_s[:, :], in_=wbuf[b][:, :],
                                    func=AF.Abs, accum_out=wabs[:, k:k + 1], waits=[ev_w[b]])
            ev_ws = VE.op("reduce_sum", st_cs[:, :], wabs[:, :], axis=AX.X,
                          waits=[ev_wfree[0], ev_wfree[1]])
            VE.op("tensor_copy", st_mx[:, :], st_cs[:, :])
            ev_wsv = VE.ev()
            ev_wsum = TE.op("matmul", trpsA[0][:, 0:1], ones_f[:, :], st_mx[:, :],
                            start=True, stop=True, waits=[ev_ones, ev_wsv])
            VE.op("tensor_scalar", out=wsc[:, :], in0=trpsA[0][:, 0:1],
                  scalar1=WMEAN_INV, scalar2=None, op0=ALU.mult, waits=[ev_wsum])
            VE.op("tensor_scalar_max", wsc[:, :], wsc[:, :], 1e-6)
            ev_wsci = VE.op("reciprocal", wsc_inv[:, :], wsc[:, :])
            VE.op("tensor_scalar", out=wscB[:, :], in0=wsc[:, :],
                  scalar1=0.5, scalar2=None, op0=ALU.mult)
            ev_wscB = VE.op("tensor_mul", wscB[:, :], wscB[:, :], st_r127[:, 0:1],
                            waits=[ev_r127])
            ev_wsum_read = ev_wscB

            ev_w = [None, None]
            for k in range(NIC):
                b = k % 2
                ev_w[b] = d_wld[b].start(GE, wbuf[b][:, :], wfT[k * 128:(k + 1) * 128, :],
                                         waits=[ev_wfree[b]] if ev_wfree[b] else [])
                xi = ln_quant.xi[0]
                VE.op("tensor_scalar", out=junk[:, :], in0=wbuf[b][:, :],
                      scalar1=wsc_inv[:, 0:1], scalar2=SH, op0=ALU.mult, op1=ALU.add,
                      waits=[ev_w[b], ev_wsci])
                ev_wfree[b] = VE.ev()
                VE.op("tensor_copy", xi[:, :], junk[:, :])
                VE.op("tensor_copy", junk2[:, :], xi[:, :])
                VE.op("tensor_tensor", xm[:, :], junk[:, :], junk2[:, :], op=ALU.subtract)
                VE.op("tensor_scalar", out=junk[:, :], in0=xm[:, :], scalar1=1.0,
                      scalar2=None, op0=ALU.is_ge)
                VE.op("tensor_scalar", out=xm[:, :], in0=xm[:, :], scalar1=0.0,
                      scalar2=None, op0=ALU.is_lt)
                VE.op("tensor_tensor", junk[:, :], junk[:, :], xm[:, :], op=ALU.subtract)
                VE.op("tensor_tensor", junk[:, :], junk[:, :], junk2[:, :], op=ALU.add)
                VE.op("tensor_scalar", out=junk[:, :], in0=junk[:, :], scalar1=257.0,
                      scalar2=255.0, op0=ALU.min, op1=ALU.max)
                VE.op("tensor_scalar", out=u1_wq[:, k, :], in0=junk[:, :], scalar1=256.0,
                      scalar2=None, op0=ALU.subtract)
            ev_wq_ready = VE.ev()
            d_wqst = DB("dm_wqst")
            ev_wq_st = d_wqst.start(YE, s_wqT[:, :], u1.ap()[:, :], waits=[ev_wq_ready])
            if "wq" in dbg:
                for k in range(NIC):
                    VE.op("tensor_copy", junk[:, :], u1_wq[:, k, :], waits=[ev_wq_ready])
                    d_dbg.start(YE, dbg_t["d_wq"][:, k, :], junk[:, :], waits=[VE.ev()])

            ev_x = [None, None]
            ev_xfree = [None, None]
            ev_y1free = [[], []]
            ev_ylnfree = [[], []]
            ev_ffn = [None, None]
            ev_trans = [None, None]
            ev_ylnT_store = [None, None]
            ev_stt_prev = None
            ev_xq_e = [None, None]
            ev_xqT_e = [None, None]

            def quant_stage(t):
                b = t % 2
                ev_x[b] = d_x0[b].start(YE, xbuf[b][:, :], x0[t * 128:(t + 1) * 128, :],
                                        waits=[ev_xfree[b]] if ev_xfree[b] else [])
                w = [ev_x[b], ev_wscB]
                if ev_trans[b] is not None:
                    w.append(ev_trans[b])
                ev_xq_e[b] = ln_quant(xbuf[b][:, :], w, xq_out=xqb[b], st_st_out=st_stb[b])
                ev_xqT = transpose16(
                    trpsA, lambda ic, bb=b: xqb[bb][:, ic * 128:(ic + 1) * 128],
                    lambda g, bb=b: xqTb[bb].ap()[:, g * 4:(g + 1) * 4, :].rearrange("p c n -> p (c n)"),
                    ev_xq_e[b], ev_ffn[b])
                ev_trans[b] = TE.ev()
                ev_xqT_e[b] = ev_xqT

            def ffn_stage(t):
                b = t % 2
                for oc in range(4):
                    for ic in range(NIC):
                        w = []
                        if ic == 0 and oc == 0:
                            w = [ev_xqT_e[b], ev_wq_ready]
                            if ev_stt_prev is not None:
                                w.append(ev_stt_prev)
                        TE.op("matmul", ypsum[:, oc * 512:(oc + 1) * 512], xqTb[b][:, ic, :],
                              u1_wq[:, ic, oc * 512:(oc + 1) * 512],
                              start=(ic == 0), stop=(ic == NIC - 1),
                              inc=(ic == NIC - 1 and oc == 3), waits=w)
                ev_ffn[b] = TE.ev()

            def finish_stage(t):
                nonlocal ev_stt_prev
                b = t % 2
                yb = y1b[b]
                VE.op("scalar_tensor_tensor", out=yb[:, :], in0=ypsum[:, :],
                      scalar=st_stb[b][:, 0:1], in1=xbuf[b][:, :], op0=ALU.mult, op1=ALU.add,
                      waits=[ev_ffn[b]] + ev_y1free[b])
                ev_y1 = VE.ev()
                ev_stt_prev = ev_y1
                ev_xfree[b] = ev_y1
                ev_yln = ln2_normalize(yb[:, :], y1ln[b][:, :], ev_y1, ev_ylnfree[b])
                ev_tr = transpose16(
                    trpsA, lambda ic, bb=b: y1ln[bb][:, ic * 128:(ic + 1) * 128],
                    lambda g, bb=b: ylnT[bb].ap()[:, g * 4:(g + 1) * 4, :].rearrange("p c n -> p (c n)"),
                    ev_yln, ev_ylnT_store[b])
                for ic in range(NIC):
                    d_ylTst[b].start(YE, s_ylT[:, ic, t * 128:(t + 1) * 128], ylnT[b][:, ic, :],
                                     waits=[ev_tr] if ic == 0 else [])
                ev_ylnT_store[b] = d_ylTst[b].ev()
                ev_ylnfree[b] = [ev_tr]
                ev_y1free[b] = [ev_y1]
                if t < OWN:
                    ev_y1free[b].append(d_y1st[b].start(YE, s_y1own[t, :, :], yb[:, :],
                                                        waits=[ev_y1]))
                if "y1" in dbg:
                    ev_y1free[b].append(d_dbgY[b].start(YE, dbg_t["d_y1"][t, :, :], yb[:, :],
                                                        waits=[ev_y1]))
                if "y1ln" in dbg:
                    ev_ylnfree[b].append(d_dbgG2[b].start(GE, dbg_t["d_y1ln"][t, :, :],
                                                          y1ln[b][:, :], waits=[ev_yln]))

            quant_stage(0)
            ffn_stage(0)
            for t in range(1, NT):
                quant_stage(t)
                finish_stage(t - 1)
                ffn_stage(t)
            finish_stage(NT - 1)

            barrier()

        if upto >= 'B':
            with ExitStack() as bctx:
                Kt = bctx.enter_context(nc.sbuf_tensor("Kt", [128, 2, T], BF16))
                Vh = bctx.enter_context(nc.sbuf_tensor("Vh", [128, NT, 258], BF16))
                Qt = bctx.enter_context(nc.sbuf_tensor("Qt", [128, 2, OWN * 128], BF16))
                Pt = [bctx.enter_context(nc.sbuf_tensor(f"Pt{i}", [128, 512], BF16)) for i in range(4)]
                wk_sl = bctx.enter_context(nc.sbuf_tensor("wk_sl", [128, NIC, 256], BF16))
                wv_sl = bctx.enter_context(nc.sbuf_tensor("wv_sl", [128, NIC, 256], BF16))
                wq_sl = bctx.enter_context(nc.sbuf_tensor("wq_sl", [128, NIC, 256], BF16))
                ylstr = [bctx.enter_context(nc.sbuf_tensor(f"ylstr{i}", [128, NIC, 512], BF16)) for i in range(2)]
                attn_sb = bctx.enter_context(nc.sbuf_tensor("attn_sb", [128, 128], BF16))
                attnT = bctx.enter_context(nc.sbuf_tensor("attnT", [128, 2, OWN * 128], BF16))

                for hg in range(H // 2):
                    barrier()
                    SE.wait(d_atst.ev())
                    h0 = 2 * hg
                    d_wsl.start(GE, wk_sl[:, :, :],
                                wkT.rearrange("(c p) o -> p c o", p=128)[:, :, h0 * 128:(h0 + 2) * 128])
                    d_wsl.start(GE, wv_sl[:, :, :],
                                wvT.rearrange("(c p) o -> p c o", p=128)[:, :, h0 * 128:(h0 + 2) * 128])
                    d_wsl.start(GE, wq_sl[:, :, :],
                                wqT.rearrange("(c p) o -> p c o", p=128)[:, :, h0 * 128:(h0 + 2) * 128])
                    ev_wall = d_wsl.ev()
                    VE.op("memset", Vh.ap()[:, :, 128:129], 1.0)
                    ev_onescol = VE.op("memset", Vh.ap()[:, :, 257:258], 1.0)

                    with ExitStack() as pctx:
                        psK = [pctx.enter_context(nc.psum_tensor(f"psK{hg}_{i}", [128, 512], F32)) for i in range(2)]
                        psV = [pctx.enter_context(nc.psum_tensor(f"psV{hg}_{i}", [128, 256], F32)) for i in range(2)]
                        psQ = pctx.enter_context(nc.psum_tensor(f"psQ{hg}", [128, 512], F32))
                        ev_str = [None, None]
                        ev_strfree = [None, None]
                        evK = {0: None, 1: None}
                        evV = {0: None, 1: None}
                        evQ = None
                        for t5 in range(8):
                            b = t5 % 2
                            ev_str[b] = d_ylTld[b].start(YE, ylstr[b][:, :, :],
                                                         s_ylT[:, :, t5 * 512:(t5 + 1) * 512],
                                                         waits=[ev_strfree[b]] if ev_strfree[b] else [])
                            for dc in range(2):
                                for ic in range(NIC):
                                    w = [ev_wall, ev_str[b], evK[dc]] if ic == 0 else []
                                    TE.op("matmul", psK[dc][:, :], wk_sl[:, ic, dc * 128:(dc + 1) * 128],
                                          ylstr[b][:, ic, :], start=(ic == 0), stop=(ic == NIC - 1),
                                          inc=(ic == NIC - 1), waits=w)
                                evK[dc] = SE.op("copy", Kt[:, dc, t5 * 512:(t5 + 1) * 512],
                                                psK[dc][:, :], waits=[TE.ev()])
                            for tc in range(4):
                                kt = t5 * 4 + tc
                                pb = tc % 2
                                for ic in range(NIC):
                                    w = [ev_wall, ev_str[b], evV[pb]] if ic == 0 else []
                                    TE.op("matmul", psV[pb][:, :],
                                          ylstr[b][:, ic, tc * 128:(tc + 1) * 128],
                                          wv_sl[:, ic, :], start=(ic == 0), stop=(ic == NIC - 1),
                                          inc=(ic == NIC - 1), waits=w)
                                ev_mm = TE.ev()
                                SE.op("copy", Vh[:, kt, 0:128], psV[pb][:, 0:128], waits=[ev_mm])
                                evV[pb] = SE.op("copy", Vh[:, kt, 129:257], psV[pb][:, 128:256])
                            if t5 < 4:
                                for dc in range(2):
                                    for ic in range(NIC):
                                        w = [ev_wall, ev_str[b], evQ] if ic == 0 else []
                                        TE.op("matmul", psQ[:, :], wq_sl[:, ic, dc * 128:(dc + 1) * 128],
                                              ylstr[b][:, ic, :], start=(ic == 0), stop=(ic == NIC - 1),
                                              inc=(ic == NIC - 1), waits=w)
                                    evQ = SE.op("activation",
                                                out=Qt[:, dc, t5 * 512:(t5 + 1) * 512],
                                                in_=psQ[:, :], func=AF.Copy,
                                                scale=st_schs[:, 0:1], waits=[TE.ev()])
                            ev_strfree[b] = TE.ev()
                        ev_proj = [evK[0], evK[1], evV[0], evV[1], evQ]

                        ev_negc = [None, None]
                        ev_sqr = None
                        for dc in range(2):
                            chsc = ylstr[0].ap().rearrange("p c n -> p (c n)")[:, 0:D]
                            ev_sq = SE.op("activation", out=chsc, in_=Qt[:, dc, :],
                                          func=AF.Square,
                                          waits=ev_proj + [ev_negc[0], ev_sqr, TE.ev()])
                            for c4 in range(4):
                                ev_mm = TE.op("matmul", psQ[:, :], ones_b[:, :],
                                              chsc[:, c4 * 512:(c4 + 1) * 512], start=True,
                                              stop=True, waits=[ev_sq, ev_onesb, evQ,
                                                                ev_negc[0], ev_negc[1],
                                                                VE.ev()])
                                VE.op("reduce_max", st_gm[:, :], psQ[:, :], axis=AX.X,
                                      waits=[ev_mm])
                                if c4 == 0:
                                    VE.op("tensor_copy", st_am[:, :], st_gm[:, :])
                                else:
                                    VE.op("tensor_max", st_am[:, :], st_am[:, :], st_gm[:, :])
                            ev_q2 = VE.ev()
                            ev_first = None
                            for half in range(2):
                                ev_sqr = SE.op("activation", out=chsc,
                                               in_=Kt[:, dc, half * D:(half + 1) * D],
                                               func=AF.Square, waits=[ev_q2, TE.ev()])
                                for c4 in range(4):
                                    ev_mm = TE.op("matmul", psQ[:, :], ones_b[:, :],
                                                  chsc[:, c4 * 512:(c4 + 1) * 512], start=True,
                                                  stop=True, waits=[ev_sqr, VE.ev()])
                                    VE.op("reduce_max", st_gm[:, :], psQ[:, :], axis=AX.X,
                                          waits=[ev_mm])
                                    if half == 0 and c4 == 0:
                                        VE.op("tensor_copy", st_mx[:, :], st_gm[:, :])
                                    else:
                                        VE.op("tensor_max", st_mx[:, :], st_mx[:, :], st_gm[:, :])
                            VE.op("tensor_mul", st_gm[:, :], st_am[:, :], st_mx[:, :])
                            ev_m2 = VE.ev()
                            ev_rt = SE.op("activation", out=st_gm[:, :], in_=st_gm[:, :],
                                          func=AF.Sqrt, waits=[ev_m2])
                            ev_negc[dc] = VE.op("tensor_scalar", out=negc[dc][:, :],
                                                in0=st_gm[:, :], scalar1=-1.0,
                                                scalar2=None, op0=ALU.mult, waits=[ev_rt])
                            if "att2" in dbg and hg == 0 and dc == 0:
                                VE.op("tensor_copy", junk[:, 0:1], st_am[:, :])
                                VE.op("tensor_copy", junk[:, 1:2], st_mx[:, :])
                                VE.op("tensor_copy", junk[:, 2:3], negc[0][:, :])
                                d_dbg.start(YE, dbg_t["d_ch"][:, :], junk[:, 0:12], waits=[VE.ev()])
                                VE.op("memset", junk[:, 0:1], 0.0, waits=[d_dbg.ev()])

                    if "kvq" in dbg and hg == 0:
                        ev_d = None
                        for dc in range(2):
                            for half in range(2):
                                VE.op("tensor_copy", junk[:, :], Kt[:, dc, half * D:(half + 1) * D],
                                      waits=ev_proj + [ev_d])
                                ev_d = d_dbg.start(YE, dbg_t["d_k"][:, dc, half * D:(half + 1) * D],
                                                   junk[:, :], waits=[VE.ev()])
                            VE.op("tensor_copy", junk[:, :], Qt[:, dc, :], waits=ev_proj + [ev_d])
                            ev_d = d_dbg.start(YE, dbg_t["d_q"][:, dc, :], junk[:, :], waits=[VE.ev()])
                        for kt in range(NT):
                            VE.op("tensor_copy", junk[:, 0:258], Vh[:, kt, :], waits=ev_proj + [ev_d])
                            ev_d = d_dbg.start(YE, dbg_t["d_v"][:, kt, :], junk[:, 0:258], waits=[VE.ev()])
                        VE.op("memset", junk[:, 0:1], 0.0, waits=[ev_d])

                    with ExitStack() as actx2:
                        Sps = [actx2.enter_context(nc.psum_tensor(f"Sps{hg}_{i}", [128, 512], F32)) for i in range(4)]
                        Ops = [actx2.enter_context(nc.psum_tensor(f"Ops{hg}_{i}", [128, 129], F32)) for i in range(4)]
                        ev_exp = {i: None for i in range(4)}
                        ev_pvd = {i: None for i in range(4)}
                        ev_o_cons = None
                        ev_att_st = None
                        for dc in range(2):
                            h = h0 + dc
                            for qg in range(4):
                                i_lo = qg * 4
                                i_hi = i_lo + 4
                                kts = [(0, k) for k in range(i_hi)] + [(1, k) for k in range(i_hi)]
                                m_ev = {}

                                def sc_stage(it, i_lo=i_lo, dc=dc):
                                    side, k = kts[it]
                                    vlo = max(i_lo, k)
                                    off = (vlo - i_lo) * 128
                                    b = it % 4
                                    kcol = (k if side == 0 else 16 + k) * 128
                                    first = [ev_proj[0], ev_proj[1], ev_proj[4],
                                             ev_negc[0], ev_negc[1]]
                                    if ev_exp[b] is not None:
                                        first.append(ev_exp[b])
                                    if ev_o_cons is not None:
                                        first.append(ev_o_cons)
                                    ev_sc = TE.op("matmul", Sps[b][:, off:512],
                                                  Kt[:, dc, kcol:kcol + 128],
                                                  Qt[:, dc, i_lo * 128 + off: (i_lo + 4) * 128],
                                                  start=True, stop=True, waits=first)
                                    ev_e = SE.op("activation", out=Pt[b][:, off:512],
                                                 in_=Sps[b][:, off:512], func=AF.Exp,
                                                 bias=negc[dc][:, 0:1],
                                                 waits=[ev_sc, ev_negc[dc], ev_pvd[b]])
                                    ev_m = ev_e
                                    if side == 0 and k >= i_lo:
                                        c0 = (k - i_lo) * 128
                                        ev_m = VE.op("tensor_mul", Pt[b][:, c0:c0 + 128],
                                                     Pt[b][:, c0:c0 + 128], maskd[:, :],
                                                     waits=[ev_e, ev_consts])
                                    elif side == 1 and k >= i_lo:
                                        c0 = (k - i_lo) * 128
                                        ev_m = VE.op("tensor_scalar", out=Pt[b][:, c0:c0 + 128],
                                                     in0=Pt[b][:, c0:c0 + 128],
                                                     scalar1=msel[:, k:k + 1], scalar2=None,
                                                     op0=ALU.mult, waits=[ev_e, ev_consts])
                                    m_ev[it] = ev_m

                                def pv_stage(it, i_lo=i_lo, i_hi=i_hi, dc=dc):
                                    side, k = kts[it]
                                    vlo = max(i_lo, k)
                                    b = it % 4
                                    for i in range(vlo, i_hi):
                                        li = i - i_lo
                                        w = [m_ev[it], ev_onescol, ev_proj[2], ev_proj[3]] if i == vlo else []
                                        TE.op("matmul", Ops[li][:, :],
                                              Pt[b][:, li * 128:(li + 1) * 128],
                                              Vh[:, (k if side == 0 else 16 + k), dc * 129:(dc + 1) * 129],
                                              start=(side == 0 and k == 0),
                                              stop=(side == 1 and k == i),
                                              inc=(i == i_hi - 1), waits=w)
                                    ev_pvd[b] = TE.ev()
                                    if ("att2" in dbg and hg == 0 and dc == 0 and qg == 0
                                            and side == 0 and k == 0):
                                        VE.op("tensor_copy", junk[:, 0:512], Pt[b][:, :],
                                              waits=[m_ev[it]])
                                        d_dbg.start(YE, dbg_t["d_p0"][:, :], junk[:, 0:512],
                                                    waits=[VE.ev()])

                                sc_stage(0)
                                sc_stage(1)
                                sc_stage(2)
                                for it in range(len(kts)):
                                    if it + 3 < len(kts):
                                        sc_stage(it + 3)
                                    pv_stage(it)
                                ev_pv_all = TE.ev()
                                if "att2" in dbg and hg == 0 and dc == 0 and qg == 0:
                                    for li4 in range(4):
                                        VE.op("tensor_copy", junk2[:, li4 * 129:(li4 + 1) * 129],
                                              Ops[li4][:, :], waits=[ev_pv_all, d_dbg.ev()])
                                    VE.op("tensor_copy", junk2[:, 516:517], negc[0][:, :],
                                          waits=[ev_negc[0], ev_negc[1]])
                                    VE.op("tensor_copy", junk2[:, 517:518], negc[1][:, :])
                                    ev_dd = VE.ev()
                                    for li4 in range(4):
                                        d_dbg.start(YE, dbg_t["d_o0"][:, li4, :],
                                                    junk2[:, li4 * 129:(li4 + 1) * 129],
                                                    waits=[ev_dd])
                                    d_dbg.start(YE, dbg_t["d_nc"][:, :], junk2[:, 516:518],
                                                waits=[ev_dd])
                                    VE.op("memset", junk2[:, 0:1], 0.0, waits=[d_dbg.ev()])
                                for li in range(4):
                                    i = i_lo + li
                                    VE.op("reciprocal", st_ra[:, :], Ops[li][:, 128:129],
                                          waits=[ev_pv_all])
                                    ev_at = VE.op("tensor_scalar", out=attn_sb[:, :],
                                                  in0=Ops[li][:, 0:128],
                                                  scalar1=st_ra[:, 0:1], scalar2=None, op0=ALU.mult,
                                                  waits=[ev_att_st] if ev_att_st else [])
                                    ev_tr = TE.op("matmul", Sps[0][:, 0:128], attn_sb[:, :],
                                                  identb[:, :], start=True, stop=True,
                                                  waits=[ev_at, ev_exp[0], ev_exp[1],
                                                         ev_exp[2], ev_exp[3]])
                                    ev_att_st = SE.op("copy", attnT[:, dc, i * 128:(i + 1) * 128],
                                                      Sps[0][:, 0:128], waits=[ev_tr])
                                    ev_exp[0] = ev_att_st
                                ev_o_cons = ev_att_st
                            d_atst.start(YE, s_attnT[h, :, :], attnT[:, dc, :], waits=[ev_att_st])
                            if "attn" in dbg:
                                VE.op("tensor_copy", junk[:, :], attnT[:, dc, :],
                                      waits=[ev_att_st, d_dbg.ev()])
                                d_dbg.start(YE, dbg_t["d_at"][h, :, :], junk[:, :], waits=[VE.ev()])

                barrier()

        if upto >= 'C':
            with ExitStack() as cctx:
                u2 = cctx.enter_context(nc.sbuf_tensor("u2", [128, NIC * D], BF16))
                u2_wo = u2.ap().rearrange("p (c n) -> p c n", n=D)
                u1c = cctx.enter_context(nc.sbuf_tensor("u1c", [128, NIC * D], BF16))
                u1_wq = u1c.ap().rearrange("p (c n) -> p c n", n=D)
                d_wqld = DB("dm_wqld")
                YE.wait(d_wqst.ev())
                ev_wq_ready = d_wqld.start(YE, u1c.ap()[:, :], s_wqT[:, :])
                atb_0 = cctx.enter_context(nc.sbuf_tensor("atb_0", [128, NIC, 128], BF16))
                atb = [atb_0, atb_0]
                y1s_0 = cctx.enter_context(nc.sbuf_tensor("y1s_0", [128, D], F32))
                y1s = [y1s_0, y1s_0]
                y2 = [cctx.enter_context(nc.sbuf_tensor(f"y2_{i}", [128, D], F32)) for i in range(2)]
                xq2b_0 = cctx.enter_context(nc.sbuf_tensor("xq2b_0", [128, D], BF16))
                xq2b = [xq2b_0, xq2b_0]
                st_stc = [cctx.enter_context(nc.sbuf_tensor(f"st_stc{i}", [128, 1], F32)) for i in range(2)]
                xiC = cctx.enter_context(nc.sbuf_tensor("xiC", [128, D], mybir.dt.int32))
                ln_quant.xi[0] = xiC
                ops_ps = [cctx.enter_context(nc.psum_tensor(f"ops_ps{i}", [128, D], F32)) for i in range(1)]
                ops_ps = [ops_ps[0], ops_ps[0]]
                fps = cctx.enter_context(nc.psum_tensor("fps", [128, D], F32))

                YE.wait(d_atst.ev())
                YE.wait(d_y1st[0].ev())
                YE.wait(d_y1st[1].ev())
                d_wcd.start(GE, u2_wo[:, :, :], woT.rearrange("(c p) o -> p c o", p=128))
                ev_wo_ld = d_wcd.ev()

                ev_at = [None, None]
                ev_atfree = [None, None]
                ev_y1l = [None, None]
                ev_y1lfree = [None, None]
                ev_y2_cons = None
                ev_y3free = [None, None]
                ev_xqT2_cons = None
                ev_ffn2_done = None
                s_at_v = s_attnT.rearrange("h p q -> p h q")
                evs = {"at": [None, None], "atfree": [None, None],
                       "y1l": [None, None], "y1lfree": [None, None],
                       "oproj": [None, None], "y2": [None, None],
                       "xq2": [None, None], "xqT2": None, "ffn2": None,
                       "y3": None, "y3free": None, "trans_done": [None, None]}

                def cd_dma(t):
                    b = t % 2
                    w = [evs["atfree"][1 - b]] if evs["atfree"][1 - b] else []
                    for half in range(2):
                        evs["at"][b] = d_atld[b].start(
                            YE, atb[b][:, half * 8:(half + 1) * 8, :],
                            s_at_v[:, half * 8:(half + 1) * 8, t * 128:(t + 1) * 128],
                            waits=w if half == 0 else [])

                def cd_dma_y1(t):
                    b = t % 2
                    evs["y1l"][b] = d_y1ld[0].start(
                        YE, y1s[b][:, :], s_y1own[t, :, :],
                        waits=[evs["y1lfree"][1 - b]] if evs["y1lfree"][1 - b] else [])

                def cd_oproj(t):
                    b = t % 2
                    for oc in range(4):
                        for ic in range(NIC):
                            w = []
                            if ic == 0 and oc == 0:
                                w = [evs["at"][b], ev_wo_ld]
                                if evs["trans_done"][1 - b] is not None:
                                    w.append(evs["trans_done"][1 - b])
                                if evs["y2"][1 - b] is not None:
                                    w.append(evs["y2"][1 - b])
                            TE.op("matmul", ops_ps[b][:, oc * 512:(oc + 1) * 512],
                                  atb[b][:, ic, :],
                                  u2_wo[:, ic, oc * 512:(oc + 1) * 512],
                                  start=(ic == 0), stop=(ic == NIC - 1),
                                  inc=(ic == NIC - 1 and oc == 3), waits=w)
                    evs["oproj"][b] = TE.ev()
                    evs["atfree"][b] = evs["oproj"][b]

                def cd_y2_quant(t):
                    b = t % 2
                    VE.op("scalar_tensor_tensor", out=y2[b][:, :], in0=ops_ps[b][:, :],
                          scalar=1.0, in1=y1s[b][:, :], op0=ALU.mult, op1=ALU.add,
                          waits=[evs["oproj"][b], evs["y1l"][b]] +
                                ([evs["xq2"][b]] if evs["xq2"][b] else []))
                    ev_y2 = VE.ev()
                    evs["y2"][b] = ev_y2
                    evs["y1lfree"][b] = ev_y2
                    ln_quant.qdbg[0] = (dbg_t["d_qd"] if ("qdbg" in dbg and t == 8) else None)
                    w = [ev_y2]
                    if evs["xqT2"] is not None:
                        w.append(evs["xqT2"])
                    for tb in range(2):
                        if evs["trans_done"][tb] is not None:
                            w.append(evs["trans_done"][tb])
                    if evs["y3free"] is not None:
                        w.append(evs["y3free"])
                    evs["xq2"][b] = ln_quant(y2[b][:, :], w, xq_out=xq2b[b], st_st_out=st_stc[b])
                    if "xq2" in dbg and t == 0:
                        ev_d1 = d_dbgG2[0].start(GE, dbg_t["d_xq2"][0, :, :], xq2b[b][:, :],
                                                 waits=[evs["xq2"][b]])
                        ln_quant.ge_guard[0] = ev_d1

                def cd_trans(t):
                    b = t % 2
                    ev_evac = None
                    for g in range(4):
                        pb = g % 2
                        for kk in range(4):
                            ic = g * 4 + kk
                            w = []
                            if kk == 0:
                                w = [evs["xq2"][b], ev_consts, evs["y2"][b]]
                                if g >= 2:
                                    w.append(ev_evac)
                            TE.op("matmul",
                                  ops_ps[b][:, pb * 512 + kk * 128: pb * 512 + (kk + 1) * 128],
                                  xq2b[b][:, ic * 128:(ic + 1) * 128], identb[:, :],
                                  start=True, stop=True, inc=(kk == 3), waits=w)
                        ev_mm = TE.ev()
                        ev_evac = SE.op(
                            "copy",
                            xqT.ap()[:, g * 4:(g + 1) * 4, :].rearrange("p c n -> p (c n)"),
                            ops_ps[b][:, pb * 512:(pb + 1) * 512], waits=[ev_mm])
                    evs["xqT2"] = ev_evac
                    evs["trans_done"][b] = ev_evac

                def cd_y3(t):
                    b = t % 2
                    VE.op("scalar_tensor_tensor", out=xm[:, :], in0=fps[:, :],
                          scalar=st_stc[b][:, 0:1], in1=y2[b][:, :], op0=ALU.mult, op1=ALU.add,
                          waits=[evs["ffn2"], d_dbg.ev()] +
                                ([evs["y3free"]] if evs["y3free"] else []))
                    ev_y3 = VE.ev()
                    evs["y3"] = ev_y3
                    evs["y3free"] = d_out.start(YE, y_out[t, :, :], xm[:, :], waits=[ev_y3])

                def cd_ffn2(t):
                    b = t % 2
                    for oc in range(4):
                        for ic in range(NIC):
                            w = []
                            if ic == 0 and oc == 0:
                                w = [evs["xqT2"], ev_wq_ready]
                                if evs["y3"] is not None:
                                    w.append(evs["y3"])
                            TE.op("matmul", fps[:, oc * 512:(oc + 1) * 512], xqT[:, ic, :],
                                  u1_wq[:, ic, oc * 512:(oc + 1) * 512],
                                  start=(ic == 0), stop=(ic == NIC - 1),
                                  inc=(ic == NIC - 1 and oc == 3), waits=w)
                    evs["ffn2"] = TE.ev()

                cd_dma(0)
                cd_dma_y1(0)
                cd_oproj(0)
                for t in range(OWN):
                    if t + 1 < OWN:
                        cd_dma(t + 1)
                    cd_y2_quant(t)
                    if t + 1 < OWN:
                        cd_dma_y1(t + 1)
                    cd_trans(t)
                    if t >= 1:
                        cd_y3(t - 1)
                    cd_ffn2(t)
                    if t + 1 < OWN:
                        cd_oproj(t + 1)
                cd_y3(OWN - 1)


        YE.wait(d_out.ev())
        if dbg:
            YE.wait(d_dbg.ev())
            for dd in d_dbgY:
                YE.wait(dd.ev())
            for dd in d_dbgG2:
                GE.wait(dd.ev())

        blk = ctx.enter_context(nc.Block())

        @blk.tensor
        def _(e):
            for f in TE.todo:
                f(e)

        @blk.vector
        def _(e):
            for f in VE.todo:
                f(e)

        @blk.scalar
        def _(e):
            for f in SE.todo:
                f(e)

        @blk.gpsimd
        def _(e):
            for f in GE.todo:
                f(e)

        @blk.sync
        def _(e):
            for f in YE.todo:
                f(e)

    return nc


def host_inputs(x, wq, wk, wv, wo, wf):
    x = np.asarray(x, dtype=np.float32)
    wts = {n: np.ascontiguousarray(np.asarray(w, dtype=np.float32).T)
           for n, w in (("wqT", wq), ("wkT", wk), ("wvT", wv), ("woT", wo), ("wfT", wf))}
    ident = np.eye(128, dtype=np.float32)
    kk, qq = np.meshgrid(np.arange(128), np.arange(128), indexing="ij")
    maskd = (kk <= qq).astype(np.float32)
    in_maps = []
    for c in range(8):
        b, j = c // 2, c % 2
        ow = own_tiles(j)
        ot = [t for t in range(NT) if t not in ow]
        perm = ow + ot
        xb = x[b].reshape(NT, 128, D)[perm].reshape(T, D)
        msel = np.zeros((128, OWN), dtype=np.float32)
        for i in range(OWN):
            msel[:, i] = 1.0 if ot[i] < ow[i] else 0.0
        schs = np.full((128, 1), np.float32(1.0 / np.sqrt(np.float64(128.0))),
                       dtype=np.float32)
        m = {"x0": np.ascontiguousarray(xb), "ident_in": ident, "maskd_in": maskd,
             "msel_in": msel, "schs_in": schs}
        m.update(wts)
        in_maps.append(m)
    return in_maps


def kernel(L0_attn_x, L0_attn_wq, L0_attn_wk, L0_attn_wv, L0_attn_wo, L0_ffn_weight):
    in_maps = host_inputs(L0_attn_x, L0_attn_wq, L0_attn_wk, L0_attn_wv,
                          L0_attn_wo, L0_ffn_weight)
    nc = build()
    res = run_bass_kernel_spmd(nc, in_maps, core_ids=list(range(8)))
    out = np.empty((4, T, D), dtype=np.float32)
    for c in range(8):
        b, j = c // 2, c % 2
        ow = own_tiles(j)
        shard = res.results[c]["y_out"]
        for i, t in enumerate(ow):
            out[b, t * 128:(t + 1) * 128, :] = shard[i]
    return out
